# version 08
# baseline (speedup 1.0000x reference)
import numpy as np
import ml_dtypes
from contextlib import ExitStack
import concourse.bass as bass
import concourse.mybir as mybir
from concourse.bass_utils import run_bass_kernel_spmd
from concourse.bass_core_rust import IndirectOffsetOnAxis

F32 = mybir.dt.float32
BF16 = mybir.dt.bfloat16
I32 = mybir.dt.int32
AF = mybir.ActivationFunctionType
ALU = mybir.AluOpType
AX = mybir.AxisListType
bfloat16 = ml_dtypes.bfloat16

S = 2048
D = 1024
NH = 16
HD = 64
WIN = 128
GLB = 64
DH = 4096
NE = 16
NT = S // 128
NDC = D // 128
NT2 = 2 * NT
CAP_G = 32
CAP_S = CAP_G * 512
THRN = 8
BLK = 512
NBLK2 = DH // BLK
HCL = BLK // 128

ENGS = ("pe", "dve", "act", "gp", "sp")
DEBUG = False


class Ev:
    __slots__ = ("sem", "val")

    def __init__(self, sem, val):
        self.sem = sem
        self.val = val


class Prog:
    def __init__(self, nc, es):
        self.nc = nc
        self.sems = {}
        self.cnt = {}
        for e in ENGS:
            self._mk(es, "c_" + e)
        for d in ("dW", "dA", "dX"):
            self._mk(es, d)
        self.waited = {e: {} for e in ENGS}

    def _mk(self, es, name):
        self.sems[name] = es.enter_context(self.nc.semaphore(name))
        self.cnt[name] = 0

    def snap(self, name):
        return Ev(self.sems[name], self.cnt[name])


class Phase:
    def __init__(self, P):
        self.P = P
        self.items = {e: [] for e in ENGS}
        self.start = {e: P.cnt["c_" + e] for e in ENGS}

    def emit(self, eng, fn, waits=(), sem=None, k=1):
        P = self.P
        ws = [w for w in waits if w is not None]
        if sem is None:
            sem = "c_" + eng
        if sem is not False:
            P.cnt[sem] += k
            ev = Ev(P.sems[sem], P.cnt[sem])
            self.items[eng].append((ws, fn, P.sems[sem], k))
            return ev
        self.items[eng].append((ws, fn, None, 0))
        return None

    def dma(self, queue_eng, dsem, dst, src, waits=()):
        return self.emit(queue_eng, lambda g, d=dst, s=src: g.dma_start(d, s),
                         waits=waits, sem=dsem, k=16)

    def idma_gather(self, dsem, dst, src_ap, idx_ap, eo=0, waits=()):
        return self.emit(
            "gp",
            lambda g, d=dst, s=src_ap, i=idx_ap, e=eo: g.indirect_dma_start(
                d, None, s, IndirectOffsetOnAxis(i, 0), element_offset=e),
            waits=waits, sem=dsem, k=16)

    def idma_scatter(self, dsem, dst_ap, idx_ap, src, eo=0, waits=()):
        return self.emit(
            "gp",
            lambda g, d=dst_ap, i=idx_ap, s=src, e=eo: g.indirect_dma_start(
                d, IndirectOffsetOnAxis(i, 0), s, None, element_offset=e),
            waits=waits, sem=dsem, k=16)

    def flush(self, name):
        P = self.P
        nc = P.nc
        with nc.Block(name) as b:
            def mk(eng_name):
                def go(eng):
                    pos = self.start[eng_name]
                    own = P.sems["c_" + eng_name]
                    for ws, fn, sem, k in self.items[eng_name]:
                        for w in ws:
                            if w.sem is own and w.val > pos:
                                raise RuntimeError(
                                    f"{name}/{eng_name}: wait on own future")
                            seen = P.waited[eng_name].get(w.sem.name, -1)
                            if w.val > seen:
                                eng.wait_ge(w.sem, w.val)
                                P.waited[eng_name][w.sem.name] = w.val
                        inst = fn(eng)
                        if sem is not None:
                            inst.then_inc(sem, k)
                            if sem is own:
                                pos += 1
                return go
            if self.items["gp"]:
                b.gpsimd(mk("gp"))
            if self.items["sp"]:
                b.sync(mk("sp"))
            if self.items["pe"]:
                b.tensor(mk("pe"))
            if self.items["dve"]:
                b.vector(mk("dve"))
            if self.items["act"]:
                b.scalar(mk("act"))


def build_aux(inputs):
    b1 = np.asarray(inputs["L0_ffn_b1"], np.float32)
    bqkv = np.asarray(inputs["L0_attn_b_bqkv"], np.float32)
    wr = np.asarray(inputs["L0_ffn_Wr"], np.float32)
    aux = {
        "aux_idf": np.eye(128, dtype=np.float32),
        "aux_idb": np.eye(128, dtype=bfloat16),
        "aux_ones": np.ones((1, 128), dtype=np.float32),
        "aux_mask": np.tril(np.ones((128, 128), dtype=np.float32)),
        "aux_b1c": np.ascontiguousarray(
            b1.reshape(NE, 32, 128).transpose(2, 0, 1).reshape(128, NE * 32)),
        "aux_bqc": np.ascontiguousarray(
            bqkv[:, 0:64].reshape(8, 2, 64).transpose(2, 1, 0).reshape(128, 8)),
        "aux_bkc": np.ascontiguousarray(
            bqkv[:, 64:128].reshape(8, 2, 64).transpose(2, 1, 0).reshape(128, 8)),
        "aux_bvr": np.ascontiguousarray(bqkv[:, 128:192].reshape(1, NH * 64)),
        "aux_wr": np.ascontiguousarray(
            wr.reshape(NDC, 128, NE).transpose(1, 0, 2).reshape(128, NDC * NE)),
        "aux_boutr": np.asarray(inputs["L0_attn_b_bout"], np.float32).reshape(1, D).copy(),
        "aux_brr": np.asarray(inputs["L0_ffn_br"], np.float32).reshape(1, NE).copy(),
        "aux_su128": np.triu(np.ones((128, 128), np.float32), 1),
        "aux_su16": np.triu(np.ones((16, 16), np.float32), 1),
        "aux_io16": np.broadcast_to(np.arange(16, dtype=np.float32), (128, 16)).copy(),
        "aux_io16r": np.broadcast_to(16.0 - np.arange(16, dtype=np.float32), (128, 16)).copy(),
        "aux_iota128": np.arange(128, dtype=np.float32).reshape(128, 1),
        "aux_iota16": np.arange(16, dtype=np.float32).reshape(16, 1),
        "aux_ones128c": np.ones((128, 1), np.float32),
        "aux_ones16c": np.ones((16, 1), np.float32),
        "aux_g512": np.broadcast_to(512.0 * np.arange(CAP_G, dtype=np.float32), (16, CAP_G)).copy(),
        "aux_su32": np.triu(np.ones((32, 32), np.float32), 1),
        "aux_thr16": np.broadcast_to(512.0 * np.arange(THRN, dtype=np.float32), (16, THRN)).copy(),
        "aux_c512": np.full((16, 1), 512.0, np.float32),
        "aux_c1024": np.full((128, 1), 1024.0, np.float32),
        "aux_c4096": np.full((128, 1), 4096.0, np.float32),
    }
    return aux


def build_program(n_seq=2, parts=("attn_a", "ffn1", "attn_b", "ffn2")):
    nc = bass.Bass("TRN2", target_bir_lowering=False)
    L = {}

    def din(key, name, shape, dtype=F32):
        L[key] = nc.dram_tensor(name, list(shape), dtype, kind="ExternalInput").ap()

    din("x_in", "x", [n_seq, S, D])
    L["y_out"] = nc.dram_tensor("y", [n_seq, S, D], F32, kind="ExternalOutput").ap()
    din("Wq", "L0_attn_a_Wq", [D, D])
    din("Wk", "L0_attn_a_Wk", [D, D])
    din("Wv", "L0_attn_a_Wv", [D, D])
    din("Wo", "L0_attn_a_Wo", [D, D])
    din("Wqkv", "L0_attn_b_Wqkv", [NH, D, 3 * HD])
    din("Wout", "L0_attn_b_Wout", [D, D])
    din("W1", "L0_ffn_W1", [NE, D, DH])
    din("W2", "L0_ffn_W2", [NE, DH, D])
    din("b2_in", "L0_ffn_b2", [NE, D])
    din("a_idf", "aux_idf", [128, 128])
    din("a_idb", "aux_idb", [128, 128], BF16)
    din("a_ones", "aux_ones", [1, 128])
    din("a_mask", "aux_mask", [128, 128])
    din("a_b1c", "aux_b1c", [128, NE * 32])
    din("a_bqc", "aux_bqc", [128, 8])
    din("a_bkc", "aux_bkc", [128, 8])
    din("a_bvr", "aux_bvr", [1, NH * 64])
    din("a_wr", "aux_wr", [128, NDC * NE])
    din("a_boutr", "aux_boutr", [1, D])
    din("a_brr", "aux_brr", [1, NE])
    din("b1_in", "L0_ffn_b1", [NE, DH])
    din("a_su128", "aux_su128", [128, 128])
    din("a_su16", "aux_su16", [16, 16])
    din("a_io16", "aux_io16", [128, 16])
    din("a_io16r", "aux_io16r", [128, 16])
    din("a_iota128", "aux_iota128", [128, 1])
    din("a_iota16", "aux_iota16", [16, 1])
    din("a_ones128c", "aux_ones128c", [128, 1])
    din("a_ones16c", "aux_ones16c", [16, 1])
    din("a_g512", "aux_g512", [16, CAP_G])
    din("a_su32", "aux_su32", [32, 32])
    din("a_thr16", "aux_thr16", [16, THRN])
    din("a_c512", "aux_c512", [16, 1])
    din("a_c1024", "aux_c1024", [128, 1])
    din("a_c4096", "aux_c4096", [128, 1])
    L["w1bf"] = nc.dram_tensor("w1bf", [NE * D, DH], BF16, kind="Internal").ap()
    L["w2bf"] = nc.dram_tensor("w2bf", [NE * DH, D], BF16, kind="Internal").ap()
    L["xg_d"] = nc.dram_tensor("xg_d", [CAP_S, D], BF16, kind="Internal").ap()
    L["y_d"] = nc.dram_tensor("y_d", [CAP_S, D], BF16, kind="Internal").ap()
    L["meta_d"] = nc.dram_tensor("meta_d", [32, 16], F32, kind="Internal").ap()
    if DEBUG:
        for nm, shape, dt in (("dbg_sl", [128, 2 * NT2], I32), ("dbg_g12", [128, 2 * NT2], F32),
                              ("dbg_cb", [32, 16], F32), ("dbg_eog", [1, CAP_G], F32),
                              ("dbg_mc", [16, 16], F32), ("dbg_mc2", [16, 16], F32),
                              ("dbg_b1", [128, 32 * CAP_G], F32),
                              ("dbg_xg", [CAP_S, D], BF16), ("dbg_y", [CAP_S, D], BF16)):
            L[nm] = nc.dram_tensor(nm, shape, dt, kind="ExternalOutput").ap()

    es = ExitStack()
    sb = lambda name, shape, dt: es.enter_context(nc.sbuf_tensor(name, shape, dt))
    L["resid0"] = sb("resid0", [128, NT * D], BF16)
    L["resid1"] = sb("resid1", [128, NT * D], BF16)
    L["gate0"] = sb("gate0", [128, NT * NE], F32)
    L["gate1"] = sb("gate1", [128, NT * NE], F32)
    L["idf"] = sb("idf", [128, 128], F32)
    L["idb"] = sb("idb", [128, 128], BF16)
    L["ones32"] = sb("ones32", [1, 128], F32)
    L["mask01"] = sb("mask01", [128, 128], F32)
    L["b1c"] = sb("b1c", [128, NE * 32], F32)
    L["wr_sb"] = sb("wr_sb", [128, NDC * NE], F32)
    L["bqc"] = sb("bqc", [128, 8], F32)
    L["bkc"] = sb("bkc", [128, 8], F32)
    L["gT0"] = sb("gT0", [NE, S], F32)
    L["gT1"] = sb("gT1", [NE, S], F32)
    L["trash"] = sb("trash", [128, D], F32)
    L["su128"] = sb("su128", [128, 128], F32)
    L["su16"] = sb("su16", [16, 16], F32)
    L["io16"] = sb("io16", [128, 16], F32)
    L["io16r"] = sb("io16r", [128, 16], F32)
    L["iota128"] = sb("iota128", [128, 1], F32)
    L["iota16"] = sb("iota16", [16, 1], F32)
    L["ones128c"] = sb("ones128c", [128, 1], F32)
    L["ones16c"] = sb("ones16c", [16, 1], F32)
    L["g512"] = sb("g512", [16, CAP_G], F32)
    L["su32"] = sb("su32", [32, 32], F32)
    L["thr16"] = sb("thr16", [16, THRN], F32)
    L["c512"] = sb("c512", [16, 1], F32)
    L["c1024"] = sb("c1024", [128, 1], F32)
    L["c4096"] = sb("c4096", [128, 1], F32)
    L["bvr"] = sb("bvr", [1, NH * 64], F32)
    L["boutr"] = sb("boutr", [1, D], F32)
    L["brr"] = sb("brr", [1, NE], F32)
    L["stats"] = sb("stats", [128, 64], F32)
    L["u16"] = sb("u16", [128, 2 * 16], F32)
    L["u16b"] = sb("u16b", [128, 2 * 16], F32)

    L["ps_big"] = es.enter_context(nc.psum_tensor("ps_big", [128, 2048], F32))
    L["pb"] = [es.enter_context(nc.psum_tensor(f"pb{i}", [128, 512], F32)) for i in range(2)]
    L["pt"] = [es.enter_context(nc.psum_tensor(f"pt{i}", [128, 1024], BF16)) for i in range(2)]

    P = Prog(nc, es)
    with es:
        _emit_all(P, L, n_seq, parts)
    return nc


class _Feeder:

    def __init__(self, P, L, src_key, dst_key, nchunks=128):
        self.P, self.L = P, L
        self.src, self.dst = src_key, dst_key
        self.k = 0
        self.n = nchunks

    def feed(self, cnt, ph=None):
        if self.k >= self.n:
            return
        own = ph is None
        if own:
            ph = Phase(self.P)
        c1 = min(self.n, self.k + cnt)
        for k in range(self.k, c1):
            ph.dma("gp", "dW",
                   bass.AP(self.L[self.dst].tensor, k * 524288, [[2048, 256], [1, 2048]]),
                   bass.AP(self.L[self.src].tensor, k * 524288, [[2048, 256], [1, 2048]]))
        self.k = c1
        if own:
            ph.flush(f"pc_{self.dst}_{c1}")

    def drain(self):
        self.feed(self.n - self.k)


def _set_seq(L, s):
    L["resid"] = L[f"resid{s}"]
    L["gate"] = L[f"gate{s}"]
    L["gT"] = L[f"gT{s}"]


def _emit_all(P, L, n_seq, parts):
    nc = P.nc
    ph = Phase(P)
    dx = "dX"
    for dst, src in (
        (L["idf"], "a_idf"), (L["idb"], "a_idb"), (L["ones32"], "a_ones"),
        (L["mask01"], "a_mask"), (L["b1c"], "a_b1c"), (L["wr_sb"], "a_wr"),
        (L["bqc"], "a_bqc"), (L["bkc"], "a_bkc"),
        (L["bvr"], "a_bvr"), (L["boutr"], "a_boutr"), (L["brr"], "a_brr"),
        (L["su128"], "a_su128"), (L["su16"], "a_su16"),
        (L["io16"], "a_io16"), (L["io16r"], "a_io16r"),
        (L["iota128"], "a_iota128"), (L["iota16"], "a_iota16"),
        (L["ones128c"], "a_ones128c"), (L["ones16c"], "a_ones16c"),
        (L["g512"], "a_g512"), (L["su32"], "a_su32"), (L["thr16"], "a_thr16"), (L["c512"], "a_c512"),
        (L["c1024"], "a_c1024"), (L["c4096"], "a_c4096"),
    ):
        ph.dma("sp", dx, dst[:, :], L[src])
    tc = L["stats"][:, 60:61]
    ph.emit("dve", lambda v: v.tensor_scalar_mul(tc, L["ones128c"][:, :], 1.0 / D))
    ph.emit("dve", lambda v: v.tensor_scalar_add(tc, L["ones128c"][:, :], 1e-5))
    ph.emit("dve", lambda v: v.tensor_scalar(tc, L["ones128c"][:, :], -1.0, 1.0,
                                             ALU.mult, ALU.add))
    ph.flush("consts")
    ev_consts = P.snap(dx)

    ph = Phase(P)
    for s in range(n_seq):
        rsrc = L["x_in"][s].rearrange("(tt p) d -> p tt d", p=128)
        rdst = L[f"resid{s}"][:, :].rearrange("p (tt d) -> p tt d", tt=NT)
        for h in range(2):
            ph.dma("gp", dx, rdst[:, h * 8:(h + 1) * 8, :], rsrc[:, h * 8:(h + 1) * 8, :])
    ph.flush("loadx")
    ev_x = P.snap(dx)

    fdr1 = _Feeder(P, L, "W1", "w1bf")
    fdr2 = _Feeder(P, L, "W2", "w2bf")
    dep = [ev_x, ev_consts]
    if "attn_a" in parts:
        for s in range(n_seq):
            _set_seq(L, s)
            stkA = ExitStack()
            L["xlt"] = stkA.enter_context(nc.sbuf_tensor(f"xlta_{s}", [128, NDC * S], BF16))
            _ph_ln(P, L, s, 0, router=False, dep=dep, xlt=L["xlt"])
            _ph_attn_a(P, L, s, feeder=fdr1)
            stkA.close()
            dep = []
        fdr1.drain()
    if "ffn1" in parts:
        _ph_ffn_sparse(P, L, 1, n_seq, dep=dep)
        dep = []
    if "attn_b" in parts:
        for s in range(n_seq):
            _set_seq(L, s)
            stkB = ExitStack()
            L["xlt"] = stkB.enter_context(nc.sbuf_tensor(f"xltb_{s}", [128, NDC * S], BF16))
            _ph_ln(P, L, s, 2, router=False,
                   dep=dep + [L["ffn_done"][s], P.snap("c_act")], xlt=L["xlt"])
            _ph_attn_b(P, L, s, feeder=fdr2)
            stkB.close()
            dep = []
        fdr2.drain()
    if "ffn2" in parts:
        _ph_ffn_sparse(P, L, 3, n_seq, dep=dep)
        dep = []

    ph = Phase(P)
    evout = [P.snap("c_dve"), P.snap("c_pe")]
    for s in range(n_seq):
        ysrc = L[f"resid{s}"][:, :].rearrange("p (tt d) -> p tt d", tt=NT)
        ydst = L["y_out"][s].rearrange("(tt p) d -> p tt d", p=128)
        for h in range(2):
            ph.dma("gp", dx, ydst[:, h * 8:(h + 1) * 8, :], ysrc[:, h * 8:(h + 1) * 8, :],
                   waits=evout + ([dep[0]] if dep else []))
    ph.flush("store")

    ph = Phase(P)
    evd = P.snap(dx)
    evw = P.snap("dW")
    ph.emit("sp", lambda e: None, waits=[evd, evw], sem=False)
    ph.flush("final")


def _ph_ln(P, L, s, which, router, dep=(), xlt=None, xtm=None, xtm_off=0):
    nc = P.nc
    stk = ExitStack()
    sb = lambda name, shape, dt: stk.enter_context(nc.sbuf_tensor(name, shape, dt))
    normscr = sb(f"ln{s}{which}_ns", [128, 4 * D], F32)
    trash = L["trash"]
    ln32 = sb(f"ln{s}{which}_l32", [128, 2 * 1024], F32) if router else None
    ph = Phase(P)
    resid, stats = L["resid"], L["stats"]
    idf, ps_big, pb = L["idf"][:, :], L["ps_big"], L["pb"]
    ev_stB = {}
    ev_trans = {}
    ev_xtm = {}
    prev_bank = {b: P.snap("c_dve") for b in range(4)}
    prev_ln32 = {}
    prev_gate = {}
    prev_pbe = {0: P.snap("c_dve"), 1: P.snap("c_dve")}
    ev_enm = {}

    def col(tt, i):
        p4 = tt % 4
        return stats[:, p4 * 8 + i: p4 * 8 + i + 1]

    def stage_a(tt):
        rtile = resid[:, tt * 1024:(tt + 1) * 1024]
        w0 = list(dep) + [ev_stB.get(tt - 4)]
        e1 = ph.emit("dve", lambda v, a=col(tt, 0), b=rtile: v.reduce_sum(a, b, AX.X), waits=w0)
        e2 = ph.emit("act", lambda a, o=trash[:, :], i=rtile, c=col(tt, 2):
                     a.activation(o, i, AF.Square, bias=0.0, scale=1.0, accum_out=c),
                     waits=w0)
        return e1, e2

    def stage_b(tt, e1, e2):
        rtile = resid[:, tt * 1024:(tt + 1) * 1024]
        ns = normscr[:, (tt % 4) * 1024:((tt % 4) + 1) * 1024]
        em = ph.emit("dve", lambda v, a=col(tt, 1), b=col(tt, 0): v.tensor_scalar_mul(a, b, 1.0 / D), waits=[e1])
        emm = ph.emit("dve", lambda v, a=col(tt, 4), b=col(tt, 1): v.tensor_mul(a, b, b), waits=[em])
        v1 = ph.emit("dve", lambda v, a=col(tt, 5), b=col(tt, 2), c=col(tt, 4):
                     v.tensor_scalar(a, b, 1.0 / D, c, ALU.mult, ALU.subtract), waits=[e2, emm])
        v2 = ph.emit("dve", lambda v, a=col(tt, 5), b=col(tt, 5): v.tensor_scalar_add(a, b, 1e-5), waits=[v1])
        eln = ph.emit("act", lambda a, o=col(tt, 6), i=col(tt, 5): a.activation(o, i, AF.Ln, bias=0.0, scale=1.0), waits=[v2])
        ers = ph.emit("act", lambda a, o=col(tt, 7), i=col(tt, 6): a.activation(o, i, AF.Exp, bias=0.0, scale=-0.5), waits=[eln])
        enm = ph.emit("dve", lambda v, o=ns, i=rtile, m=col(tt, 1), r=col(tt, 7):
                      v.tensor_scalar(o, i, m, r, ALU.subtract, ALU.mult),
                      waits=[ers, ev_trans.get(tt - 4), ev_xtm.get(tt - 4)])
        ev_enm[tt] = enm
        ev_stB[tt] = enm
        if xtm is not None:
            ev_xtm[tt] = ph.emit("act",
                                 lambda a, o=xtm[:, (xtm_off + tt) * D:(xtm_off + tt + 1) * D],
                                 i=ns:
                                 a.activation(o, i, AF.Copy, bias=0.0, scale=1.0),
                                 waits=[enm])

    def stage_c(tt):
        par2 = tt % 2
        ns = normscr[:, (tt % 4) * 1024:((tt % 4) + 1) * 1024]
        enm = ev_enm[tt]
        last_tr = None
        ev_last = []
        for half in range(2):
            bank = (2 * tt + half) % 4
            bslots = ps_big[:, bank * 512:(bank + 1) * 512]
            trs = []
            for j in range(4):
                dc = half * 4 + j
                etr = ph.emit("pe", lambda t, o=bslots[:, j * 128:(j + 1) * 128],
                              i=ns[:, dc * 128:(dc + 1) * 128], ii=idf:
                              t.transpose(o, i, ii),
                              waits=[enm, prev_bank.get(bank)] if j == 0 else ())
                trs.append(etr)
            evs = []
            eng = "act" if half == 0 else "dve"
            if xlt is not None:
                for j in range(4):
                    dc = half * 4 + j
                    dst = xlt[:, dc * S + tt * 128: dc * S + tt * 128 + 128]
                    if eng == "act":
                        ea = ph.emit("act", lambda a, o=dst, i=bslots[:, j * 128:(j + 1) * 128]:
                                     a.activation(o, i, AF.Copy, bias=0.0, scale=1.0), waits=[trs[-1]])
                    else:
                        ea = ph.emit("dve", lambda v, o=dst, i=bslots[:, j * 128:(j + 1) * 128]:
                                     v.tensor_copy(o, i), waits=[trs[-1]])
                    evs.append(ea)
            if router:
                for j in range(4):
                    dc = half * 4 + j
                    dst32 = ln32[:, par2 * 1024 + dc * 128: par2 * 1024 + dc * 128 + 128]
                    if eng == "act":
                        ev = ph.emit("act", lambda a, o=dst32, i=bslots[:, j * 128:(j + 1) * 128]:
                                     a.activation(o, i, AF.Copy, bias=0.0, scale=1.0),
                                     waits=[trs[-1], prev_ln32.get(par2)])
                    else:
                        ev = ph.emit("dve", lambda v, o=dst32, i=bslots[:, j * 128:(j + 1) * 128]:
                                     v.tensor_copy(o, i),
                                     waits=[trs[-1], prev_ln32.get(par2)])
                    evs.append(ev)
            prev_bank[bank] = evs[-1]
            ev_last.extend(evs)
            last_tr = trs[-1]
        ev_trans[tt] = last_tr
        if router:
            pbank = pb[par2]
            wmm = ev_last + [prev_pbe.get(par2)]
            for dc in range(NDC):
                ph.emit("pe", lambda t, o=pbank[:, 0:16],
                        l=ln32[:, par2 * 1024 + dc * 128: par2 * 1024 + dc * 128 + 128],
                        r=L["wr_sb"][:, dc * NE:(dc + 1) * NE], s_=(dc == 0):
                        t.matmul(o, l, r, start=s_, stop=False),
                        waits=wmm if dc == 0 else (), sem=False)
            mm_ev = ph.emit("pe", lambda t, o=pbank[:, 0:16], l=L["ones32"][:1, :],
                            r=L["brr"][:1, :]: t.matmul(o, l, r, start=False, stop=True))
            prev_ln32[par2] = mm_ev
            uA = L["u16"][:, par2 * 16:(par2 + 1) * 16]
            uB = L["u16b"][:, par2 * 16:(par2 + 1) * 16]
            c = lambda i: stats[:, 32 + par2 * 8 + i: 32 + par2 * 8 + i + 1]
            e0 = ph.emit("dve", lambda v, o=c(0), i=pbank[:, 0:16]:
                         v.reduce_max(o, i, AX.X, negate=True),
                         waits=[mm_ev, prev_gate.get(par2)])
            e1b = ph.emit("act", lambda a, o=uA, i=pbank[:, 0:16], b=c(0):
                          a.activation(o, i, AF.Exp, bias=b, scale=1.0), waits=[e0])
            prev_pbe[par2] = e1b
            e2 = ph.emit("dve", lambda v, o=c(1), i=uA: v.reduce_sum(o, i, AX.X), waits=[e1b])
            e3 = ph.emit("dve", lambda v, o=c(2), i=c(1): v.reciprocal(o, i), waits=[e2])
            e4 = ph.emit("dve", lambda v, o=c(3), i=uA: v.reduce_max(o, i, AX.X), waits=[e1b])
            e5 = ph.emit("dve", lambda v, o=uB, i=uA, m=c(3): v.tensor_scalar(o, i, m, None, ALU.is_equal), waits=[e4])
            e6 = ph.emit("dve", lambda v, o=uB, i=uB: v.tensor_scalar(o, i, -1.0, 1.0, ALU.mult, ALU.add), waits=[e5])
            e7 = ph.emit("dve", lambda v, o=uB, i=uA, j=uB: v.tensor_mul(o, i, j), waits=[e6])
            e8 = ph.emit("dve", lambda v, o=c(4), i=uB: v.reduce_max(o, i, AX.X), waits=[e7])
            e9 = ph.emit("dve", lambda v, o=uB, i=uA, m=c(4): v.tensor_scalar(o, i, m, None, ALU.is_ge), waits=[e8])
            e10 = ph.emit("dve", lambda v, o=uB, i=uA, j=uB: v.tensor_mul(o, i, j), waits=[e9])
            gdst = L["gate"][:, tt * NE:(tt + 1) * NE]
            e11 = ph.emit("dve", lambda v, o=gdst, i=uB, r=c(2): v.tensor_scalar(o, i, r, None, ALU.mult), waits=[e10, e3])
            prev_gate[par2] = e11
            egt = ph.emit("pe", lambda t, o=pbank[0:16, 128:256], i=gdst, ii=idf:
                          t.transpose(o, i, ii), waits=[e11, prev_pbe[par2]])
            egte = ph.emit("dve", lambda v, o=L["gT"][0:16, tt * 128:(tt + 1) * 128],
                           i=pbank[0:16, 128:256]: v.tensor_copy(o, i), waits=[egt])
            prev_pbe[par2] = egte

    ab = {}
    for tt in range(NT):
        ab[tt] = stage_a(tt)
        if tt >= 1:
            stage_b(tt - 1, *ab.pop(tt - 1))
        if tt >= 2:
            stage_c(tt - 2)
    stage_b(NT - 1, *ab.pop(NT - 1))
    stage_c(NT - 2)
    stage_c(NT - 1)
    ph.flush(f"s{s}_ln{which}")
    stk.close()


def _ph_attn_a(P, L, s, feeder=None):
    nc = P.nc
    NG = 4
    HPG = NH // NG
    IW = HPG * HD
    es2 = ExitStack()
    sb = lambda name, shape, dt: es2.enter_context(nc.sbuf_tensor(name, shape, dt))
    wstage = [sb(f"aa_ws{i}_{s}", [128, 2048], BF16) for i in range(2)]
    wqT = sb(f"aa_wq_{s}", [128, NDC * IW], BF16)
    wkT = sb(f"aa_wk_{s}", [128, NDC * IW], BF16)
    wvT = sb(f"aa_wv_{s}", [128, NDC * IW], BF16)
    woT = sb(f"aa_wo_{s}", [128, 2 * 1024], BF16)
    qaT = sb(f"aa_q_{s}", [128, 2 * S], BF16)
    kaT = sb(f"aa_k_{s}", [128, 2 * S], BF16)
    vga = sb(f"aa_v_{s}", [128, NT * IW], BF16)
    outg = sb(f"aa_o_{s}", [128, 2 * S], BF16)
    ascr = sb(f"aa_s_{s}", [128, 4 * 192], F32)
    abf = sb(f"aa_ab_{s}", [128, 4 * 192], BF16)
    atT = sb(f"aa_at_{s}", [128, 2 * 256], BF16)
    xlt, resid, idb = L["xlt"], L["resid"], L["idb"][:, :]
    ps_big, pb, pt = L["ps_big"], L["pb"], L["pt"]
    Wsrcs = (L["Wq"], L["Wk"], L["Wv"])
    with es2:
        pre = {}
        for g in range(NG):
            ph = Phase(P)
            dA = "dA"

            def tr_batches(items, src_of, dst_of, prev_pt, wait_extra):
                done = []
                for b0 in range(0, len(items), 8):
                    chunk = items[b0:b0 + 8]
                    bank = pt[(b0 // 8) % 2]
                    trs = []
                    for j, itx in enumerate(chunk):
                        etr = ph.emit("pe", lambda t, o=bank[:, j * 128:(j + 1) * 128],
                                      i=src_of(itx), ii=idb: t.transpose(o, i, ii),
                                      waits=([wait_extra, prev_pt.get((b0 // 8) % 2)]
                                             if j == 0 else ()))
                        trs.append(etr)
                    for j, itx in enumerate(chunk):
                        ea = ph.emit("act", lambda a, o=dst_of(itx), i=bank[:, j * 128:(j + 1) * 128]:
                                     a.activation(o, i, AF.Copy, bias=0.0, scale=1.0),
                                     waits=[trs[-1]])
                        done.append(ea)
                    prev_pt[(b0 // 8) % 2] = done[-1]
                return done

            prev_pt = {0: P.snap("c_dve"), 1: P.snap("c_dve")}
            evq = pre.pop((g, 0), None) or ph.dma(
                "gp", dA, wstage[0][:, :].rearrange("p (it d) -> p it d", it=2),
                bass.AP(Wsrcs[0].tensor, (g * IW) * D, [[D, 128], [128 * D, 2], [1, D]]))
            evk = pre.pop((g, 1), None) or ph.dma(
                "gp", dA, wstage[1][:, :].rearrange("p (it d) -> p it d", it=2),
                bass.AP(Wsrcs[1].tensor, (g * IW) * D, [[D, 128], [128 * D, 2], [1, D]]))
            items = [(it, dc) for it in range(2) for dc in range(NDC)]
            evs_q = tr_batches(
                items, lambda x: wstage[0][:, x[0] * 1024 + x[1] * 128: x[0] * 1024 + x[1] * 128 + 128],
                lambda x: wqT[:, x[1] * IW + x[0] * 128: x[1] * IW + x[0] * 128 + 128],
                prev_pt, evq)
            evv = ph.dma("gp", dA, wstage[0][:, :].rearrange("p (it d) -> p it d", it=2),
                         bass.AP(Wsrcs[2].tensor, (g * IW) * D, [[D, 128], [128 * D, 2], [1, D]]),
                         waits=[evs_q[-1]])
            evs_k = tr_batches(
                items, lambda x: wstage[1][:, x[0] * 1024 + x[1] * 128: x[0] * 1024 + x[1] * 128 + 128],
                lambda x: wkT[:, x[1] * IW + x[0] * 128: x[1] * IW + x[0] * 128 + 128],
                prev_pt, evk)
            evo = ph.dma("gp", dA, wstage[1][:, :].rearrange("p (oc i) -> p oc i", oc=NDC),
                         bass.AP(L["Wo"].tensor, g * IW, [[D, 128], [128 * D, 8], [1, IW]]),
                         waits=[evs_k[-1]])
            tr_batches(
                items, lambda x: wstage[0][:, x[0] * 1024 + x[1] * 128: x[0] * 1024 + x[1] * 128 + 128],
                lambda x: wvT[:, x[1] * IW + x[0] * 128: x[1] * IW + x[0] * 128 + 128],
                prev_pt, evv)
            items_o = [(oc, it) for oc in range(NDC) for it in range(2)]
            tr_batches(
                items_o, lambda x: wstage[1][:, x[0] * 256 + x[1] * 128: x[0] * 256 + x[1] * 128 + 128],
                lambda x: woT[:, x[1] * 1024 + x[0] * 128: x[1] * 1024 + x[0] * 128 + 128],
                prev_pt, evo)
            ph.flush(f"s{s}_aa_g{g}_w")

            ph = Phase(P)
            pvt = {0: P.snap("c_dve"), 1: P.snap("c_dve")}
            for wi, (wT, dstT) in enumerate(((wqT, qaT), (wkT, kaT))):
                for it in range(2):
                    for ng in range(4):
                        key = (it * 4 + ng) % 2
                        pbank = pb[key]
                        emm = None
                        for dc in range(NDC):
                            emm = ph.emit("pe", lambda t, o=pbank[:, :],
                                          l=wT[:, dc * IW + it * 128: dc * IW + it * 128 + 128],
                                          r=xlt[:, dc * S + ng * 512: dc * S + (ng + 1) * 512],
                                          s_=(dc == 0), p_=(dc == NDC - 1):
                                          t.matmul(o, l, r, start=s_, stop=p_),
                                          waits=[pvt.get(key)] if dc == 0 else (),
                                          sem=False if dc < NDC - 1 else None)
                        dst = dstT[:, it * S + ng * 512: it * S + (ng + 1) * 512]
                        eev = ph.emit("act", lambda a, o=dst, i=pbank[:, :]:
                                      a.activation(o, i, AF.Copy, bias=0.0, scale=1.0), waits=[emm])
                        pvt[key] = eev
            for tt in range(NT):
                key = tt % 2
                pbank = pb[key]
                emm = None
                for dc in range(NDC):
                    emm = ph.emit("pe", lambda t, o=pbank[:, 0:IW],
                                  l=xlt[:, dc * S + tt * 128: dc * S + tt * 128 + 128],
                                  r=wvT[:, dc * IW:(dc + 1) * IW],
                                  s_=(dc == 0), p_=(dc == NDC - 1):
                                  t.matmul(o, l, r, start=s_, stop=p_),
                                  waits=[pvt.get(key)] if dc == 0 else (),
                                  sem=False if dc < NDC - 1 else None)
                dst = vga[:, tt * IW:(tt + 1) * IW]
                eev = ph.emit("act", lambda a, o=dst, i=pbank[:, 0:IW]:
                              a.activation(o, i, AF.Copy, bias=0.0, scale=1.0), waits=[emm])
                pvt[key] = eev
            ev_proj = P.snap("c_act")

            sc_free = {}
            prev_ab4 = {}
            prev_abf_rd = {}
            prev_at = {}
            prev_av = {0: pvt.get(0), 1: pvt.get(1)}
            stc = L["stats"]
            pending = []
            first = True

            def emit_tail(w, h, par4, e_ab, idx):
                par2 = idx % 2
                it = h // 2
                rb = 64 * (h % 2)
                absl = abf[:, par4 * 192:(par4 + 1) * 192]
                bank = pt[par2]
                e_t1 = ph.emit("pe", lambda t, o=bank[0:64, 0:128], i=absl[:, 0:64], ii=idb:
                               t.transpose(o, i, ii), waits=[e_ab, prev_at.get(par2)])
                e_t2 = ph.emit("pe", lambda t, o=bank[:, 128:256], i=absl[:, 64:192], ii=idb:
                               t.transpose(o, i, ii))
                prev_abf_rd[par4] = e_t2
                atsl = atT[:, par2 * 256:(par2 + 1) * 256]
                e_te = ph.emit("dve", lambda v, o=atsl, i=bank[:, 0:256]:
                               v.tensor_copy(o, i), waits=[e_t2])
                prev_at[par2] = e_te
                avp = pb[par2]
                ph.emit("pe", lambda t, o=avp[0:64, 0:128],
                        l=vga[0:64, h * 64: h * 64 + 64],
                        r=atsl[0:64, 0:128]:
                        t.matmul(o, l, r, start=True, stop=False),
                        waits=[e_te, prev_av.get(par2)], sem=False)
                e_a2 = ph.emit("pe", lambda t, o=avp[0:64, 0:128],
                               l=vga[:, w * IW + h * 64: w * IW + h * 64 + 64],
                               r=atsl[:, 128:256]:
                               t.matmul(o, l, r, start=False, stop=True))
                dst = outg[rb:rb + 64, it * S + w * 128: it * S + w * 128 + 128]
                e_oe = ph.emit("act", lambda a, o=dst, i=avp[0:64, 0:128]:
                               a.activation(o, i, AF.Copy, bias=0.0, scale=1.0), waits=[e_a2])
                prev_av[par2] = e_oe

            for idx, (w, h) in enumerate([(w, h) for w in range(NT) for h in range(HPG)]):
                par4 = idx % 4
                it = h // 2
                rb = 64 * (h % 2)
                qsl = qaT[rb:rb + 64, it * S + w * 128: it * S + w * 128 + 128]
                scp = ps_big[:, par4 * 512: par4 * 512 + 192]
                e_s1 = ph.emit("pe", lambda t, o=scp[:, 0:64], l=qsl,
                               r=kaT[rb:rb + 64, it * S: it * S + 64]:
                               t.matmul(o, l, r, start=True, stop=True),
                               waits=[sc_free.get(par4)] + ([ev_proj] if first else []))
                first = False
                e_s2 = ph.emit("pe", lambda t, o=scp[:, 64:192], l=qsl,
                               r=kaT[rb:rb + 64, it * S + w * 128: it * S + w * 128 + 128]:
                               t.matmul(o, l, r, start=True, stop=True))
                asl = ascr[:, par4 * 192:(par4 + 1) * 192]
                absl = abf[:, par4 * 192:(par4 + 1) * 192]
                c1 = stc[:, 48 + par4: 48 + par4 + 1]
                e_e = ph.emit("act", lambda a, o=asl, i=scp:
                              a.activation(o, i, AF.Exp, bias=0.0, scale=0.125),
                              waits=[e_s1, e_s2, prev_ab4.get(par4)])
                sc_free[par4] = e_e
                if len(pending) > 1:
                    emit_tail(*pending.pop(0))
                e_r = ph.emit("dve", lambda v, o=c1, i=asl: v.reduce_sum(o, i, AX.X), waits=[e_e])
                e_rc = ph.emit("dve", lambda v, o=c1, i=c1: v.reciprocal(o, i), waits=[e_r])
                e_ab = ph.emit("dve", lambda v, o=absl, i=asl, r=c1:
                               v.tensor_scalar(o, i, r, None, ALU.mult),
                               waits=[e_rc, prev_abf_rd.get(par4)])
                prev_ab4[par4] = e_ab
                pending.append((w, h, par4, e_ab, idx))
            while pending:
                emit_tail(*pending.pop(0))

            lastout = P.snap("c_act")
            pvt2 = dict(prev_av)
            for tt in range(NT):
                for oc in range(2):
                    key = (tt * 2 + oc) % 2
                    pbank = pb[key]
                    emm = None
                    for it in range(2):
                        emm = ph.emit("pe", lambda t, o=pbank[:, :],
                                      l=outg[:, it * S + tt * 128: it * S + tt * 128 + 128],
                                      r=woT[:, it * 1024 + oc * 512: it * 1024 + (oc + 1) * 512],
                                      s_=(it == 0), p_=(it == 1):
                                      t.matmul(o, l, r, start=s_, stop=p_),
                                      waits=[lastout, pvt2.get(key)] if it == 0 else (),
                                      sem=False if it < 1 else None)
                    dst = resid[:, tt * 1024 + oc * 512: tt * 1024 + (oc + 1) * 512]
                    eev = ph.emit("dve", lambda v, o=dst, i=dst, j=pbank[:, :]:
                                  v.tensor_add(o, i, j), waits=[emm])
                    pvt2[key] = eev
            if g + 1 < NG:
                pre[(g + 1, 0)] = ph.dma(
                    "gp", dA, wstage[0][:, :].rearrange("p (it d) -> p it d", it=2),
                    bass.AP(Wsrcs[0].tensor, ((g + 1) * IW) * D, [[D, 128], [128 * D, 2], [1, D]]))
                pre[(g + 1, 1)] = ph.dma(
                    "gp", dA, wstage[1][:, :].rearrange("p (it d) -> p it d", it=2),
                    bass.AP(Wsrcs[1].tensor, ((g + 1) * IW) * D, [[D, 128], [128 * D, 2], [1, D]]))
            ph.flush(f"s{s}_aa_g{g}_at")
            if feeder is not None:
                feeder.feed(16)


def _ph_ffn(P, L, s, which):
    nc = P.nc
    NBLK = 4
    BH = DH // NBLK
    NHC = BH // 128
    NTG = 4
    es2 = ExitStack()
    sb = lambda name, shape, dt: es2.enter_context(nc.sbuf_tensor(name, shape, dt))
    w1b = [sb(f"f{which}w1{i}_{s}", [128, NDC * BH], BF16) for i in range(2)]
    w2b = [sb(f"f{which}w2{i}_{s}", [128, NHC * D], BF16) for i in range(2)]
    hT = sb(f"f{which}h_{s}", [128, NHC * 512], BF16)
    xlt, resid, gate = L["xlt"], L["resid"], L["gate"]
    ps_big, pb = L["ps_big"], L["pb"]
    with es2:
        ph = Phase(P)
        dW = "dW"
        pv_hbank = {}
        pv_ybank = {}
        w_ld = {}

        def load_w(e, blk, par, waits):
            ph.dma("gp", dW, w1b[par][:, :].rearrange("p (dc hh) -> p dc hh", dc=NDC),
                   bass.AP(L["W1"].tensor, e * D * DH + blk * BH,
                           [[DH, 128], [128 * DH, NDC], [1, BH]]), waits=waits)
            return ph.dma("gp", dW, w2b[par][:, :].rearrange("p (hc d) -> p hc d", hc=NHC),
                          bass.AP(L["W2"].tensor, e * DH * D + blk * BH * D,
                                  [[D, 128], [128 * D, NHC], [1, D]]))

        seqi = [(e, blk) for e in range(NE) for blk in range(NBLK)]
        w_ld[0] = load_w(*seqi[0], 0, [])
        w_ld[1] = load_w(*seqi[1], 1, [])
        nload = 2
        for ib, (e, blk) in enumerate(seqi):
            par = ib % 2
            lastB = None
            for g in range(NTG):
                relu_evs = []
                for hc in range(NHC):
                    bank = hc % 4
                    psl = ps_big[:, bank * 512:(bank + 1) * 512]
                    emm = None
                    for dc in range(NDC):
                        w0 = [w_ld[par], pv_hbank.get(bank)] if dc == 0 else ()
                        emm = ph.emit("pe", lambda t, o=psl,
                                      l=w1b[par][:, dc * BH + hc * 128: dc * BH + hc * 128 + 128],
                                      r=xlt[:, dc * S + g * 512: dc * S + (g + 1) * 512],
                                      s_=(dc == 0), p_=(dc == NDC - 1):
                                      t.matmul(o, l, r, start=s_, stop=p_),
                                      waits=w0, sem=False if dc < NDC - 1 else None)
                    b1col = L["b1c"][:, e * 32 + blk * NHC + hc: e * 32 + blk * NHC + hc + 1]
                    erl = ph.emit("act", lambda a, o=hT[:, hc * 512:(hc + 1) * 512], i=psl, b=b1col:
                                  a.activation(o, i, AF.Relu, bias=b, scale=1.0), waits=[emm])
                    pv_hbank[bank] = erl
                    relu_evs.append(erl)
                for tt4 in range(4):
                    tt = g * 4 + tt4
                    for dc2 in range(2):
                        ypar = (tt4 * 2 + dc2) % 2
                        pbank = pb[ypar]
                        emm = None
                        for hc in range(NHC):
                            last = (hc == NHC - 1)
                            w0 = [relu_evs[hc], pv_ybank.get(ypar) if hc == 0 else None]
                            emm = ph.emit("pe", lambda t, o=pbank[:, :],
                                          l=hT[:, hc * 512 + tt4 * 128: hc * 512 + tt4 * 128 + 128],
                                          r=w2b[par][:, hc * D + dc2 * 512: hc * D + (dc2 + 1) * 512],
                                          s_=(hc == 0), p_=last:
                                          t.matmul(o, l, r, start=s_, stop=p_),
                                          waits=w0, sem=False if not last else None)
                        gcol = gate[:, tt * NE + e: tt * NE + e + 1]
                        dst = resid[:, tt * 1024 + dc2 * 512: tt * 1024 + (dc2 + 1) * 512]
                        est = ph.emit("dve", lambda v, o=dst, i=pbank[:, :], sc=gcol, j=dst:
                                      v.scalar_tensor_tensor(o, i, sc, j, ALU.mult, ALU.add),
                                      waits=[emm, pv_ybank.get(ypar)])
                        pv_ybank[ypar] = est
                        lastB = emm
            if nload < len(seqi):
                w_ld[par] = load_w(*seqi[nload], par, [lastB])
                nload += 1
        for tt in range(NT):
            for dc2 in range(2):
                ypar = (tt * 2 + dc2) % 2
                pbank = pb[ypar]
                emm = ph.emit("pe", lambda t, o=pbank[:, :],
                              l=L["gT"][0:16, tt * 128:(tt + 1) * 128],
                              r=L["b2nat"][0:16, dc2 * 512:(dc2 + 1) * 512]:
                              t.matmul(o, l, r, start=True, stop=True),
                              waits=[pv_ybank.get(ypar)])
                dst = resid[:, tt * 1024 + dc2 * 512: tt * 1024 + (dc2 + 1) * 512]
                est = ph.emit("dve", lambda v, o=dst, i=dst, j=pbank[:, :]:
                              v.tensor_add(o, i, j), waits=[emm, pv_ybank.get(ypar)])
                pv_ybank[ypar] = est
        ph.flush(f"s{s}_ffn{which}")


def _ph_ffn_sparse(P, L, which, n_seq, dep=()):
    nc = P.nc
    ntile = n_seq * NT
    stk = ExitStack()
    sb = lambda name, shape, dt: stk.enter_context(nc.sbuf_tensor(name, shape, dt))
    pref = f"fs{which}"
    sloti = sb(pref + "_sl", [128, 2 * ntile], I32)
    g12 = sb(pref + "_g12", [128, 2 * ntile], F32)
    b1all = sb(pref + "_b1a", [128, 32 * CAP_G], F32)
    w1i = sb(pref + "_w1i", [128, CAP_G], I32)
    w2i = sb(pref + "_w2i", [128, CAP_G], I32)
    stk2 = ExitStack()
    xtm = stk2.enter_context(nc.sbuf_tensor(pref + "_xtm", [128, ntile * D], BF16))
    for s in range(n_seq):
        _set_seq(L, s)
        _ph_ln(P, L, s, which, router=True, dep=(dep if s == 0 else ()),
               xtm=xtm, xtm_off=s * NT)
    _ph_meta(P, L, which, ntile, xtm, sloti, g12, b1all, w1i, w2i)
    stk2.close()
    if which == 1:
        w1src = bass.AP(L["w1bf"].tensor, 0, [[DH, NE * D], [1, DH]])
        w2src = bass.AP(L["W2"].tensor, 0, [[D, NE * DH], [1, D]])
    else:
        w1src = bass.AP(L["w1bf"].tensor, 0, [[DH, NE * D], [1, DH]])
        w2src = bass.AP(L["w2bf"].tensor, 0, [[D, NE * DH], [1, D]])
    _ph_groups(P, L, which, b1all, w1i, w2i, w1src, w2src)
    _ph_consume(P, L, which, ntile, sloti, g12)
    stk.close()


def _ph_meta(P, L, which, ntile, xtm, sloti, g12, b1all, w1i, w2i):
    nc = P.nc
    stk = ExitStack()
    sb = lambda name, shape, dt: stk.enter_context(nc.sbuf_tensor(name, shape, dt))
    pref = f"fm{which}"
    ohb = sb(pref + "_oh", [128, ntile * 16], F32)
    mcols = sb(pref + "_mc", [16, 16], F32)
    Mge = sb(pref + "_mge", [16, CAP_G], F32)
    ebc16 = sb(pref + "_eb16", [16, CAP_G], F32)
    eoh = sb(pref + "_eoh", [16, CAP_G], F32)
    ebc128 = sb(pref + "_eb128", [128, CAP_G], F32)
    wif = sb(pref + "_wif", [128, 2 * CAP_G], F32)
    eogr = sb(pref + "_eog", [1, CAP_G], F32)
    tcsb = sb(pref + "_tc", [16, 32], F32)
    tctsb = sb(pref + "_tct", [32, 16], F32)
    cbsb = sb(pref + "_cb", [32, 16], F32)
    brow = sb(pref + "_br", [1, 16], F32)
    cbrow = sb(pref + "_cbr", [1, 512], F32)
    Ssb = sb(pref + "_S", [128, 2 * 16], F32)
    mscr = sb(pref + "_ms", [128, 40], F32)
    b1nat = sb(pref + "_b1n", [NE, DH], F32)
    ph = Phase(P)
    ev_b1n = ph.dma("sp", "dX", b1nat[:, :], L["b1_in"])
    pb0, pb1, ps_big = L["pb"][0], L["pb"][1], L["ps_big"]
    io16, io16r = L["io16"], L["io16r"]
    onesr = L["ones32"]
    d_pe, d_dve, d_act = P.snap("c_pe"), P.snap("c_dve"), P.snap("c_act")
    d_dw = P.snap("dW")

    _last = [None]

    def dv(fn, waits=()):
        e = ph.emit("dve", fn, waits=list(waits) + [_last[0]])
        _last[0] = e
        return e

    for T in range(ntile):
        gate_s = L[f"gate{T // NT}"]
        tt = T % NT
        dv(lambda v, T=T, g=gate_s, tt=tt: v.tensor_scalar(
            ohb[:, T * 16:(T + 1) * 16], g[:, tt * NE:(tt + 1) * NE], 0.0, None, ALU.is_gt),
           waits=[d_act, d_pe] if T == 0 else ())
    ev_oh = _last[0]
    ev_tc = None
    for T in range(ntile):
        ev_tc = ph.emit("pe", lambda t, T=T: t.matmul(
            pb0[0:16, 64 + T:65 + T], ohb[:, T * 16:(T + 1) * 16], L["ones128c"][:, :],
            start=True, stop=True), waits=[ev_oh, d_act, d_dve] if T == 0 else ())
    ev_tce = ph.emit("act", lambda a: a.activation(tcsb[:, 0:ntile], pb0[0:16, 64:64 + ntile],
                                                   AF.Copy, bias=0.0, scale=1.0), waits=[ev_tc])
    dv(lambda v: v.reduce_sum(mcols[:, 0:1], tcsb[:, 0:ntile], AX.X), waits=[ev_tce])
    dv(lambda v: v.tensor_scalar(
        mcols[:, 2:2 + THRN], L["thr16"][:, :], mcols[:, 0:1], None, ALU.is_lt))
    dv(lambda v: v.reduce_sum(mcols[:, 10:11], mcols[:, 2:2 + THRN], AX.X))
    ev_pc = dv(lambda v: v.tensor_scalar(
        mcols[:, 11:12], mcols[:, 10:11], L["c512"][:, :], None, ALU.mult))
    ev_bm = ph.emit("pe", lambda t: t.matmul(pb0[0:16, 0:1], L["su16"][:, :], mcols[:, 11:12],
                                             start=True, stop=True),
                    waits=[ev_pc, ev_tce])
    ev_bc = ph.emit("act", lambda a: a.activation(mcols[:, 12:13], pb0[0:16, 0:1], AF.Copy,
                                                  bias=0.0, scale=1.0), waits=[ev_bm])
    ev_brm = ph.emit("pe", lambda t: t.transpose(pb0[0:1, 16:32], mcols[:, 12:13],
                                                 L["idf"][0:16, 0:16]),
                     waits=[ev_bc])
    ev_br = dv(lambda v: v.tensor_copy(brow[0:1, 0:16], pb0[0:1, 16:32]), waits=[ev_brm])
    ev_mg = dv(lambda v: v.tensor_scalar(Mge[:, :], L["g512"][:, :], mcols[0:16, 12:13],
                                         None, ALU.is_ge), waits=[ev_bc])
    ev_em = ph.emit("pe", lambda t: t.matmul(pb1[0:1, 0:CAP_G], L["ones16c"][:, :], Mge[:, :],
                                             start=True, stop=True), waits=[ev_mg, d_act, d_dve])
    ev_eog = ph.emit("act", lambda a: a.activation(eogr[0:1, :], pb1[0:1, 0:CAP_G], AF.Copy,
                                                   bias=-1.0, scale=1.0), waits=[ev_em])
    ev_b16m = ph.emit("pe", lambda t: t.matmul(pb0[0:16, 32:32 + CAP_G], onesr[0:1, 0:16],
                                               eogr[0:1, :], start=True, stop=True),
                      waits=[ev_eog])
    ev_b16 = dv(lambda v: v.tensor_copy(ebc16[:, :], pb0[0:16, 32:32 + CAP_G]),
                waits=[ev_b16m])
    ev_eoh = dv(lambda v: v.tensor_scalar(eoh[:, :], ebc16[:, :], L["iota16"][:, :],
                                          None, ALU.is_equal))
    ev_b128m = ph.emit("pe", lambda t: t.matmul(pb1[:, 32:32 + CAP_G], onesr[0:1, :],
                                                eogr[0:1, :], start=True, stop=True),
                       waits=[ev_eog])
    ev_b128 = dv(lambda v: v.tensor_copy(ebc128[:, :], pb1[:, 32:32 + CAP_G]),
                 waits=[ev_b128m])
    dv(lambda v: v.tensor_scalar(wif[:, 0:CAP_G], ebc128[:, :], L["c1024"][:, :],
                                 L["iota128"][:, :], ALU.mult, ALU.add))
    dv(lambda v: v.tensor_scalar(wif[:, CAP_G:2 * CAP_G], ebc128[:, :], L["c4096"][:, :],
                                 L["iota128"][:, :], ALU.mult, ALU.add))
    dv(lambda v: v.tensor_copy(w1i[:, :], wif[:, 0:CAP_G]))
    ev_wi = dv(lambda v: v.tensor_copy(w2i[:, :], wif[:, CAP_G:2 * CAP_G]))
    ev_b1mm = None
    for hc in range(32):
        ev_b1mm = ph.emit("pe", lambda t, hc=hc: t.matmul(
            ps_big[:, hc * CAP_G:(hc + 1) * CAP_G], b1nat[0:16, hc * 128:(hc + 1) * 128],
            eoh[:, :], start=True, stop=True),
            waits=[ev_eoh, ev_b1n, d_act, d_dve] if hc == 0 else ())
    ev_b1e = ph.emit("act", lambda a: a.activation(
        b1all[:, :], ps_big[:, 0:32 * CAP_G], AF.Copy, bias=0.0, scale=1.0),
        waits=[ev_b1mm])
    ev_tct = ph.emit("pe", lambda t: t.transpose(pb1[0:ntile, 64:80], tcsb[0:16, 0:ntile],
                                                 L["idf"][0:16, 0:16]),
                     waits=[ev_tce, ev_b128])
    ev_tcte = dv(lambda v, n=ntile: v.tensor_copy(tctsb[0:n, :], pb1[0:n, 64:80]), waits=[ev_tct])
    ph.emit("pe", lambda t, n=ntile: t.matmul(pb1[0:n, 80:96], L["su32"][0:n, 0:n], tctsb[0:n, :],
                                              start=True, stop=False), waits=[ev_tcte], sem=False)
    ev_cum = ph.emit("pe", lambda t, n=ntile: t.matmul(pb1[0:n, 80:96], onesr[0:1, 0:n],
                                                       brow[0:1, 0:16], start=False, stop=True),
                     waits=[ev_br])
    ev_cbe = ph.emit("act", lambda a, n=ntile: a.activation(cbsb[0:n, :], pb1[0:n, 80:96], AF.Copy,
                                                            bias=0.0, scale=1.0), waits=[ev_cum])
    ev_d1 = ph.dma("sp", "dX", L["meta_d"], cbsb[:, :], waits=[ev_cbe])
    ev_d2 = ph.dma("sp", "dX", cbrow[0:1, :],
                   bass.AP(L["meta_d"].tensor, 0, [[512, 1], [1, 512]]), waits=[ev_d1])
    ev_Sread = {}
    ev_Se = {}
    first_sc = [d_dw]
    A, B = mscr[:, 0:16], mscr[:, 16:32]
    ehi, tmp = mscr[:, 32:33], mscr[:, 33:34]
    sc1, sc2 = mscr[:, 35:36], mscr[:, 36:37]
    for T in range(ntile):
        sl = T % 2
        gate_s = L[f"gate{T // NT}"]
        tt = T % NT
        ph.emit("pe", lambda t, T=T, sl=sl: t.matmul(
            pb1[:, 96 + sl * 16:112 + sl * 16], L["su128"][:, :], ohb[:, T * 16:(T + 1) * 16],
            start=True, stop=False),
            waits=[ev_Sread.get(T - 2), ev_Se.get(T - 1),
                   ev_d2 if T == 0 else None], sem=False)
        ev_Sm = ph.emit("pe", lambda t, T=T, sl=sl: t.matmul(
            pb1[:, 96 + sl * 16:112 + sl * 16], onesr[0:1, :], cbrow[0:1, T * 16:T * 16 + 16],
            start=False, stop=True))
        ev_Se[T] = ph.emit("act", lambda a, sl=sl: a.activation(
            Ssb[:, sl * 16:(sl + 1) * 16], pb1[:, 96 + sl * 16:112 + sl * 16], AF.Copy,
            bias=0.0, scale=1.0), waits=[ev_Sm])
        oh_t = ohb[:, T * 16:(T + 1) * 16]
        G_t = gate_s[:, tt * NE:(tt + 1) * NE]
        S_sl = Ssb[:, sl * 16:(sl + 1) * 16]
        dv(lambda v, o=oh_t: v.tensor_mul(A, o, io16[:, :]))
        dv(lambda v: v.reduce_max(ehi, A, AX.X))
        dv(lambda v, o=oh_t: v.tensor_mul(A, o, io16r[:, :]))
        dv(lambda v: v.reduce_max(tmp, A, AX.X))
        dv(lambda v: v.tensor_scalar(A, io16r[:, :], tmp, None, ALU.is_equal))
        dv(lambda v, g=G_t: v.tensor_mul(B, g, A))
        dv(lambda v, T=T: v.reduce_sum(g12[:, 2 * T:2 * T + 1], B, AX.X))
        dv(lambda v, ss=S_sl: v.tensor_mul(B, ss, A), waits=[ev_Se[T]])
        dv(lambda v: v.reduce_sum(sc1, B, AX.X))
        ev_i1 = dv(lambda v, T=T: v.tensor_copy(sloti[:, 2 * T:2 * T + 1], sc1))
        dv(lambda v: v.tensor_scalar(A, io16[:, :], ehi, None, ALU.is_equal))
        dv(lambda v, g=G_t: v.tensor_mul(B, g, A))
        dv(lambda v, T=T: v.reduce_sum(g12[:, 2 * T + 1:2 * T + 2], B, AX.X))
        dv(lambda v, ss=S_sl: v.tensor_mul(B, ss, A))
        dv(lambda v: v.reduce_sum(sc2, B, AX.X))
        ev_i2 = dv(lambda v, T=T: v.tensor_copy(sloti[:, 2 * T + 1:2 * T + 2], sc2))
        ev_Sread[T] = ev_i2
        ph.idma_scatter("dW", L["xg_d"], sloti[:, 2 * T:2 * T + 1],
                        xtm[:, T * D:(T + 1) * D],
                        waits=first_sc + [ev_i1, d_act])
        first_sc = []
        ph.idma_scatter("dW", L["xg_d"], sloti[:, 2 * T + 1:2 * T + 2],
                        xtm[:, T * D:(T + 1) * D], waits=[ev_i2])
    if DEBUG and which == 1:
        evl = P.snap("c_dve")
        ph.dma("sp", "dX", L["dbg_sl"], sloti[:, :], waits=[evl])
        ph.dma("sp", "dX", L["dbg_g12"], g12[:, :], waits=[evl])
        ph.dma("sp", "dX", L["dbg_cb"], cbsb[:, :], waits=[ev_cbe])
        ph.dma("sp", "dX", L["dbg_eog"], eogr[0:1, :], waits=[ev_eog])
        ph.dma("sp", "dX", L["dbg_mc"], mcols[:, :], waits=[evl])
        ph.dma("sp", "dX", L["dbg_b1"], b1all[:, :], waits=[ev_b1e])
        evs = P.snap("dW")
        ph.dma("gp", "dW", L["dbg_xg"].rearrange("(g p) d -> p g d", p=128),
               bass.AP(L["xg_d"].tensor, 0, [[D, 128], [128 * D, CAP_S // 128], [1, D]]),
               waits=[evs])
    ph.flush(f"meta{which}")
    stk.close()


def _ph_groups(P, L, which, b1all, w1i, w2i, w1src, w2src):
    nc = P.nc
    stk = ExitStack()
    sb = lambda name, shape, dt: stk.enter_context(nc.sbuf_tensor(name, shape, dt))
    pref = f"fg{which}"
    w1g = [sb(pref + f"_w1{i}", [128, NDC * BLK], BF16) for i in range(2)]
    w2g = [sb(pref + f"_w2{i}", [128, HCL * D], BF16) for i in range(2)]
    xg_tm = [sb(pref + "_xt0", [128, 4 * D], BF16)] * 2
    xgT = sb(pref + "_xT", [128, NDC * 512], BF16)
    hT = sb(pref + "_hT", [128, HCL * 512], BF16)
    y_sb = [sb(pref + "_y0", [128, 4 * D], F32)] * 2
    ph = Phase(P)
    ps_big, pb, pt = L["ps_big"], L["pb"], L["pt"]
    idb = L["idb"][:, :]
    d_pe, d_dve, d_act = P.snap("c_pe"), P.snap("c_dve"), P.snap("c_act")
    ev_scat = P.snap("dW")
    ev_dx = P.snap("dX")
    ev_xload = {}
    ev_w = {}
    ev_wcons = {}
    ev_yd = {}
    ev_tr = {}
    pv_h = {0: d_act, 1: d_act}
    pv_y = {q: d_dve for q in range(4)}
    pv_pt = {0: d_dve, 1: d_dve}

    def gather(gi):
        g, blk = gi // NBLK2, gi % NBLK2
        par = gi % 2
        w0 = [ev_scat, ev_dx, ev_wcons.get(gi - 2)]
        for dc in range(NDC):
            ph.idma_gather("dW", w1g[par][:, dc * BLK:(dc + 1) * BLK], w1src,
                           w1i[:, g:g + 1], eo=dc * 128 * DH + blk * BLK,
                           waits=w0 if dc == 0 else ())
            w0 = ()
        for hcl in range(HCL):
            ev = ph.idma_gather("dW", w2g[par][:, hcl * D:(hcl + 1) * D], w2src,
                                w2i[:, g:g + 1], eo=(blk * BLK + hcl * 128) * D)
        ev_w[gi] = ev

    gather(0)
    gather(1)
    ngath = 2
    for g in range(CAP_G):
        ev_xload[g] = ph.dma(
            "sp", "dX", xg_tm[g % 2][:, :].rearrange("p (st d) -> p st d", st=4),
            bass.AP(L["xg_d"].tensor, 512 * g * D, [[D, 128], [128 * D, 4], [1, D]]),
            waits=[ev_scat, ev_tr.get(g - 1)])
        tr_done = []
        for st in range(4):
            for dc8 in range(0, NDC, 4):
                par2 = (st * 2 + dc8 // 4) % 2
                bank = pt[par2]
                trs = []
                for j in range(4):
                    dc = dc8 + j
                    etr = ph.emit("pe", lambda t, o=bank[:, j * 128:(j + 1) * 128],
                                  i=xg_tm[g % 2][:, st * D + dc * 128: st * D + dc * 128 + 128],
                                  ii=idb: t.transpose(o, i, ii),
                                  waits=([ev_xload[g], pv_pt[par2], d_pe] if j == 0 else ()))
                    trs.append(etr)
                for j in range(4):
                    dc = dc8 + j
                    ea = ph.emit("dve", lambda v, o=xgT[:, dc * 512 + st * 128: dc * 512 + st * 128 + 128],
                                 i=bank[:, j * 128:(j + 1) * 128]: v.tensor_copy(o, i),
                                 waits=[trs[-1]])
                pv_pt[par2] = ea
                tr_done.append(ea)
        ev_tr[g] = trs[-1]
        ev_ylast = None
        est_prev = {}
        for blk in range(NBLK2):
            gi = g * NBLK2 + blk
            par = gi % 2
            relu_evs = []
            for hcl in range(HCL):
                hb = pb[hcl % 2]
                w0 = [ev_w[gi], pv_h[hcl % 2], tr_done[-1] if hcl == 0 else None]
                emm = None
                for dc in range(NDC):
                    emm = ph.emit("pe", lambda t, o=hb[:, :],
                                  l=w1g[par][:, dc * BLK + hcl * 128: dc * BLK + hcl * 128 + 128],
                                  r=xgT[:, dc * 512:(dc + 1) * 512],
                                  s_=(dc == 0), p_=(dc == NDC - 1):
                                  t.matmul(o, l, r, start=s_, stop=p_),
                                  waits=w0 if dc == 0 else (),
                                  sem=False if dc < NDC - 1 else None)
                hc = blk * HCL + hcl
                erl = ph.emit("act", lambda a, o=hT[:, hcl * 512:(hcl + 1) * 512], i=hb[:, :],
                              b=b1all[:, hc * CAP_G + g: hc * CAP_G + g + 1]:
                              a.activation(o, i, AF.Relu, bias=b, scale=1.0), waits=[emm])
                pv_h[hcl % 2] = erl
                relu_evs.append(erl)
            for st in range(4):
                for dh in range(2):
                    q = (st * 2 + dh) % 4
                    yb = ps_big[:, q * 512:(q + 1) * 512]
                    w0 = [relu_evs[-1], pv_y[q]]
                    emm = None
                    for hcl in range(HCL):
                        emm = ph.emit("pe", lambda t, o=yb,
                                      l=hT[:, hcl * 512 + st * 128: hcl * 512 + st * 128 + 128],
                                      r=w2g[par][:, hcl * D + dh * 512: hcl * D + (dh + 1) * 512],
                                      s_=(hcl == 0), p_=(hcl == HCL - 1):
                                      t.matmul(o, l, r, start=s_, stop=p_),
                                      waits=w0 if hcl == 0 else (),
                                      sem=False if hcl < HCL - 1 else None)
                    ev_ylast = emm
                    dst = y_sb[g % 2][:, st * D + dh * 512: st * D + (dh + 1) * 512]
                    if blk == 0:
                        est = ph.emit("dve", lambda v, o=dst, i=yb: v.tensor_copy(o, i),
                                      waits=[emm, ev_yd.get(g - 1)])
                    else:
                        est = ph.emit("dve", lambda v, o=dst, i=yb: v.tensor_add(o, o, i),
                                      waits=[emm, est_prev.get((st, dh))])
                    est_prev[(st, dh)] = est
                    pv_y[q] = est
            ev_wcons[gi] = ev_ylast
            if ngath < CAP_G * NBLK2:
                gather(ngath)
                ngath += 1
        ev_yd[g] = ph.dma(
            "gp", "dW",
            bass.AP(L["y_d"].tensor, 512 * g * D, [[D, 128], [128 * D, 4], [1, D]]),
            y_sb[g % 2][:, :].rearrange("p (st d) -> p st d", st=4),
            waits=[pv_y[0], pv_y[1], pv_y[2], pv_y[3]])
    if DEBUG and which == 1:
        ph.dma("gp", "dW", L["dbg_y"].rearrange("(g p) d -> p g d", p=128),
               bass.AP(L["y_d"].tensor, 0, [[D, 128], [128 * D, CAP_S // 128], [1, D]]),
               waits=[ev_yd[CAP_G - 1]])
    ph.flush(f"grp{which}")
    stk.close()


def _ph_consume(P, L, which, ntile, sloti, g12):
    nc = P.nc
    stk = ExitStack()
    sb = lambda name, shape, dt: stk.enter_context(nc.sbuf_tensor(name, shape, dt))
    pref = f"fc{which}"
    yg = sb(pref + "_yg", [128, 4 * D], BF16)
    ygf = sb(pref + "_yf", [128, 4 * D], F32)
    b2nat = sb(pref + "_b2n", [NE, D], F32)
    ph = Phase(P)
    pb = L["pb"]
    ev_b2n = ph.dma("sp", "dX", b2nat[:, :], L["b2_in"])
    ev_ydall = P.snap("dW")
    d_act, d_dve = P.snap("c_act"), P.snap("c_dve")
    ysrc = bass.AP(L["y_d"].tensor, 0, [[D, CAP_S], [1, D]])
    ev_cast = {}
    pv_stt = {}
    pv_b2 = {0: P.snap("c_act"), 1: P.snap("c_act")}
    L["ffn_done"] = {}
    for T in range(ntile):
        par = T % 2
        resid = L[f"resid{T // NT}"]
        tt = T % NT
        evg = []
        for k in range(2):
            evg.append(ph.idma_gather(
                "dW", yg[:, (par * 2 + k) * D:(par * 2 + k + 1) * D], ysrc,
                sloti[:, 2 * T + k:2 * T + k + 1],
                waits=[ev_ydall, ev_cast.get(T - 2)] if k == 0 else ()))
        ev_cast[T] = ph.emit("act", lambda a, o=ygf[:, par * 2 * D:(par * 2 + 2) * D],
                             i=yg[:, par * 2 * D:(par * 2 + 2) * D]:
                             a.activation(o, i, AF.Copy, bias=0.0, scale=1.0),
                             waits=[evg[-1], d_act if T == 0 else None,
                                    pv_stt.get(T - 2)])
        e_sl = {}
        for k in range(2):
            for dh in range(2):
                e_sl[(k, dh)] = ph.emit(
                    "dve",
                    lambda v, o=resid[:, tt * D + dh * 512: tt * D + (dh + 1) * 512],
                    i=ygf[:, (par * 2 + k) * D + dh * 512:(par * 2 + k) * D + (dh + 1) * 512],
                    sc=g12[:, 2 * T + k:2 * T + k + 1]:
                    v.scalar_tensor_tensor(o, i, sc, o, ALU.mult, ALU.add),
                    waits=[ev_cast[T], e_sl.get((k - 1, dh)), pv_stt.get(T - 1)])
        pv_stt[T] = e_sl[(1, 1)]
        gTs = L[f"gT{T // NT}"]
        for dc2 in range(2):
            ypar = (T * 2 + dc2) % 2
            pbank = pb[ypar]
            emm = ph.emit("pe", lambda t, o=pbank[:, :],
                          l=gTs[0:16, tt * 128:(tt + 1) * 128],
                          r=b2nat[0:16, dc2 * 512:(dc2 + 1) * 512]:
                          t.matmul(o, l, r, start=True, stop=True),
                          waits=[pv_b2[ypar], d_dve, ev_b2n if T == 0 else None])
            dst = resid[:, tt * 1024 + dc2 * 512: tt * 1024 + (dc2 + 1) * 512]
            est = ph.emit("dve", lambda v, o=dst, i=dst, j=pbank[:, :]:
                          v.tensor_add(o, i, j), waits=[emm, e_sl[(1, dc2)]])
            pv_b2[ypar] = est
        if T % NT == NT - 1:
            L["ffn_done"][T // NT] = pv_b2[(T * 2 + 1) % 2]
    ph.flush(f"cons{which}")
    stk.close()


def _ph_attn_b(P, L, s, feeder=None):
    nc = P.nc
    NG = 4
    HPG = NH // NG
    IW = HPG * HD
    NIT = IW // 128
    es2 = ExitStack()
    sb = lambda name, shape, dt: es2.enter_context(nc.sbuf_tensor(name, shape, dt))
    wbuf = [sb(f"ab_w{i}_{s}", [128, NDC * IW], BF16) for i in range(2)]
    woutb = sb(f"ab_wo_{s}", [128, NIT * D], BF16)
    qbT = sb(f"ab_q_{s}", [128, NIT * S], BF16)
    kbT = sb(f"ab_k_{s}", [128, NIT * S], BF16)
    vb = sb(f"ab_v_{s}", [128, NT * IW], BF16)
    outg = sb(f"ab_o_{s}", [128, NIT * S], BF16)
    bscr = sb(f"ab_s_{s}", [128, 2 * S], F32)
    abf = sb(f"ab_ab_{s}", [128, 2 * S], BF16)
    atb = sb(f"ab_at_{s}", [128, 2 * 1024], BF16)
    xlt, resid, idb, mask01 = L["xlt"], L["resid"], L["idb"][:, :], L["mask01"]
    ps_big, pb, pt = L["ps_big"], L["pb"], L["pt"]
    stats = L["stats"]
    with es2:
        for g in range(NG):
            ph = Phase(P)
            dA = "dA"
            def load_qkv(off, par, waits):
                evs = []
                for dc in range(NDC):
                    evs.append(ph.dma(
                        "gp", dA,
                        wbuf[par][:, dc * IW:(dc + 1) * IW].rearrange("p (hh e) -> p hh e", hh=HPG),
                        bass.AP(L["Wqkv"].tensor, g * HPG * D * 192 + dc * 128 * 192 + off,
                                [[192, 128], [D * 192, HPG], [1, 64]]),
                        waits=waits if dc == 0 else ()))
                return evs[-1]

            ev_q = load_qkv(0, 0, [])
            ev_k = load_qkv(64, 1, [])
            ewo = ph.dma("gp", dA, woutb[:, :].rearrange("p (it d) -> p it d", it=NIT),
                         bass.AP(L["Wout"].tensor, g * IW * D, [[D, 128], [128 * D, NIT], [1, D]]))
            pvt = {0: P.snap("c_dve"), 1: P.snap("c_dve")}
            qdone = None
            for wi, dstT, wev in ((0, qbT, ev_q), (1, kbT, ev_k)):
                bcol = L["bqc"] if wi == 0 else L["bkc"]
                for it in range(NIT):
                    for ng in range(4):
                        key = (it * 4 + ng) % 2
                        pbank = pb[key]
                        emm = None
                        for dc in range(NDC):
                            w0 = [wev, pvt.get(key)] if dc == 0 else ()
                            emm = ph.emit("pe", lambda t, o=pbank[:, :],
                                          l=wbuf[wi][:, dc * IW + it * 128: dc * IW + it * 128 + 128],
                                          r=xlt[:, dc * S + ng * 512: dc * S + (ng + 1) * 512],
                                          s_=(dc == 0), p_=(dc == NDC - 1):
                                          t.matmul(o, l, r, start=s_, stop=p_),
                                          waits=w0, sem=False if dc < NDC - 1 else None)
                        dst = dstT[:, it * S + ng * 512: it * S + (ng + 1) * 512]
                        bc = bcol[:, g * NIT + it: g * NIT + it + 1]
                        eev = ph.emit("act", lambda a, o=dst, i=pbank[:, :], b=bc:
                                      a.activation(o, i, AF.Identity, bias=b, scale=1.0), waits=[emm])
                        pvt[key] = eev
                if wi == 0:
                    qdone = emm
            ev_v = load_qkv(128, 0, [qdone])
            for tt in range(NT):
                key = tt % 2
                pbank = pb[key]
                emm = None
                for dc in range(NDC):
                    w0 = [ev_v, pvt.get(key)] if dc == 0 else ()
                    emm = ph.emit("pe", lambda t, o=pbank[:, 0:IW],
                                  l=xlt[:, dc * S + tt * 128: dc * S + tt * 128 + 128],
                                  r=wbuf[0][:, dc * IW:(dc + 1) * IW],
                                  s_=(dc == 0), p_=False:
                                  t.matmul(o, l, r, start=s_, stop=False),
                                  waits=w0, sem=False)
                emm = ph.emit("pe", lambda t, o=pbank[:, 0:IW], l=L["ones32"][:1, :],
                              r=L["bvr"][:1, g * IW:(g + 1) * IW]:
                              t.matmul(o, l, r, start=False, stop=True))
                dst = vb[:, tt * IW:(tt + 1) * IW]
                eev = ph.emit("act", lambda a, o=dst, i=pbank[:, 0:IW]:
                              a.activation(o, i, AF.Copy, bias=0.0, scale=1.0), waits=[emm])
                pvt[key] = eev
            ph.flush(f"s{s}_ab_g{g}_w")

            ph = Phase(P)
            ev_proj = P.snap("c_act")
            half_ev = {}
            prev_bs = {}
            prev_abf_rd = {}
            prev_at = {0: P.snap("c_dve"), 1: P.snap("c_dve")}
            prev_avp = {}
            cidx = 0
            first = True
            pending = []
            tiles = [(h, qt) for h in range(HPG) for qt in range(NT)]

            def emit_tail(h, qt, par, e_ab, idx):
                nk = qt + 1
                it = h // 2
                rb = 64 * (h % 2)
                avp = pb[par]
                e_av = None
                nbt = (nk + 7) // 8
                te_evs = []
                for bt in range(nbt):
                    k0 = bt * 8
                    k1 = min(nk, k0 + 8)
                    bank = pt[(idx + bt) % 2]
                    trs = []
                    for kt in range(k0, k1):
                        ev = ph.emit("pe", lambda t, o=bank[:, (kt - k0) * 128:(kt - k0 + 1) * 128],
                                     i=abf[:, par * S + kt * 128: par * S + (kt + 1) * 128], ii=idb:
                                     t.transpose(o, i, ii),
                                     waits=[e_ab, prev_at.get((idx + bt) % 2)] if kt == k0 else ())
                        trs.append(ev)
                    prev_abf_rd[par] = trs[-1]
                    abase = ((idx + bt) % 2) * 1024
                    e_te = ph.emit("dve", lambda v, o=atb[:, abase: abase + (k1 - k0) * 128],
                                   i=bank[:, 0:(k1 - k0) * 128]: v.tensor_copy(o, i), waits=[trs[-1]])
                    prev_at[(idx + bt) % 2] = e_te
                    te_evs.append(e_te)
                for bt in range(nbt):
                    k0 = bt * 8
                    k1 = min(nk, k0 + 8)
                    abase = ((idx + bt) % 2) * 1024
                    for kt in range(k0, k1):
                        last = (kt == nk - 1)
                        ev = ph.emit("pe", lambda t, o=avp[0:64, 0:128],
                                     l=vb[:, kt * IW + h * 64: kt * IW + h * 64 + 64],
                                     r=atb[:, abase + (kt - k0) * 128: abase + (kt - k0 + 1) * 128],
                                     s_=(kt == 0), p_=last:
                                     t.matmul(o, l, r, start=s_, stop=p_),
                                     waits=[te_evs[bt], prev_avp.get(par)] if kt == k0 else (),
                                     sem=False if not last else None)
                        if last:
                            e_av = ev
                dst = outg[rb:rb + 64, it * S + qt * 128: it * S + qt * 128 + 128]
                e_oe = ph.emit("act", lambda a, o=dst, i=avp[0:64, 0:128]:
                               a.activation(o, i, AF.Copy, bias=0.0, scale=1.0), waits=[e_av])
                prev_avp[par] = e_oe

            for idx, (h, qt) in enumerate(tiles):
                par = idx % 2
                nk = qt + 1
                ncols = nk * 128
                it = h // 2
                rb = 64 * (h % 2)
                qsl = qbT[rb:rb + 64, it * S + qt * 128: it * S + qt * 128 + 128]
                e_e = None
                for c0 in range(0, ncols, 1024):
                    c1 = min(ncols, c0 + 1024)
                    half = cidx % 2
                    base = half * 1024
                    sc_evs = []
                    firstmm = True
                    for cc0 in range(c0, c1, 512):
                        cc1 = min(c1, cc0 + 512)
                        w0 = []
                        if firstmm:
                            w0 = [half_ev.get(half)] + ([ev_proj] if first else [])
                            first = False
                            firstmm = False
                        ev = ph.emit("pe", lambda t, o=ps_big[:, base + cc0 - c0: base + cc1 - c0],
                                     l=qsl, r=kbT[rb:rb + 64, it * S + cc0: it * S + cc1]:
                                     t.matmul(o, l, r, start=True, stop=True), waits=w0)
                        sc_evs.append(ev)
                    we = sc_evs + ([prev_bs.get(par)] if c0 == 0 else [])
                    e_e = ph.emit("act", lambda a, o=bscr[:, par * S + c0: par * S + c1],
                                  i=ps_big[:, base: base + c1 - c0]:
                                  a.activation(o, i, AF.Exp, bias=0.0, scale=0.125), waits=we)
                    half_ev[half] = e_e
                    cidx += 1
                if pending:
                    emit_tail(*pending.pop(0))
                bsl = bscr[:, par * S: par * S + ncols]
                dsl = bscr[:, par * S + (nk - 1) * 128: par * S + ncols]
                e_z = ph.emit("dve", lambda v, o=dsl, i=dsl, j=mask01[:, :]:
                              v.tensor_mul(o, i, j), waits=[e_e])
                cr = stats[:, 52 + par: 52 + par + 1]
                e_r = ph.emit("dve", lambda v, o=cr, i=bsl: v.reduce_sum(o, i, AX.X), waits=[e_z])
                e_rc = ph.emit("dve", lambda v, o=cr, i=cr: v.reciprocal(o, i), waits=[e_r])
                absl = abf[:, par * S: par * S + ncols]
                e_ab = ph.emit("dve", lambda v, o=absl, i=bsl, r=cr:
                               v.tensor_scalar(o, i, r, None, ALU.mult),
                               waits=[e_rc, prev_abf_rd.get(par)])
                prev_bs[par] = e_ab
                pending.append((h, qt, par, e_ab, idx))
            while pending:
                emit_tail(*pending.pop(0))
            lastout = P.snap("c_act")
            pvt2 = dict(prev_avp)
            for tt in range(NT):
                for oc in range(2):
                    key = (tt * 2 + oc) % 2
                    pbank = pb[key]
                    emm = None
                    for it in range(NIT):
                        last = (it == NIT - 1) and (g != 0)
                        w0 = [lastout, pvt2.get(key), ewo] if it == 0 else ()
                        emm = ph.emit("pe", lambda t, o=pbank[:, :],
                                      l=outg[:, it * S + tt * 128: it * S + tt * 128 + 128],
                                      r=woutb[:, it * 1024 + oc * 512: it * 1024 + (oc + 1) * 512],
                                      s_=(it == 0), p_=last:
                                      t.matmul(o, l, r, start=s_, stop=p_),
                                      waits=w0, sem=False if not last else None)
                    if g == 0:
                        emm = ph.emit("pe", lambda t, o=pbank[:, :], l=L["ones32"][:1, :],
                                      r=L["boutr"][:1, oc * 512:(oc + 1) * 512]:
                                      t.matmul(o, l, r, start=False, stop=True))
                    dst = resid[:, tt * 1024 + oc * 512: tt * 1024 + (oc + 1) * 512]
                    eev = ph.emit("dve", lambda v, o=dst, i=dst, j=pbank[:, :]:
                                  v.tensor_add(o, i, j), waits=[emm])
                    pvt2[key] = eev
            ph.flush(f"s{s}_ab_g{g}_at")
            if feeder is not None:
                feeder.feed(16)


KEYS_DEV = ("L0_attn_a_Wq", "L0_attn_a_Wk", "L0_attn_a_Wv", "L0_attn_a_Wo",
            "L0_attn_b_Wqkv", "L0_attn_b_Wout", "L0_ffn_W1", "L0_ffn_W2",
            "L0_ffn_b1", "L0_ffn_b2")


def kernel(**inputs):
    x = np.ascontiguousarray(np.asarray(inputs["L0_attn_a_x"], np.float32))
    aux = build_aux(inputs)
    nc = build_program(n_seq=2)
    base = {k: np.ascontiguousarray(np.asarray(inputs[k], np.float32)) for k in KEYS_DEV}
    base.update(aux)
    in_maps = []
    for c in range(8):
        m = dict(base)
        m["x"] = x[2 * c:2 * c + 2]
        in_maps.append(m)
    res = run_bass_kernel_spmd(nc, in_maps, core_ids=list(range(8)))
    return np.concatenate([r["y"] for r in res.results], axis=0)


# version 09
# speedup vs baseline: 1.1075x; 1.1075x over previous
import numpy as np
import ml_dtypes
from contextlib import ExitStack
import concourse.bass as bass
import concourse.mybir as mybir
from concourse.bass_utils import run_bass_kernel_spmd
from concourse.bass_core_rust import IndirectOffsetOnAxis

F32 = mybir.dt.float32
BF16 = mybir.dt.bfloat16
I32 = mybir.dt.int32
AF = mybir.ActivationFunctionType
ALU = mybir.AluOpType
AX = mybir.AxisListType
bfloat16 = ml_dtypes.bfloat16

S = 2048
D = 1024
NH = 16
HD = 64
WIN = 128
GLB = 64
DH = 4096
NE = 16
NT = S // 128
NDC = D // 128
NT2 = 2 * NT
CAP_G = 32
CAP_S = CAP_G * 512
THRN = 8
BLK = 512
NBLK2 = DH // BLK
HCL = BLK // 128

ENGS = ("pe", "dve", "act", "gp", "sp")
DEBUG = False


class Ev:
    __slots__ = ("sem", "val")

    def __init__(self, sem, val):
        self.sem = sem
        self.val = val


class Prog:
    def __init__(self, nc, es):
        self.nc = nc
        self.sems = {}
        self.cnt = {}
        for e in ENGS:
            self._mk(es, "c_" + e)
        for d in ("dW", "dA", "dX"):
            self._mk(es, d)
        self.waited = {e: {} for e in ENGS}

    def _mk(self, es, name):
        self.sems[name] = es.enter_context(self.nc.semaphore(name))
        self.cnt[name] = 0

    def snap(self, name):
        return Ev(self.sems[name], self.cnt[name])


class Phase:
    def __init__(self, P):
        self.P = P
        self.items = {e: [] for e in ENGS}
        self.start = {e: P.cnt["c_" + e] for e in ENGS}

    def emit(self, eng, fn, waits=(), sem=None, k=1):
        P = self.P
        ws = [w for w in waits if w is not None]
        if sem is None:
            sem = "c_" + eng
        if sem is not False:
            P.cnt[sem] += k
            ev = Ev(P.sems[sem], P.cnt[sem])
            self.items[eng].append((ws, fn, P.sems[sem], k))
            return ev
        self.items[eng].append((ws, fn, None, 0))
        return None

    def dma(self, queue_eng, dsem, dst, src, waits=()):
        return self.emit(queue_eng, lambda g, d=dst, s=src: g.dma_start(d, s),
                         waits=waits, sem=dsem, k=16)

    def idma_gather(self, dsem, dst, src_ap, idx_ap, eo=0, waits=()):
        return self.emit(
            "gp",
            lambda g, d=dst, s=src_ap, i=idx_ap, e=eo: g.indirect_dma_start(
                d, None, s, IndirectOffsetOnAxis(i, 0), element_offset=e),
            waits=waits, sem=dsem, k=16)

    def idma_scatter(self, dsem, dst_ap, idx_ap, src, eo=0, waits=()):
        return self.emit(
            "gp",
            lambda g, d=dst_ap, i=idx_ap, s=src, e=eo: g.indirect_dma_start(
                d, IndirectOffsetOnAxis(i, 0), s, None, element_offset=e),
            waits=waits, sem=dsem, k=16)

    def flush(self, name):
        P = self.P
        nc = P.nc
        with nc.Block(name) as b:
            def mk(eng_name):
                def go(eng):
                    pos = self.start[eng_name]
                    own = P.sems["c_" + eng_name]
                    for ws, fn, sem, k in self.items[eng_name]:
                        for w in ws:
                            if w.sem is own and w.val > pos:
                                raise RuntimeError(
                                    f"{name}/{eng_name}: wait on own future")
                            seen = P.waited[eng_name].get(w.sem.name, -1)
                            if w.val > seen:
                                eng.wait_ge(w.sem, w.val)
                                P.waited[eng_name][w.sem.name] = w.val
                        inst = fn(eng)
                        if sem is not None:
                            inst.then_inc(sem, k)
                            if sem is own:
                                pos += 1
                return go
            if self.items["gp"]:
                b.gpsimd(mk("gp"))
            if self.items["sp"]:
                b.sync(mk("sp"))
            if self.items["pe"]:
                b.tensor(mk("pe"))
            if self.items["dve"]:
                b.vector(mk("dve"))
            if self.items["act"]:
                b.scalar(mk("act"))


def build_aux(inputs):
    b1 = np.asarray(inputs["L0_ffn_b1"], np.float32)
    bqkv = np.asarray(inputs["L0_attn_b_bqkv"], np.float32)
    wr = np.asarray(inputs["L0_ffn_Wr"], np.float32)
    aux = {
        "aux_idf": np.eye(128, dtype=np.float32),
        "aux_idb": np.eye(128, dtype=bfloat16),
        "aux_ones": np.ones((1, 128), dtype=np.float32),
        "aux_mask": np.tril(np.ones((128, 128), dtype=np.float32)),
        "aux_b1c": np.ascontiguousarray(
            b1.reshape(NE, 32, 128).transpose(2, 0, 1).reshape(128, NE * 32)),
        "aux_bqc": np.ascontiguousarray(
            bqkv[:, 0:64].reshape(8, 2, 64).transpose(2, 1, 0).reshape(128, 8)),
        "aux_bkc": np.ascontiguousarray(
            bqkv[:, 64:128].reshape(8, 2, 64).transpose(2, 1, 0).reshape(128, 8)),
        "aux_bvr": np.ascontiguousarray(bqkv[:, 128:192].reshape(1, NH * 64)),
        "aux_wr": np.ascontiguousarray(
            wr.reshape(NDC, 128, NE).transpose(1, 0, 2).reshape(128, NDC * NE)),
        "aux_boutr": np.asarray(inputs["L0_attn_b_bout"], np.float32).reshape(1, D).copy(),
        "aux_brr": np.asarray(inputs["L0_ffn_br"], np.float32).reshape(1, NE).copy(),
        "aux_su128": np.triu(np.ones((128, 128), np.float32), 1),
        "aux_su16": np.triu(np.ones((16, 16), np.float32), 1),
        "aux_io16": np.broadcast_to(np.arange(16, dtype=np.float32), (128, 16)).copy(),
        "aux_io16r": np.broadcast_to(16.0 - np.arange(16, dtype=np.float32), (128, 16)).copy(),
        "aux_iota128": np.arange(128, dtype=np.float32).reshape(128, 1),
        "aux_iota16": np.arange(16, dtype=np.float32).reshape(16, 1),
        "aux_ones128c": np.ones((128, 1), np.float32),
        "aux_ones16c": np.ones((16, 1), np.float32),
        "aux_g512": np.broadcast_to(512.0 * np.arange(CAP_G, dtype=np.float32), (16, CAP_G)).copy(),
        "aux_su32": np.triu(np.ones((32, 32), np.float32), 1),
        "aux_thr16": np.broadcast_to(512.0 * np.arange(THRN, dtype=np.float32), (16, THRN)).copy(),
        "aux_c512": np.full((16, 1), 512.0, np.float32),
        "aux_c1024": np.full((128, 1), 1024.0, np.float32),
        "aux_c4096": np.full((128, 1), 4096.0, np.float32),
    }
    return aux


def build_program(n_seq=2, parts=("attn_a", "ffn1", "attn_b", "ffn2")):
    nc = bass.Bass("TRN2", target_bir_lowering=False)
    L = {}

    def din(key, name, shape, dtype=F32):
        L[key] = nc.dram_tensor(name, list(shape), dtype, kind="ExternalInput").ap()

    din("x_in", "x", [n_seq, S, D])
    L["y_out"] = nc.dram_tensor("y", [n_seq, S, D], F32, kind="ExternalOutput").ap()
    din("Wq", "L0_attn_a_Wq", [D, D])
    din("Wk", "L0_attn_a_Wk", [D, D])
    din("Wv", "L0_attn_a_Wv", [D, D])
    din("Wo", "L0_attn_a_Wo", [D, D])
    din("Wqkv", "L0_attn_b_Wqkv", [NH, D, 3 * HD])
    din("Wout", "L0_attn_b_Wout", [D, D])
    din("W1", "L0_ffn_W1", [NE, D, DH])
    din("W2", "L0_ffn_W2", [NE, DH, D])
    din("b2_in", "L0_ffn_b2", [NE, D])
    din("a_idf", "aux_idf", [128, 128])
    din("a_idb", "aux_idb", [128, 128], BF16)
    din("a_ones", "aux_ones", [1, 128])
    din("a_mask", "aux_mask", [128, 128])
    din("a_b1c", "aux_b1c", [128, NE * 32])
    din("a_bqc", "aux_bqc", [128, 8])
    din("a_bkc", "aux_bkc", [128, 8])
    din("a_bvr", "aux_bvr", [1, NH * 64])
    din("a_wr", "aux_wr", [128, NDC * NE])
    din("a_boutr", "aux_boutr", [1, D])
    din("a_brr", "aux_brr", [1, NE])
    din("b1_in", "L0_ffn_b1", [NE, DH])
    din("a_su128", "aux_su128", [128, 128])
    din("a_su16", "aux_su16", [16, 16])
    din("a_io16", "aux_io16", [128, 16])
    din("a_io16r", "aux_io16r", [128, 16])
    din("a_iota128", "aux_iota128", [128, 1])
    din("a_iota16", "aux_iota16", [16, 1])
    din("a_ones128c", "aux_ones128c", [128, 1])
    din("a_ones16c", "aux_ones16c", [16, 1])
    din("a_g512", "aux_g512", [16, CAP_G])
    din("a_su32", "aux_su32", [32, 32])
    din("a_thr16", "aux_thr16", [16, THRN])
    din("a_c512", "aux_c512", [16, 1])
    din("a_c1024", "aux_c1024", [128, 1])
    din("a_c4096", "aux_c4096", [128, 1])
    L["w1bf"] = nc.dram_tensor("w1bf", [NE * D, DH], BF16, kind="Internal").ap()
    L["w2bf"] = nc.dram_tensor("w2bf", [NE * D, DH], BF16, kind="Internal").ap()
    L["xg_d"] = nc.dram_tensor("xg_d", [CAP_S, D], BF16, kind="Internal").ap()
    L["y_d"] = nc.dram_tensor("y_d", [CAP_S, D], BF16, kind="Internal").ap()
    L["meta_d"] = nc.dram_tensor("meta_d", [32, 16], F32, kind="Internal").ap()
    if DEBUG:
        for nm, shape, dt in (("dbg_sl", [128, 2 * NT2], I32), ("dbg_g12", [128, 2 * NT2], F32),
                              ("dbg_cb", [32, 16], F32), ("dbg_eog", [1, CAP_G], F32),
                              ("dbg_mc", [16, 16], F32), ("dbg_mc2", [16, 16], F32),
                              ("dbg_b1", [128, 32 * CAP_G], F32),
                              ("dbg_xg", [CAP_S, D], BF16), ("dbg_y", [CAP_S, D], BF16)):
            L[nm] = nc.dram_tensor(nm, shape, dt, kind="ExternalOutput").ap()

    es = ExitStack()
    sb = lambda name, shape, dt: es.enter_context(nc.sbuf_tensor(name, shape, dt))
    L["resid0"] = sb("resid0", [128, NT * D], BF16)
    L["resid1"] = sb("resid1", [128, NT * D], BF16)
    L["gate0"] = sb("gate0", [128, NT * NE], F32)
    L["gate1"] = sb("gate1", [128, NT * NE], F32)
    L["idf"] = sb("idf", [128, 128], F32)
    L["idb"] = sb("idb", [128, 128], BF16)
    L["ones32"] = sb("ones32", [1, 128], F32)
    L["mask01"] = sb("mask01", [128, 128], F32)
    L["b1c"] = sb("b1c", [128, NE * 32], F32)
    L["wr_sb"] = sb("wr_sb", [128, NDC * NE], F32)
    L["bqc"] = sb("bqc", [128, 8], F32)
    L["bkc"] = sb("bkc", [128, 8], F32)
    L["gT0"] = sb("gT0", [NE, S], F32)
    L["gT1"] = sb("gT1", [NE, S], F32)
    L["trash"] = sb("trash", [128, D], F32)
    L["su128"] = sb("su128", [128, 128], F32)
    L["su16"] = sb("su16", [16, 16], F32)
    L["io16"] = sb("io16", [128, 16], F32)
    L["io16r"] = sb("io16r", [128, 16], F32)
    L["iota128"] = sb("iota128", [128, 1], F32)
    L["iota16"] = sb("iota16", [16, 1], F32)
    L["ones128c"] = sb("ones128c", [128, 1], F32)
    L["ones16c"] = sb("ones16c", [16, 1], F32)
    L["g512"] = sb("g512", [16, CAP_G], F32)
    L["su32"] = sb("su32", [32, 32], F32)
    L["thr16"] = sb("thr16", [16, THRN], F32)
    L["c512"] = sb("c512", [16, 1], F32)
    L["c1024"] = sb("c1024", [128, 1], F32)
    L["c4096"] = sb("c4096", [128, 1], F32)
    L["bvr"] = sb("bvr", [1, NH * 64], F32)
    L["boutr"] = sb("boutr", [1, D], F32)
    L["brr"] = sb("brr", [1, NE], F32)
    L["stats"] = sb("stats", [128, 64], F32)
    L["u16"] = sb("u16", [128, 2 * 16], F32)
    L["u16b"] = sb("u16b", [128, 2 * 16], F32)

    L["ps_big"] = es.enter_context(nc.psum_tensor("ps_big", [128, 2048], F32))
    L["pb"] = [es.enter_context(nc.psum_tensor(f"pb{i}", [128, 512], F32)) for i in range(2)]
    L["pt"] = [es.enter_context(nc.psum_tensor(f"pt{i}", [128, 1024], BF16)) for i in range(2)]

    P = Prog(nc, es)
    with es:
        _emit_all(P, L, n_seq, parts)
    return nc


class _Feeder:

    def __init__(self, P, L, src_key, dst_key, nchunks=128):
        self.P, self.L = P, L
        self.src, self.dst = src_key, dst_key
        self.k = 0
        self.n = nchunks

    def feed(self, cnt, ph=None):
        if self.k >= self.n:
            return
        own = ph is None
        if own:
            ph = Phase(self.P)
        c1 = min(self.n, self.k + cnt)
        for k in range(self.k, c1):
            e, blk = k // 8, k % 8
            dst = bass.AP(self.L[self.dst].tensor, (e * 1024 + blk * 128) * 4096,
                          [[4096, 128], [1, 4096]])
            if self.src == "W1":
                s = bass.AP(self.L[self.src].tensor, e * 4194304 + blk * 512,
                            [[4096, 128], [524288, 8], [1, 512]])
                dst = dst.rearrange("p (dc j) -> p dc j", dc=8)
            else:
                s = bass.AP(self.L[self.src].tensor, (e * 4096 + blk * 512) * 1024,
                            [[1024, 128], [131072, 4], [1, 1024]])
                dst = dst.rearrange("p (hc j) -> p hc j", hc=4)
            ph.dma("gp", "dW", dst, s)
        self.k = c1
        if own:
            ph.flush(f"pc_{self.dst}_{c1}")

    def drain(self):
        self.feed(self.n - self.k)


def _set_seq(L, s):
    L["resid"] = L[f"resid{s}"]
    L["gate"] = L[f"gate{s}"]
    L["gT"] = L[f"gT{s}"]


def _emit_all(P, L, n_seq, parts):
    nc = P.nc
    ph = Phase(P)
    dx = "dX"
    for dst, src in (
        (L["idf"], "a_idf"), (L["idb"], "a_idb"), (L["ones32"], "a_ones"),
        (L["mask01"], "a_mask"), (L["b1c"], "a_b1c"), (L["wr_sb"], "a_wr"),
        (L["bqc"], "a_bqc"), (L["bkc"], "a_bkc"),
        (L["bvr"], "a_bvr"), (L["boutr"], "a_boutr"), (L["brr"], "a_brr"),
        (L["su128"], "a_su128"), (L["su16"], "a_su16"),
        (L["io16"], "a_io16"), (L["io16r"], "a_io16r"),
        (L["iota128"], "a_iota128"), (L["iota16"], "a_iota16"),
        (L["ones128c"], "a_ones128c"), (L["ones16c"], "a_ones16c"),
        (L["g512"], "a_g512"), (L["su32"], "a_su32"), (L["thr16"], "a_thr16"), (L["c512"], "a_c512"),
        (L["c1024"], "a_c1024"), (L["c4096"], "a_c4096"),
    ):
        ph.dma("sp", dx, dst[:, :], L[src])
    tc = L["stats"][:, 60:61]
    ph.emit("dve", lambda v: v.tensor_scalar_mul(tc, L["ones128c"][:, :], 1.0 / D))
    ph.emit("dve", lambda v: v.tensor_scalar_add(tc, L["ones128c"][:, :], 1e-5))
    ph.emit("dve", lambda v: v.tensor_scalar(tc, L["ones128c"][:, :], -1.0, 1.0,
                                             ALU.mult, ALU.add))
    ph.flush("consts")
    ev_consts = P.snap(dx)

    ph = Phase(P)
    for s in range(n_seq):
        rsrc = L["x_in"][s].rearrange("(tt p) d -> p tt d", p=128)
        rdst = L[f"resid{s}"][:, :].rearrange("p (tt d) -> p tt d", tt=NT)
        for h in range(2):
            ph.dma("gp", dx, rdst[:, h * 8:(h + 1) * 8, :], rsrc[:, h * 8:(h + 1) * 8, :])
    ph.flush("loadx")
    ev_x = P.snap(dx)

    fdr1 = _Feeder(P, L, "W1", "w1bf")
    fdr2 = _Feeder(P, L, "W2", "w2bf")
    dep = [ev_x, ev_consts]
    if "attn_a" in parts:
        for s in range(n_seq):
            _set_seq(L, s)
            stkA = ExitStack()
            L["xlt"] = stkA.enter_context(nc.sbuf_tensor(f"xlta_{s}", [128, NDC * S], BF16))
            _ph_ln(P, L, s, 0, router=False, dep=dep, xlt=L["xlt"])
            _ph_attn_a(P, L, s, feeder=fdr1)
            stkA.close()
            dep = []
        fdr1.drain()
    if "ffn1" in parts:
        _ph_ffn_sparse(P, L, 1, n_seq, dep=dep)
        dep = []
    if "attn_b" in parts:
        for s in range(n_seq):
            _set_seq(L, s)
            stkB = ExitStack()
            L["xlt"] = stkB.enter_context(nc.sbuf_tensor(f"xltb_{s}", [128, NDC * S], BF16))
            _ph_ln(P, L, s, 2, router=False,
                   dep=dep + [L["ffn_done"][s], P.snap("c_act")], xlt=L["xlt"])
            _ph_attn_b(P, L, s, feeder=fdr2)
            stkB.close()
            dep = []
        fdr2.drain()
    if "ffn2" in parts:
        _ph_ffn_sparse(P, L, 3, n_seq, dep=dep)
        dep = []

    ph = Phase(P)
    evout = [P.snap("c_dve"), P.snap("c_pe")]
    for s in range(n_seq):
        ysrc = L[f"resid{s}"][:, :].rearrange("p (tt d) -> p tt d", tt=NT)
        ydst = L["y_out"][s].rearrange("(tt p) d -> p tt d", p=128)
        for h in range(2):
            ph.dma("gp", dx, ydst[:, h * 8:(h + 1) * 8, :], ysrc[:, h * 8:(h + 1) * 8, :],
                   waits=evout + ([dep[0]] if dep else []))
    ph.flush("store")

    ph = Phase(P)
    evd = P.snap(dx)
    evw = P.snap("dW")
    ph.emit("sp", lambda e: None, waits=[evd, evw], sem=False)
    ph.flush("final")


def _ph_ln(P, L, s, which, router, dep=(), xlt=None, xtm=None, xtm_off=0):
    nc = P.nc
    stk = ExitStack()
    sb = lambda name, shape, dt: stk.enter_context(nc.sbuf_tensor(name, shape, dt))
    normscr = sb(f"ln{s}{which}_ns", [128, 4 * D], F32)
    trash = L["trash"]
    ln32 = sb(f"ln{s}{which}_l32", [128, 2 * 1024], F32) if router else None
    ph = Phase(P)
    resid, stats = L["resid"], L["stats"]
    idf, ps_big, pb = L["idf"][:, :], L["ps_big"], L["pb"]
    ev_stB = {}
    ev_trans = {}
    ev_xtm = {}
    prev_bank = {b: P.snap("c_dve") for b in range(4)}
    prev_ln32 = {}
    prev_gate = {}
    prev_pbe = {0: P.snap("c_dve"), 1: P.snap("c_dve")}
    ev_enm = {}

    def col(tt, i):
        p4 = tt % 4
        return stats[:, p4 * 8 + i: p4 * 8 + i + 1]

    def stage_a(tt):
        rtile = resid[:, tt * 1024:(tt + 1) * 1024]
        w0 = list(dep) + [ev_stB.get(tt - 4)]
        e1 = ph.emit("dve", lambda v, a=col(tt, 0), b=rtile: v.reduce_sum(a, b, AX.X), waits=w0)
        e2 = ph.emit("act", lambda a, o=trash[:, :], i=rtile, c=col(tt, 2):
                     a.activation(o, i, AF.Square, bias=0.0, scale=1.0, accum_out=c),
                     waits=w0)
        return e1, e2

    def stage_b(tt, e1, e2):
        rtile = resid[:, tt * 1024:(tt + 1) * 1024]
        ns = normscr[:, (tt % 4) * 1024:((tt % 4) + 1) * 1024]
        em = ph.emit("dve", lambda v, a=col(tt, 1), b=col(tt, 0): v.tensor_scalar_mul(a, b, 1.0 / D), waits=[e1])
        emm = ph.emit("dve", lambda v, a=col(tt, 4), b=col(tt, 1): v.tensor_mul(a, b, b), waits=[em])
        v1 = ph.emit("dve", lambda v, a=col(tt, 5), b=col(tt, 2), c=col(tt, 4):
                     v.tensor_scalar(a, b, 1.0 / D, c, ALU.mult, ALU.subtract), waits=[e2, emm])
        v2 = ph.emit("dve", lambda v, a=col(tt, 5), b=col(tt, 5): v.tensor_scalar_add(a, b, 1e-5), waits=[v1])
        eln = ph.emit("act", lambda a, o=col(tt, 6), i=col(tt, 5): a.activation(o, i, AF.Ln, bias=0.0, scale=1.0), waits=[v2])
        ers = ph.emit("act", lambda a, o=col(tt, 7), i=col(tt, 6): a.activation(o, i, AF.Exp, bias=0.0, scale=-0.5), waits=[eln])
        enm = ph.emit("dve", lambda v, o=ns, i=rtile, m=col(tt, 1), r=col(tt, 7):
                      v.tensor_scalar(o, i, m, r, ALU.subtract, ALU.mult),
                      waits=[ers, ev_trans.get(tt - 4), ev_xtm.get(tt - 4)])
        ev_enm[tt] = enm
        ev_stB[tt] = enm
        if xtm is not None:
            ev_xtm[tt] = ph.emit("act",
                                 lambda a, o=xtm[:, (xtm_off + tt) * D:(xtm_off + tt + 1) * D],
                                 i=ns:
                                 a.activation(o, i, AF.Copy, bias=0.0, scale=1.0),
                                 waits=[enm])

    def stage_c(tt):
        par2 = tt % 2
        ns = normscr[:, (tt % 4) * 1024:((tt % 4) + 1) * 1024]
        enm = ev_enm[tt]
        last_tr = None
        ev_last = []
        for half in range(2):
            bank = (2 * tt + half) % 4
            bslots = ps_big[:, bank * 512:(bank + 1) * 512]
            trs = []
            for j in range(4):
                dc = half * 4 + j
                etr = ph.emit("pe", lambda t, o=bslots[:, j * 128:(j + 1) * 128],
                              i=ns[:, dc * 128:(dc + 1) * 128], ii=idf:
                              t.transpose(o, i, ii),
                              waits=[enm, prev_bank.get(bank)] if j == 0 else ())
                trs.append(etr)
            evs = []
            eng = "act" if half == 0 else "dve"
            if xlt is not None:
                for j in range(4):
                    dc = half * 4 + j
                    dst = xlt[:, dc * S + tt * 128: dc * S + tt * 128 + 128]
                    if eng == "act":
                        ea = ph.emit("act", lambda a, o=dst, i=bslots[:, j * 128:(j + 1) * 128]:
                                     a.activation(o, i, AF.Copy, bias=0.0, scale=1.0), waits=[trs[-1]])
                    else:
                        ea = ph.emit("dve", lambda v, o=dst, i=bslots[:, j * 128:(j + 1) * 128]:
                                     v.tensor_copy(o, i), waits=[trs[-1]])
                    evs.append(ea)
            if router:
                for j in range(4):
                    dc = half * 4 + j
                    dst32 = ln32[:, par2 * 1024 + dc * 128: par2 * 1024 + dc * 128 + 128]
                    if eng == "act":
                        ev = ph.emit("act", lambda a, o=dst32, i=bslots[:, j * 128:(j + 1) * 128]:
                                     a.activation(o, i, AF.Copy, bias=0.0, scale=1.0),
                                     waits=[trs[-1], prev_ln32.get(par2)])
                    else:
                        ev = ph.emit("dve", lambda v, o=dst32, i=bslots[:, j * 128:(j + 1) * 128]:
                                     v.tensor_copy(o, i),
                                     waits=[trs[-1], prev_ln32.get(par2)])
                    evs.append(ev)
            prev_bank[bank] = evs[-1]
            ev_last.extend(evs)
            last_tr = trs[-1]
        ev_trans[tt] = last_tr
        if router:
            pbank = pb[par2]
            wmm = ev_last + [prev_pbe.get(par2)]
            for dc in range(NDC):
                ph.emit("pe", lambda t, o=pbank[:, 0:16],
                        l=ln32[:, par2 * 1024 + dc * 128: par2 * 1024 + dc * 128 + 128],
                        r=L["wr_sb"][:, dc * NE:(dc + 1) * NE], s_=(dc == 0):
                        t.matmul(o, l, r, start=s_, stop=False),
                        waits=wmm if dc == 0 else (), sem=False)
            mm_ev = ph.emit("pe", lambda t, o=pbank[:, 0:16], l=L["ones32"][:1, :],
                            r=L["brr"][:1, :]: t.matmul(o, l, r, start=False, stop=True))
            prev_ln32[par2] = mm_ev
            uA = L["u16"][:, par2 * 16:(par2 + 1) * 16]
            uB = L["u16b"][:, par2 * 16:(par2 + 1) * 16]
            c = lambda i: stats[:, 32 + par2 * 8 + i: 32 + par2 * 8 + i + 1]
            e0 = ph.emit("dve", lambda v, o=c(0), i=pbank[:, 0:16]:
                         v.reduce_max(o, i, AX.X, negate=True),
                         waits=[mm_ev, prev_gate.get(par2)])
            e1b = ph.emit("act", lambda a, o=uA, i=pbank[:, 0:16], b=c(0):
                          a.activation(o, i, AF.Exp, bias=b, scale=1.0), waits=[e0])
            prev_pbe[par2] = e1b
            e2 = ph.emit("dve", lambda v, o=c(1), i=uA: v.reduce_sum(o, i, AX.X), waits=[e1b])
            e3 = ph.emit("dve", lambda v, o=c(2), i=c(1): v.reciprocal(o, i), waits=[e2])
            e4 = ph.emit("dve", lambda v, o=c(3), i=uA: v.reduce_max(o, i, AX.X), waits=[e1b])
            e5 = ph.emit("dve", lambda v, o=uB, i=uA, m=c(3): v.tensor_scalar(o, i, m, None, ALU.is_equal), waits=[e4])
            e6 = ph.emit("dve", lambda v, o=uB, i=uB: v.tensor_scalar(o, i, -1.0, 1.0, ALU.mult, ALU.add), waits=[e5])
            e7 = ph.emit("dve", lambda v, o=uB, i=uA, j=uB: v.tensor_mul(o, i, j), waits=[e6])
            e8 = ph.emit("dve", lambda v, o=c(4), i=uB: v.reduce_max(o, i, AX.X), waits=[e7])
            e9 = ph.emit("dve", lambda v, o=uB, i=uA, m=c(4): v.tensor_scalar(o, i, m, None, ALU.is_ge), waits=[e8])
            e10 = ph.emit("dve", lambda v, o=uB, i=uA, j=uB: v.tensor_mul(o, i, j), waits=[e9])
            gdst = L["gate"][:, tt * NE:(tt + 1) * NE]
            e11 = ph.emit("dve", lambda v, o=gdst, i=uB, r=c(2): v.tensor_scalar(o, i, r, None, ALU.mult), waits=[e10, e3])
            prev_gate[par2] = e11
            egt = ph.emit("pe", lambda t, o=pbank[0:16, 128:256], i=gdst, ii=idf:
                          t.transpose(o, i, ii), waits=[e11, prev_pbe[par2]])
            egte = ph.emit("dve", lambda v, o=L["gT"][0:16, tt * 128:(tt + 1) * 128],
                           i=pbank[0:16, 128:256]: v.tensor_copy(o, i), waits=[egt])
            prev_pbe[par2] = egte

    ab = {}
    for tt in range(NT):
        ab[tt] = stage_a(tt)
        if tt >= 1:
            stage_b(tt - 1, *ab.pop(tt - 1))
        if tt >= 2:
            stage_c(tt - 2)
    stage_b(NT - 1, *ab.pop(NT - 1))
    stage_c(NT - 2)
    stage_c(NT - 1)
    ph.flush(f"s{s}_ln{which}")
    stk.close()


def _ph_attn_a(P, L, s, feeder=None):
    nc = P.nc
    NG = 4
    HPG = NH // NG
    IW = HPG * HD
    es2 = ExitStack()
    sb = lambda name, shape, dt: es2.enter_context(nc.sbuf_tensor(name, shape, dt))
    wstage = [sb(f"aa_ws{i}_{s}", [128, 2048], BF16) for i in range(2)]
    wqT = sb(f"aa_wq_{s}", [128, NDC * IW], BF16)
    wkT = sb(f"aa_wk_{s}", [128, NDC * IW], BF16)
    wvT = sb(f"aa_wv_{s}", [128, NDC * IW], BF16)
    woT = sb(f"aa_wo_{s}", [128, 2 * 1024], BF16)
    qaT = sb(f"aa_q_{s}", [128, 2 * S], BF16)
    kaT = sb(f"aa_k_{s}", [128, 2 * S], BF16)
    vga = sb(f"aa_v_{s}", [128, NT * IW], BF16)
    outg = sb(f"aa_o_{s}", [128, 2 * S], BF16)
    ascr = sb(f"aa_s_{s}", [128, 4 * 192], F32)
    abf = sb(f"aa_ab_{s}", [128, 4 * 192], BF16)
    atT = sb(f"aa_at_{s}", [128, 2 * 256], BF16)
    xlt, resid, idb = L["xlt"], L["resid"], L["idb"][:, :]
    ps_big, pb, pt = L["ps_big"], L["pb"], L["pt"]
    Wsrcs = (L["Wq"], L["Wk"], L["Wv"])
    with es2:
        pre = {}
        for g in range(NG):
            ph = Phase(P)
            dA = "dA"

            def tr_batches(items, src_of, dst_of, prev_pt, wait_extra):
                done = []
                for b0 in range(0, len(items), 8):
                    chunk = items[b0:b0 + 8]
                    bank = pt[(b0 // 8) % 2]
                    trs = []
                    for j, itx in enumerate(chunk):
                        etr = ph.emit("pe", lambda t, o=bank[:, j * 128:(j + 1) * 128],
                                      i=src_of(itx), ii=idb: t.transpose(o, i, ii),
                                      waits=([wait_extra, prev_pt.get((b0 // 8) % 2)]
                                             if j == 0 else ()))
                        trs.append(etr)
                    for j, itx in enumerate(chunk):
                        ea = ph.emit("act", lambda a, o=dst_of(itx), i=bank[:, j * 128:(j + 1) * 128]:
                                     a.activation(o, i, AF.Copy, bias=0.0, scale=1.0),
                                     waits=[trs[-1]])
                        done.append(ea)
                    prev_pt[(b0 // 8) % 2] = done[-1]
                return done

            prev_pt = {0: P.snap("c_dve"), 1: P.snap("c_dve")}
            evq = pre.pop((g, 0), None) or ph.dma(
                "gp", dA, wstage[0][:, :].rearrange("p (it d) -> p it d", it=2),
                bass.AP(Wsrcs[0].tensor, (g * IW) * D, [[D, 128], [128 * D, 2], [1, D]]))
            evk = pre.pop((g, 1), None) or ph.dma(
                "gp", dA, wstage[1][:, :].rearrange("p (it d) -> p it d", it=2),
                bass.AP(Wsrcs[1].tensor, (g * IW) * D, [[D, 128], [128 * D, 2], [1, D]]))
            items = [(it, dc) for it in range(2) for dc in range(NDC)]
            evs_q = tr_batches(
                items, lambda x: wstage[0][:, x[0] * 1024 + x[1] * 128: x[0] * 1024 + x[1] * 128 + 128],
                lambda x: wqT[:, x[1] * IW + x[0] * 128: x[1] * IW + x[0] * 128 + 128],
                prev_pt, evq)
            evv = ph.dma("gp", dA, wstage[0][:, :].rearrange("p (it d) -> p it d", it=2),
                         bass.AP(Wsrcs[2].tensor, (g * IW) * D, [[D, 128], [128 * D, 2], [1, D]]),
                         waits=[evs_q[-1]])
            evs_k = tr_batches(
                items, lambda x: wstage[1][:, x[0] * 1024 + x[1] * 128: x[0] * 1024 + x[1] * 128 + 128],
                lambda x: wkT[:, x[1] * IW + x[0] * 128: x[1] * IW + x[0] * 128 + 128],
                prev_pt, evk)
            evo = ph.dma("gp", dA, wstage[1][:, :].rearrange("p (oc i) -> p oc i", oc=NDC),
                         bass.AP(L["Wo"].tensor, g * IW, [[D, 128], [128 * D, 8], [1, IW]]),
                         waits=[evs_k[-1]])
            tr_batches(
                items, lambda x: wstage[0][:, x[0] * 1024 + x[1] * 128: x[0] * 1024 + x[1] * 128 + 128],
                lambda x: wvT[:, x[1] * IW + x[0] * 128: x[1] * IW + x[0] * 128 + 128],
                prev_pt, evv)
            items_o = [(oc, it) for oc in range(NDC) for it in range(2)]
            tr_batches(
                items_o, lambda x: wstage[1][:, x[0] * 256 + x[1] * 128: x[0] * 256 + x[1] * 128 + 128],
                lambda x: woT[:, x[1] * 1024 + x[0] * 128: x[1] * 1024 + x[0] * 128 + 128],
                prev_pt, evo)
            ph.flush(f"s{s}_aa_g{g}_w")

            ph = Phase(P)
            pvt = {0: P.snap("c_dve"), 1: P.snap("c_dve")}
            for wi, (wT, dstT) in enumerate(((wqT, qaT), (wkT, kaT))):
                for it in range(2):
                    for ng in range(4):
                        key = (it * 4 + ng) % 2
                        pbank = pb[key]
                        emm = None
                        for dc in range(NDC):
                            emm = ph.emit("pe", lambda t, o=pbank[:, :],
                                          l=wT[:, dc * IW + it * 128: dc * IW + it * 128 + 128],
                                          r=xlt[:, dc * S + ng * 512: dc * S + (ng + 1) * 512],
                                          s_=(dc == 0), p_=(dc == NDC - 1):
                                          t.matmul(o, l, r, start=s_, stop=p_),
                                          waits=[pvt.get(key)] if dc == 0 else (),
                                          sem=False if dc < NDC - 1 else None)
                        dst = dstT[:, it * S + ng * 512: it * S + (ng + 1) * 512]
                        eev = ph.emit("act", lambda a, o=dst, i=pbank[:, :]:
                                      a.activation(o, i, AF.Copy, bias=0.0, scale=1.0), waits=[emm])
                        pvt[key] = eev
            for tt in range(NT):
                key = tt % 2
                pbank = pb[key]
                emm = None
                for dc in range(NDC):
                    emm = ph.emit("pe", lambda t, o=pbank[:, 0:IW],
                                  l=xlt[:, dc * S + tt * 128: dc * S + tt * 128 + 128],
                                  r=wvT[:, dc * IW:(dc + 1) * IW],
                                  s_=(dc == 0), p_=(dc == NDC - 1):
                                  t.matmul(o, l, r, start=s_, stop=p_),
                                  waits=[pvt.get(key)] if dc == 0 else (),
                                  sem=False if dc < NDC - 1 else None)
                dst = vga[:, tt * IW:(tt + 1) * IW]
                eev = ph.emit("act", lambda a, o=dst, i=pbank[:, 0:IW]:
                              a.activation(o, i, AF.Copy, bias=0.0, scale=1.0), waits=[emm])
                pvt[key] = eev
            ev_proj = P.snap("c_act")

            sc_free = {}
            prev_ab4 = {}
            prev_abf_rd = {}
            prev_at = {}
            prev_av = {0: pvt.get(0), 1: pvt.get(1)}
            stc = L["stats"]
            pending = []
            first = True

            def emit_tail(w, h, par4, e_ab, idx):
                par2 = idx % 2
                it = h // 2
                rb = 64 * (h % 2)
                absl = abf[:, par4 * 192:(par4 + 1) * 192]
                bank = pt[par2]
                e_t1 = ph.emit("pe", lambda t, o=bank[0:64, 0:128], i=absl[:, 0:64], ii=idb:
                               t.transpose(o, i, ii), waits=[e_ab, prev_at.get(par2)])
                e_t2 = ph.emit("pe", lambda t, o=bank[:, 128:256], i=absl[:, 64:192], ii=idb:
                               t.transpose(o, i, ii))
                prev_abf_rd[par4] = e_t2
                atsl = atT[:, par2 * 256:(par2 + 1) * 256]
                e_te = ph.emit("dve", lambda v, o=atsl, i=bank[:, 0:256]:
                               v.tensor_copy(o, i), waits=[e_t2])
                prev_at[par2] = e_te
                avp = pb[par2]
                ph.emit("pe", lambda t, o=avp[0:64, 0:128],
                        l=vga[0:64, h * 64: h * 64 + 64],
                        r=atsl[0:64, 0:128]:
                        t.matmul(o, l, r, start=True, stop=False),
                        waits=[e_te, prev_av.get(par2)], sem=False)
                e_a2 = ph.emit("pe", lambda t, o=avp[0:64, 0:128],
                               l=vga[:, w * IW + h * 64: w * IW + h * 64 + 64],
                               r=atsl[:, 128:256]:
                               t.matmul(o, l, r, start=False, stop=True))
                dst = outg[rb:rb + 64, it * S + w * 128: it * S + w * 128 + 128]
                e_oe = ph.emit("act", lambda a, o=dst, i=avp[0:64, 0:128]:
                               a.activation(o, i, AF.Copy, bias=0.0, scale=1.0), waits=[e_a2])
                prev_av[par2] = e_oe

            for idx, (w, h) in enumerate([(w, h) for w in range(NT) for h in range(HPG)]):
                par4 = idx % 4
                it = h // 2
                rb = 64 * (h % 2)
                qsl = qaT[rb:rb + 64, it * S + w * 128: it * S + w * 128 + 128]
                scp = ps_big[:, par4 * 512: par4 * 512 + 192]
                e_s1 = ph.emit("pe", lambda t, o=scp[:, 0:64], l=qsl,
                               r=kaT[rb:rb + 64, it * S: it * S + 64]:
                               t.matmul(o, l, r, start=True, stop=True),
                               waits=[sc_free.get(par4)] + ([ev_proj] if first else []))
                first = False
                e_s2 = ph.emit("pe", lambda t, o=scp[:, 64:192], l=qsl,
                               r=kaT[rb:rb + 64, it * S + w * 128: it * S + w * 128 + 128]:
                               t.matmul(o, l, r, start=True, stop=True))
                asl = ascr[:, par4 * 192:(par4 + 1) * 192]
                absl = abf[:, par4 * 192:(par4 + 1) * 192]
                c1 = stc[:, 48 + par4: 48 + par4 + 1]
                e_e = ph.emit("act", lambda a, o=asl, i=scp:
                              a.activation(o, i, AF.Exp, bias=0.0, scale=0.125),
                              waits=[e_s1, e_s2, prev_ab4.get(par4)])
                sc_free[par4] = e_e
                if len(pending) > 1:
                    emit_tail(*pending.pop(0))
                e_r = ph.emit("dve", lambda v, o=c1, i=asl: v.reduce_sum(o, i, AX.X), waits=[e_e])
                e_rc = ph.emit("dve", lambda v, o=c1, i=c1: v.reciprocal(o, i), waits=[e_r])
                e_ab = ph.emit("dve", lambda v, o=absl, i=asl, r=c1:
                               v.tensor_scalar(o, i, r, None, ALU.mult),
                               waits=[e_rc, prev_abf_rd.get(par4)])
                prev_ab4[par4] = e_ab
                pending.append((w, h, par4, e_ab, idx))
            while pending:
                emit_tail(*pending.pop(0))

            lastout = P.snap("c_act")
            pvt2 = dict(prev_av)
            for tt in range(NT):
                for oc in range(2):
                    key = (tt * 2 + oc) % 2
                    pbank = pb[key]
                    emm = None
                    for it in range(2):
                        emm = ph.emit("pe", lambda t, o=pbank[:, :],
                                      l=outg[:, it * S + tt * 128: it * S + tt * 128 + 128],
                                      r=woT[:, it * 1024 + oc * 512: it * 1024 + (oc + 1) * 512],
                                      s_=(it == 0), p_=(it == 1):
                                      t.matmul(o, l, r, start=s_, stop=p_),
                                      waits=[lastout, pvt2.get(key)] if it == 0 else (),
                                      sem=False if it < 1 else None)
                    dst = resid[:, tt * 1024 + oc * 512: tt * 1024 + (oc + 1) * 512]
                    eev = ph.emit("dve", lambda v, o=dst, i=dst, j=pbank[:, :]:
                                  v.tensor_add(o, i, j), waits=[emm])
                    pvt2[key] = eev
            if g + 1 < NG:
                pre[(g + 1, 0)] = ph.dma(
                    "gp", dA, wstage[0][:, :].rearrange("p (it d) -> p it d", it=2),
                    bass.AP(Wsrcs[0].tensor, ((g + 1) * IW) * D, [[D, 128], [128 * D, 2], [1, D]]))
                pre[(g + 1, 1)] = ph.dma(
                    "gp", dA, wstage[1][:, :].rearrange("p (it d) -> p it d", it=2),
                    bass.AP(Wsrcs[1].tensor, ((g + 1) * IW) * D, [[D, 128], [128 * D, 2], [1, D]]))
            ph.flush(f"s{s}_aa_g{g}_at")
            if feeder is not None:
                feeder.feed(16)


def _ph_ffn(P, L, s, which):
    nc = P.nc
    NBLK = 4
    BH = DH // NBLK
    NHC = BH // 128
    NTG = 4
    es2 = ExitStack()
    sb = lambda name, shape, dt: es2.enter_context(nc.sbuf_tensor(name, shape, dt))
    w1b = [sb(f"f{which}w1{i}_{s}", [128, NDC * BH], BF16) for i in range(2)]
    w2b = [sb(f"f{which}w2{i}_{s}", [128, NHC * D], BF16) for i in range(2)]
    hT = sb(f"f{which}h_{s}", [128, NHC * 512], BF16)
    xlt, resid, gate = L["xlt"], L["resid"], L["gate"]
    ps_big, pb = L["ps_big"], L["pb"]
    with es2:
        ph = Phase(P)
        dW = "dW"
        pv_hbank = {}
        pv_ybank = {}
        w_ld = {}

        def load_w(e, blk, par, waits):
            ph.dma("gp", dW, w1b[par][:, :].rearrange("p (dc hh) -> p dc hh", dc=NDC),
                   bass.AP(L["W1"].tensor, e * D * DH + blk * BH,
                           [[DH, 128], [128 * DH, NDC], [1, BH]]), waits=waits)
            return ph.dma("gp", dW, w2b[par][:, :].rearrange("p (hc d) -> p hc d", hc=NHC),
                          bass.AP(L["W2"].tensor, e * DH * D + blk * BH * D,
                                  [[D, 128], [128 * D, NHC], [1, D]]))

        seqi = [(e, blk) for e in range(NE) for blk in range(NBLK)]
        w_ld[0] = load_w(*seqi[0], 0, [])
        w_ld[1] = load_w(*seqi[1], 1, [])
        nload = 2
        for ib, (e, blk) in enumerate(seqi):
            par = ib % 2
            lastB = None
            for g in range(NTG):
                relu_evs = []
                for hc in range(NHC):
                    bank = hc % 4
                    psl = ps_big[:, bank * 512:(bank + 1) * 512]
                    emm = None
                    for dc in range(NDC):
                        w0 = [w_ld[par], pv_hbank.get(bank)] if dc == 0 else ()
                        emm = ph.emit("pe", lambda t, o=psl,
                                      l=w1b[par][:, dc * BH + hc * 128: dc * BH + hc * 128 + 128],
                                      r=xlt[:, dc * S + g * 512: dc * S + (g + 1) * 512],
                                      s_=(dc == 0), p_=(dc == NDC - 1):
                                      t.matmul(o, l, r, start=s_, stop=p_),
                                      waits=w0, sem=False if dc < NDC - 1 else None)
                    b1col = L["b1c"][:, e * 32 + blk * NHC + hc: e * 32 + blk * NHC + hc + 1]
                    erl = ph.emit("act", lambda a, o=hT[:, hc * 512:(hc + 1) * 512], i=psl, b=b1col:
                                  a.activation(o, i, AF.Relu, bias=b, scale=1.0), waits=[emm])
                    pv_hbank[bank] = erl
                    relu_evs.append(erl)
                for tt4 in range(4):
                    tt = g * 4 + tt4
                    for dc2 in range(2):
                        ypar = (tt4 * 2 + dc2) % 2
                        pbank = pb[ypar]
                        emm = None
                        for hc in range(NHC):
                            last = (hc == NHC - 1)
                            w0 = [relu_evs[hc], pv_ybank.get(ypar) if hc == 0 else None]
                            emm = ph.emit("pe", lambda t, o=pbank[:, :],
                                          l=hT[:, hc * 512 + tt4 * 128: hc * 512 + tt4 * 128 + 128],
                                          r=w2b[par][:, hc * D + dc2 * 512: hc * D + (dc2 + 1) * 512],
                                          s_=(hc == 0), p_=last:
                                          t.matmul(o, l, r, start=s_, stop=p_),
                                          waits=w0, sem=False if not last else None)
                        gcol = gate[:, tt * NE + e: tt * NE + e + 1]
                        dst = resid[:, tt * 1024 + dc2 * 512: tt * 1024 + (dc2 + 1) * 512]
                        est = ph.emit("dve", lambda v, o=dst, i=pbank[:, :], sc=gcol, j=dst:
                                      v.scalar_tensor_tensor(o, i, sc, j, ALU.mult, ALU.add),
                                      waits=[emm, pv_ybank.get(ypar)])
                        pv_ybank[ypar] = est
                        lastB = emm
            if nload < len(seqi):
                w_ld[par] = load_w(*seqi[nload], par, [lastB])
                nload += 1
        for tt in range(NT):
            for dc2 in range(2):
                ypar = (tt * 2 + dc2) % 2
                pbank = pb[ypar]
                emm = ph.emit("pe", lambda t, o=pbank[:, :],
                              l=L["gT"][0:16, tt * 128:(tt + 1) * 128],
                              r=L["b2nat"][0:16, dc2 * 512:(dc2 + 1) * 512]:
                              t.matmul(o, l, r, start=True, stop=True),
                              waits=[pv_ybank.get(ypar)])
                dst = resid[:, tt * 1024 + dc2 * 512: tt * 1024 + (dc2 + 1) * 512]
                est = ph.emit("dve", lambda v, o=dst, i=dst, j=pbank[:, :]:
                              v.tensor_add(o, i, j), waits=[emm, pv_ybank.get(ypar)])
                pv_ybank[ypar] = est
        ph.flush(f"s{s}_ffn{which}")


def _ph_ffn_sparse(P, L, which, n_seq, dep=()):
    nc = P.nc
    ntile = n_seq * NT
    stk = ExitStack()
    sb = lambda name, shape, dt: stk.enter_context(nc.sbuf_tensor(name, shape, dt))
    pref = f"fs{which}"
    sloti = sb(pref + "_sl", [128, 2 * ntile], I32)
    g12 = sb(pref + "_g12", [128, 2 * ntile], F32)
    b1all = sb(pref + "_b1a", [128, 32 * CAP_G], F32)
    w1i = sb(pref + "_w1i", [128, CAP_G], I32)
    w2i = sb(pref + "_w2i", [128, CAP_G], I32)
    stk2 = ExitStack()
    xtm = stk2.enter_context(nc.sbuf_tensor(pref + "_xtm", [128, ntile * D], BF16))
    for s in range(n_seq):
        _set_seq(L, s)
        _ph_ln(P, L, s, which, router=True, dep=(dep if s == 0 else ()),
               xtm=xtm, xtm_off=s * NT)
    _ph_meta(P, L, which, ntile, xtm, sloti, g12, b1all, w1i, w2i)
    stk2.close()
    w1src = bass.AP(L["w1bf"].tensor, 0, [[4096, NE * D], [1, 4096]])
    if which == 1:
        w2src = bass.AP(L["W2"].tensor, 0, [[D, NE * DH], [1, D]])
        w2_shuf = False
    else:
        w2src = bass.AP(L["w2bf"].tensor, 0, [[4096, NE * D], [1, 4096]])
        w2_shuf = True
    _ph_groups(P, L, which, b1all, w1i, w2i, w1src, w2src, w2_shuf)
    _ph_consume(P, L, which, ntile, sloti, g12)
    stk.close()


def _ph_meta(P, L, which, ntile, xtm, sloti, g12, b1all, w1i, w2i):
    nc = P.nc
    stk = ExitStack()
    sb = lambda name, shape, dt: stk.enter_context(nc.sbuf_tensor(name, shape, dt))
    pref = f"fm{which}"
    ohb = sb(pref + "_oh", [128, ntile * 16], F32)
    mcols = sb(pref + "_mc", [16, 16], F32)
    Mge = sb(pref + "_mge", [16, CAP_G], F32)
    ebc16 = sb(pref + "_eb16", [16, CAP_G], F32)
    eoh = sb(pref + "_eoh", [16, CAP_G], F32)
    ebc128 = sb(pref + "_eb128", [128, CAP_G], F32)
    wif = sb(pref + "_wif", [128, 2 * CAP_G], F32)
    eogr = sb(pref + "_eog", [1, CAP_G], F32)
    tcsb = sb(pref + "_tc", [16, 32], F32)
    tctsb = sb(pref + "_tct", [32, 16], F32)
    cbsb = sb(pref + "_cb", [32, 16], F32)
    brow = sb(pref + "_br", [1, 16], F32)
    cbrow = sb(pref + "_cbr", [1, 512], F32)
    Ssb = sb(pref + "_S", [128, 2 * 16], F32)
    mscr = sb(pref + "_ms", [128, 40], F32)
    b1nat = sb(pref + "_b1n", [NE, DH], F32)
    ph = Phase(P)
    ev_b1n = ph.dma("sp", "dX", b1nat[:, :], L["b1_in"])
    pb0, pb1, ps_big = L["pb"][0], L["pb"][1], L["ps_big"]
    io16, io16r = L["io16"], L["io16r"]
    onesr = L["ones32"]
    d_pe, d_dve, d_act = P.snap("c_pe"), P.snap("c_dve"), P.snap("c_act")
    d_dw = P.snap("dW")

    _last = [None]

    def dv(fn, waits=()):
        e = ph.emit("dve", fn, waits=list(waits) + [_last[0]])
        _last[0] = e
        return e

    for T in range(ntile):
        gate_s = L[f"gate{T // NT}"]
        tt = T % NT
        dv(lambda v, T=T, g=gate_s, tt=tt: v.tensor_scalar(
            ohb[:, T * 16:(T + 1) * 16], g[:, tt * NE:(tt + 1) * NE], 0.0, None, ALU.is_gt),
           waits=[d_act, d_pe] if T == 0 else ())
    ev_oh = _last[0]
    ev_tc = None
    for T in range(ntile):
        ev_tc = ph.emit("pe", lambda t, T=T: t.matmul(
            pb0[0:16, 64 + T:65 + T], ohb[:, T * 16:(T + 1) * 16], L["ones128c"][:, :],
            start=True, stop=True), waits=[ev_oh, d_act, d_dve] if T == 0 else ())
    ev_tce = ph.emit("act", lambda a: a.activation(tcsb[:, 0:ntile], pb0[0:16, 64:64 + ntile],
                                                   AF.Copy, bias=0.0, scale=1.0), waits=[ev_tc])
    dv(lambda v: v.reduce_sum(mcols[:, 0:1], tcsb[:, 0:ntile], AX.X), waits=[ev_tce])
    dv(lambda v: v.tensor_scalar(
        mcols[:, 2:2 + THRN], L["thr16"][:, :], mcols[:, 0:1], None, ALU.is_lt))
    dv(lambda v: v.reduce_sum(mcols[:, 10:11], mcols[:, 2:2 + THRN], AX.X))
    ev_pc = dv(lambda v: v.tensor_scalar(
        mcols[:, 11:12], mcols[:, 10:11], L["c512"][:, :], None, ALU.mult))
    ev_bm = ph.emit("pe", lambda t: t.matmul(pb0[0:16, 0:1], L["su16"][:, :], mcols[:, 11:12],
                                             start=True, stop=True),
                    waits=[ev_pc, ev_tce])
    ev_bc = ph.emit("act", lambda a: a.activation(mcols[:, 12:13], pb0[0:16, 0:1], AF.Copy,
                                                  bias=0.0, scale=1.0), waits=[ev_bm])
    ev_brm = ph.emit("pe", lambda t: t.transpose(pb0[0:1, 16:32], mcols[:, 12:13],
                                                 L["idf"][0:16, 0:16]),
                     waits=[ev_bc])
    ev_br = dv(lambda v: v.tensor_copy(brow[0:1, 0:16], pb0[0:1, 16:32]), waits=[ev_brm])
    ev_mg = dv(lambda v: v.tensor_scalar(Mge[:, :], L["g512"][:, :], mcols[0:16, 12:13],
                                         None, ALU.is_ge), waits=[ev_bc])
    ev_em = ph.emit("pe", lambda t: t.matmul(pb1[0:1, 0:CAP_G], L["ones16c"][:, :], Mge[:, :],
                                             start=True, stop=True), waits=[ev_mg, d_act, d_dve])
    ev_eog = ph.emit("act", lambda a: a.activation(eogr[0:1, :], pb1[0:1, 0:CAP_G], AF.Copy,
                                                   bias=-1.0, scale=1.0), waits=[ev_em])
    ev_b16m = ph.emit("pe", lambda t: t.matmul(pb0[0:16, 32:32 + CAP_G], onesr[0:1, 0:16],
                                               eogr[0:1, :], start=True, stop=True),
                      waits=[ev_eog])
    ev_b16 = dv(lambda v: v.tensor_copy(ebc16[:, :], pb0[0:16, 32:32 + CAP_G]),
                waits=[ev_b16m])
    ev_eoh = dv(lambda v: v.tensor_scalar(eoh[:, :], ebc16[:, :], L["iota16"][:, :],
                                          None, ALU.is_equal))
    ev_b128m = ph.emit("pe", lambda t: t.matmul(pb1[:, 32:32 + CAP_G], onesr[0:1, :],
                                                eogr[0:1, :], start=True, stop=True),
                       waits=[ev_eog])
    ev_b128 = dv(lambda v: v.tensor_copy(ebc128[:, :], pb1[:, 32:32 + CAP_G]),
                 waits=[ev_b128m])
    dv(lambda v: v.tensor_scalar(wif[:, 0:CAP_G], ebc128[:, :], L["c1024"][:, :],
                                 L["iota128"][:, :], ALU.mult, ALU.add))
    dv(lambda v: v.tensor_scalar(wif[:, CAP_G:2 * CAP_G], ebc128[:, :], L["c4096"][:, :],
                                 L["iota128"][:, :], ALU.mult, ALU.add))
    dv(lambda v: v.tensor_copy(w1i[:, :], wif[:, 0:CAP_G]))
    ev_wi = dv(lambda v: v.tensor_copy(w2i[:, :], wif[:, CAP_G:2 * CAP_G]))
    ev_b1mm = None
    for hc in range(32):
        ev_b1mm = ph.emit("pe", lambda t, hc=hc: t.matmul(
            ps_big[:, hc * CAP_G:(hc + 1) * CAP_G], b1nat[0:16, hc * 128:(hc + 1) * 128],
            eoh[:, :], start=True, stop=True),
            waits=[ev_eoh, ev_b1n, d_act, d_dve] if hc == 0 else ())
    ev_b1e = ph.emit("act", lambda a: a.activation(
        b1all[:, :], ps_big[:, 0:32 * CAP_G], AF.Copy, bias=0.0, scale=1.0),
        waits=[ev_b1mm])
    ev_tct = ph.emit("pe", lambda t: t.transpose(pb1[0:ntile, 64:80], tcsb[0:16, 0:ntile],
                                                 L["idf"][0:16, 0:16]),
                     waits=[ev_tce, ev_b128])
    ev_tcte = dv(lambda v, n=ntile: v.tensor_copy(tctsb[0:n, :], pb1[0:n, 64:80]), waits=[ev_tct])
    ph.emit("pe", lambda t, n=ntile: t.matmul(pb1[0:n, 80:96], L["su32"][0:n, 0:n], tctsb[0:n, :],
                                              start=True, stop=False), waits=[ev_tcte], sem=False)
    ev_cum = ph.emit("pe", lambda t, n=ntile: t.matmul(pb1[0:n, 80:96], onesr[0:1, 0:n],
                                                       brow[0:1, 0:16], start=False, stop=True),
                     waits=[ev_br])
    ev_cbe = ph.emit("act", lambda a, n=ntile: a.activation(cbsb[0:n, :], pb1[0:n, 80:96], AF.Copy,
                                                            bias=0.0, scale=1.0), waits=[ev_cum])
    ev_d1 = ph.dma("sp", "dX", L["meta_d"], cbsb[:, :], waits=[ev_cbe])
    ev_d2 = ph.dma("sp", "dX", cbrow[0:1, :],
                   bass.AP(L["meta_d"].tensor, 0, [[512, 1], [1, 512]]), waits=[ev_d1])
    ev_Sread = {}
    ev_Se = {}
    first_sc = [d_dw]
    A, B = mscr[:, 0:16], mscr[:, 16:32]
    ehi, tmp = mscr[:, 32:33], mscr[:, 33:34]
    sc1, sc2 = mscr[:, 35:36], mscr[:, 36:37]
    for T in range(ntile):
        sl = T % 2
        gate_s = L[f"gate{T // NT}"]
        tt = T % NT
        ph.emit("pe", lambda t, T=T, sl=sl: t.matmul(
            pb1[:, 96 + sl * 16:112 + sl * 16], L["su128"][:, :], ohb[:, T * 16:(T + 1) * 16],
            start=True, stop=False),
            waits=[ev_Sread.get(T - 2), ev_Se.get(T - 1),
                   ev_d2 if T == 0 else None], sem=False)
        ev_Sm = ph.emit("pe", lambda t, T=T, sl=sl: t.matmul(
            pb1[:, 96 + sl * 16:112 + sl * 16], onesr[0:1, :], cbrow[0:1, T * 16:T * 16 + 16],
            start=False, stop=True))
        ev_Se[T] = ph.emit("act", lambda a, sl=sl: a.activation(
            Ssb[:, sl * 16:(sl + 1) * 16], pb1[:, 96 + sl * 16:112 + sl * 16], AF.Copy,
            bias=0.0, scale=1.0), waits=[ev_Sm])
        oh_t = ohb[:, T * 16:(T + 1) * 16]
        G_t = gate_s[:, tt * NE:(tt + 1) * NE]
        S_sl = Ssb[:, sl * 16:(sl + 1) * 16]
        dv(lambda v, o=oh_t: v.tensor_mul(A, o, io16[:, :]))
        dv(lambda v: v.reduce_max(ehi, A, AX.X))
        dv(lambda v, o=oh_t: v.tensor_mul(A, o, io16r[:, :]))
        dv(lambda v: v.reduce_max(tmp, A, AX.X))
        dv(lambda v: v.tensor_scalar(A, io16r[:, :], tmp, None, ALU.is_equal))
        dv(lambda v, g=G_t: v.tensor_mul(B, g, A))
        dv(lambda v, T=T: v.reduce_sum(g12[:, 2 * T:2 * T + 1], B, AX.X))
        dv(lambda v, ss=S_sl: v.tensor_mul(B, ss, A), waits=[ev_Se[T]])
        dv(lambda v: v.reduce_sum(sc1, B, AX.X))
        ev_i1 = dv(lambda v, T=T: v.tensor_copy(sloti[:, 2 * T:2 * T + 1], sc1))
        dv(lambda v: v.tensor_scalar(A, io16[:, :], ehi, None, ALU.is_equal))
        dv(lambda v, g=G_t: v.tensor_mul(B, g, A))
        dv(lambda v, T=T: v.reduce_sum(g12[:, 2 * T + 1:2 * T + 2], B, AX.X))
        dv(lambda v, ss=S_sl: v.tensor_mul(B, ss, A))
        dv(lambda v: v.reduce_sum(sc2, B, AX.X))
        ev_i2 = dv(lambda v, T=T: v.tensor_copy(sloti[:, 2 * T + 1:2 * T + 2], sc2))
        ev_Sread[T] = ev_i2
        ph.idma_scatter("dW", L["xg_d"], sloti[:, 2 * T:2 * T + 1],
                        xtm[:, T * D:(T + 1) * D],
                        waits=first_sc + [ev_i1, d_act])
        first_sc = []
        ph.idma_scatter("dW", L["xg_d"], sloti[:, 2 * T + 1:2 * T + 2],
                        xtm[:, T * D:(T + 1) * D], waits=[ev_i2])
    if DEBUG and which == 1:
        evl = P.snap("c_dve")
        ph.dma("sp", "dX", L["dbg_sl"], sloti[:, :], waits=[evl])
        ph.dma("sp", "dX", L["dbg_g12"], g12[:, :], waits=[evl])
        ph.dma("sp", "dX", L["dbg_cb"], cbsb[:, :], waits=[ev_cbe])
        ph.dma("sp", "dX", L["dbg_eog"], eogr[0:1, :], waits=[ev_eog])
        ph.dma("sp", "dX", L["dbg_mc"], mcols[:, :], waits=[evl])
        ph.dma("sp", "dX", L["dbg_b1"], b1all[:, :], waits=[ev_b1e])
        evs = P.snap("dW")
        ph.dma("gp", "dW", L["dbg_xg"].rearrange("(g p) d -> p g d", p=128),
               bass.AP(L["xg_d"].tensor, 0, [[D, 128], [128 * D, CAP_S // 128], [1, D]]),
               waits=[evs])
    ph.flush(f"meta{which}")
    stk.close()


def _ph_groups(P, L, which, b1all, w1i, w2i, w1src, w2src, w2_shuf):
    nc = P.nc
    stk = ExitStack()
    sb = lambda name, shape, dt: stk.enter_context(nc.sbuf_tensor(name, shape, dt))
    pref = f"fg{which}"
    w1g = [sb(pref + f"_w1{i}", [128, NDC * BLK], BF16) for i in range(2)]
    w2g = [sb(pref + f"_w2{i}", [128, HCL * D], BF16) for i in range(2)]
    xg_tm = [sb(pref + "_xt0", [128, 4 * D], BF16)] * 2
    xgT = sb(pref + "_xT", [128, NDC * 512], BF16)
    hT = sb(pref + "_hT", [128, HCL * 512], BF16)
    y_sb = [sb(pref + "_y0", [128, 4 * D], F32)] * 2
    ph = Phase(P)
    ps_big, pb, pt = L["ps_big"], L["pb"], L["pt"]
    idb = L["idb"][:, :]
    d_pe, d_dve, d_act = P.snap("c_pe"), P.snap("c_dve"), P.snap("c_act")
    ev_scat = P.snap("dW")
    ev_dx = P.snap("dX")
    ev_xload = {}
    ev_w = {}
    ev_wcons = {}
    ev_yd = {}
    ev_tr = {}
    pv_h = {0: d_act, 1: d_act}
    pv_y = {q: d_dve for q in range(4)}
    pv_pt = {0: d_dve, 1: d_dve}

    def gather(gi):
        g, blk = gi // NBLK2, gi % NBLK2
        par = gi % 2
        w0 = [ev_scat, ev_dx, ev_wcons.get(gi - 2)]
        ph.idma_gather("dW", w1g[par][:, :], w1src, w1i[:, g:g + 1],
                       eo=blk * 524288, waits=w0)
        if w2_shuf:
            ev = ph.idma_gather("dW", w2g[par][:, :], w2src, w1i[:, g:g + 1],
                                eo=blk * 524288)
        else:
            for hcl in range(HCL):
                ev = ph.idma_gather("dW", w2g[par][:, hcl * D:(hcl + 1) * D], w2src,
                                    w2i[:, g:g + 1], eo=(blk * BLK + hcl * 128) * D)
        ev_w[gi] = ev

    gather(0)
    gather(1)
    ngath = 2
    for g in range(CAP_G):
        ev_xload[g] = ph.dma(
            "sp", "dX", xg_tm[g % 2][:, :].rearrange("p (st d) -> p st d", st=4),
            bass.AP(L["xg_d"].tensor, 512 * g * D, [[D, 128], [128 * D, 4], [1, D]]),
            waits=[ev_scat, ev_tr.get(g - 1)])
        tr_done = []
        for st in range(4):
            for dc8 in range(0, NDC, 4):
                par2 = (st * 2 + dc8 // 4) % 2
                bank = pt[par2]
                trs = []
                for j in range(4):
                    dc = dc8 + j
                    etr = ph.emit("pe", lambda t, o=bank[:, j * 128:(j + 1) * 128],
                                  i=xg_tm[g % 2][:, st * D + dc * 128: st * D + dc * 128 + 128],
                                  ii=idb: t.transpose(o, i, ii),
                                  waits=([ev_xload[g], pv_pt[par2], d_pe] if j == 0 else ()))
                    trs.append(etr)
                for j in range(4):
                    dc = dc8 + j
                    ea = ph.emit("dve", lambda v, o=xgT[:, dc * 512 + st * 128: dc * 512 + st * 128 + 128],
                                 i=bank[:, j * 128:(j + 1) * 128]: v.tensor_copy(o, i),
                                 waits=[trs[-1]])
                pv_pt[par2] = ea
                tr_done.append(ea)
        ev_tr[g] = trs[-1]
        ev_ylast = None
        est_prev = {}
        for blk in range(NBLK2):
            gi = g * NBLK2 + blk
            par = gi % 2
            relu_evs = []
            for hcl in range(HCL):
                hb = pb[hcl % 2]
                w0 = [ev_w[gi], pv_h[hcl % 2], tr_done[-1] if hcl == 0 else None]
                emm = None
                for dc in range(NDC):
                    emm = ph.emit("pe", lambda t, o=hb[:, :],
                                  l=w1g[par][:, dc * BLK + hcl * 128: dc * BLK + hcl * 128 + 128],
                                  r=xgT[:, dc * 512:(dc + 1) * 512],
                                  s_=(dc == 0), p_=(dc == NDC - 1):
                                  t.matmul(o, l, r, start=s_, stop=p_),
                                  waits=w0 if dc == 0 else (),
                                  sem=False if dc < NDC - 1 else None)
                hc = blk * HCL + hcl
                erl = ph.emit("act", lambda a, o=hT[:, hcl * 512:(hcl + 1) * 512], i=hb[:, :],
                              b=b1all[:, hc * CAP_G + g: hc * CAP_G + g + 1]:
                              a.activation(o, i, AF.Relu, bias=b, scale=1.0), waits=[emm])
                pv_h[hcl % 2] = erl
                relu_evs.append(erl)
            for st in range(4):
                for dh in range(2):
                    q = (st * 2 + dh) % 4
                    yb = ps_big[:, q * 512:(q + 1) * 512]
                    w0 = [relu_evs[-1], pv_y[q]]
                    emm = None
                    for hcl in range(HCL):
                        emm = ph.emit("pe", lambda t, o=yb,
                                      l=hT[:, hcl * 512 + st * 128: hcl * 512 + st * 128 + 128],
                                      r=w2g[par][:, hcl * D + dh * 512: hcl * D + (dh + 1) * 512],
                                      s_=(hcl == 0), p_=(hcl == HCL - 1):
                                      t.matmul(o, l, r, start=s_, stop=p_),
                                      waits=w0 if hcl == 0 else (),
                                      sem=False if hcl < HCL - 1 else None)
                    ev_ylast = emm
                    dst = y_sb[g % 2][:, st * D + dh * 512: st * D + (dh + 1) * 512]
                    if blk == 0:
                        est = ph.emit("dve", lambda v, o=dst, i=yb: v.tensor_copy(o, i),
                                      waits=[emm, ev_yd.get(g - 1)])
                    else:
                        est = ph.emit("dve", lambda v, o=dst, i=yb: v.tensor_add(o, o, i),
                                      waits=[emm, est_prev.get((st, dh))])
                    est_prev[(st, dh)] = est
                    pv_y[q] = est
            ev_wcons[gi] = ev_ylast
            if ngath < CAP_G * NBLK2:
                gather(ngath)
                ngath += 1
        ev_yd[g] = ph.dma(
            "gp", "dW",
            bass.AP(L["y_d"].tensor, 512 * g * D, [[D, 128], [128 * D, 4], [1, D]]),
            y_sb[g % 2][:, :].rearrange("p (st d) -> p st d", st=4),
            waits=[pv_y[0], pv_y[1], pv_y[2], pv_y[3]])
    if DEBUG and which == 1:
        ph.dma("gp", "dW", L["dbg_y"].rearrange("(g p) d -> p g d", p=128),
               bass.AP(L["y_d"].tensor, 0, [[D, 128], [128 * D, CAP_S // 128], [1, D]]),
               waits=[ev_yd[CAP_G - 1]])
    ph.flush(f"grp{which}")
    stk.close()


def _ph_consume(P, L, which, ntile, sloti, g12):
    nc = P.nc
    stk = ExitStack()
    sb = lambda name, shape, dt: stk.enter_context(nc.sbuf_tensor(name, shape, dt))
    pref = f"fc{which}"
    yg = sb(pref + "_yg", [128, 4 * D], BF16)
    ygf = sb(pref + "_yf", [128, 4 * D], F32)
    b2nat = sb(pref + "_b2n", [NE, D], F32)
    ph = Phase(P)
    pb = L["pb"]
    ev_b2n = ph.dma("sp", "dX", b2nat[:, :], L["b2_in"])
    ev_ydall = P.snap("dW")
    d_act, d_dve = P.snap("c_act"), P.snap("c_dve")
    ysrc = bass.AP(L["y_d"].tensor, 0, [[D, CAP_S], [1, D]])
    ev_cast = {}
    pv_stt = {}
    pv_b2 = {0: P.snap("c_act"), 1: P.snap("c_act")}
    L["ffn_done"] = {}
    for T in range(ntile):
        par = T % 2
        resid = L[f"resid{T // NT}"]
        tt = T % NT
        evg = []
        for k in range(2):
            evg.append(ph.idma_gather(
                "dW", yg[:, (par * 2 + k) * D:(par * 2 + k + 1) * D], ysrc,
                sloti[:, 2 * T + k:2 * T + k + 1],
                waits=[ev_ydall, ev_cast.get(T - 2)] if k == 0 else ()))
        ev_cast[T] = ph.emit("act", lambda a, o=ygf[:, par * 2 * D:(par * 2 + 2) * D],
                             i=yg[:, par * 2 * D:(par * 2 + 2) * D]:
                             a.activation(o, i, AF.Copy, bias=0.0, scale=1.0),
                             waits=[evg[-1], d_act if T == 0 else None,
                                    pv_stt.get(T - 2)])
        e_sl = {}
        for k in range(2):
            for dh in range(2):
                e_sl[(k, dh)] = ph.emit(
                    "dve",
                    lambda v, o=resid[:, tt * D + dh * 512: tt * D + (dh + 1) * 512],
                    i=ygf[:, (par * 2 + k) * D + dh * 512:(par * 2 + k) * D + (dh + 1) * 512],
                    sc=g12[:, 2 * T + k:2 * T + k + 1]:
                    v.scalar_tensor_tensor(o, i, sc, o, ALU.mult, ALU.add),
                    waits=[ev_cast[T], e_sl.get((k - 1, dh)), pv_stt.get(T - 1)])
        pv_stt[T] = e_sl[(1, 1)]
        gTs = L[f"gT{T // NT}"]
        for dc2 in range(2):
            ypar = (T * 2 + dc2) % 2
            pbank = pb[ypar]
            emm = ph.emit("pe", lambda t, o=pbank[:, :],
                          l=gTs[0:16, tt * 128:(tt + 1) * 128],
                          r=b2nat[0:16, dc2 * 512:(dc2 + 1) * 512]:
                          t.matmul(o, l, r, start=True, stop=True),
                          waits=[pv_b2[ypar], d_dve, ev_b2n if T == 0 else None])
            dst = resid[:, tt * 1024 + dc2 * 512: tt * 1024 + (dc2 + 1) * 512]
            est = ph.emit("dve", lambda v, o=dst, i=dst, j=pbank[:, :]:
                          v.tensor_add(o, i, j), waits=[emm, e_sl[(1, dc2)]])
            pv_b2[ypar] = est
        if T % NT == NT - 1:
            L["ffn_done"][T // NT] = pv_b2[(T * 2 + 1) % 2]
    ph.flush(f"cons{which}")
    stk.close()


def _ph_attn_b(P, L, s, feeder=None):
    nc = P.nc
    NG = 4
    HPG = NH // NG
    IW = HPG * HD
    NIT = IW // 128
    es2 = ExitStack()
    sb = lambda name, shape, dt: es2.enter_context(nc.sbuf_tensor(name, shape, dt))
    wbuf = [sb(f"ab_w{i}_{s}", [128, NDC * IW], BF16) for i in range(2)]
    woutb = sb(f"ab_wo_{s}", [128, NIT * D], BF16)
    qbT = sb(f"ab_q_{s}", [128, NIT * S], BF16)
    kbT = sb(f"ab_k_{s}", [128, NIT * S], BF16)
    vb = sb(f"ab_v_{s}", [128, NT * IW], BF16)
    outg = sb(f"ab_o_{s}", [128, NIT * S], BF16)
    bscr = sb(f"ab_s_{s}", [128, 2 * S], F32)
    abf = sb(f"ab_ab_{s}", [128, 2 * S], BF16)
    atb = sb(f"ab_at_{s}", [128, 2 * 1024], BF16)
    xlt, resid, idb, mask01 = L["xlt"], L["resid"], L["idb"][:, :], L["mask01"]
    ps_big, pb, pt = L["ps_big"], L["pb"], L["pt"]
    stats = L["stats"]
    with es2:
        for g in range(NG):
            ph = Phase(P)
            dA = "dA"
            def load_qkv(off, par, waits):
                evs = []
                for dc in range(NDC):
                    evs.append(ph.dma(
                        "gp", dA,
                        wbuf[par][:, dc * IW:(dc + 1) * IW].rearrange("p (hh e) -> p hh e", hh=HPG),
                        bass.AP(L["Wqkv"].tensor, g * HPG * D * 192 + dc * 128 * 192 + off,
                                [[192, 128], [D * 192, HPG], [1, 64]]),
                        waits=waits if dc == 0 else ()))
                return evs[-1]

            ev_q = load_qkv(0, 0, [])
            ev_k = load_qkv(64, 1, [])
            ewo = ph.dma("gp", dA, woutb[:, :].rearrange("p (it d) -> p it d", it=NIT),
                         bass.AP(L["Wout"].tensor, g * IW * D, [[D, 128], [128 * D, NIT], [1, D]]))
            pvt = {0: P.snap("c_dve"), 1: P.snap("c_dve")}
            qdone = None
            for wi, dstT, wev in ((0, qbT, ev_q), (1, kbT, ev_k)):
                bcol = L["bqc"] if wi == 0 else L["bkc"]
                for it in range(NIT):
                    for ng in range(4):
                        key = (it * 4 + ng) % 2
                        pbank = pb[key]
                        emm = None
                        for dc in range(NDC):
                            w0 = [wev, pvt.get(key)] if dc == 0 else ()
                            emm = ph.emit("pe", lambda t, o=pbank[:, :],
                                          l=wbuf[wi][:, dc * IW + it * 128: dc * IW + it * 128 + 128],
                                          r=xlt[:, dc * S + ng * 512: dc * S + (ng + 1) * 512],
                                          s_=(dc == 0), p_=(dc == NDC - 1):
                                          t.matmul(o, l, r, start=s_, stop=p_),
                                          waits=w0, sem=False if dc < NDC - 1 else None)
                        dst = dstT[:, it * S + ng * 512: it * S + (ng + 1) * 512]
                        bc = bcol[:, g * NIT + it: g * NIT + it + 1]
                        eev = ph.emit("act", lambda a, o=dst, i=pbank[:, :], b=bc:
                                      a.activation(o, i, AF.Identity, bias=b, scale=1.0), waits=[emm])
                        pvt[key] = eev
                if wi == 0:
                    qdone = emm
            ev_v = load_qkv(128, 0, [qdone])
            for tt in range(NT):
                key = tt % 2
                pbank = pb[key]
                emm = None
                for dc in range(NDC):
                    w0 = [ev_v, pvt.get(key)] if dc == 0 else ()
                    emm = ph.emit("pe", lambda t, o=pbank[:, 0:IW],
                                  l=xlt[:, dc * S + tt * 128: dc * S + tt * 128 + 128],
                                  r=wbuf[0][:, dc * IW:(dc + 1) * IW],
                                  s_=(dc == 0), p_=False:
                                  t.matmul(o, l, r, start=s_, stop=False),
                                  waits=w0, sem=False)
                emm = ph.emit("pe", lambda t, o=pbank[:, 0:IW], l=L["ones32"][:1, :],
                              r=L["bvr"][:1, g * IW:(g + 1) * IW]:
                              t.matmul(o, l, r, start=False, stop=True))
                dst = vb[:, tt * IW:(tt + 1) * IW]
                eev = ph.emit("act", lambda a, o=dst, i=pbank[:, 0:IW]:
                              a.activation(o, i, AF.Copy, bias=0.0, scale=1.0), waits=[emm])
                pvt[key] = eev
            ph.flush(f"s{s}_ab_g{g}_w")

            ph = Phase(P)
            ev_proj = P.snap("c_act")
            half_ev = {}
            prev_bs = {}
            prev_abf_rd = {}
            prev_at = {0: P.snap("c_dve"), 1: P.snap("c_dve")}
            prev_avp = {}
            cidx = 0
            first = True
            pending = []
            tiles = [(h, qt) for h in range(HPG) for qt in range(NT)]

            def emit_tail(h, qt, par, e_ab, idx):
                nk = qt + 1
                it = h // 2
                rb = 64 * (h % 2)
                avp = pb[par]
                e_av = None
                nbt = (nk + 7) // 8
                te_evs = []
                for bt in range(nbt):
                    k0 = bt * 8
                    k1 = min(nk, k0 + 8)
                    bank = pt[(idx + bt) % 2]
                    trs = []
                    for kt in range(k0, k1):
                        ev = ph.emit("pe", lambda t, o=bank[:, (kt - k0) * 128:(kt - k0 + 1) * 128],
                                     i=abf[:, par * S + kt * 128: par * S + (kt + 1) * 128], ii=idb:
                                     t.transpose(o, i, ii),
                                     waits=[e_ab, prev_at.get((idx + bt) % 2)] if kt == k0 else ())
                        trs.append(ev)
                    prev_abf_rd[par] = trs[-1]
                    abase = ((idx + bt) % 2) * 1024
                    e_te = ph.emit("dve", lambda v, o=atb[:, abase: abase + (k1 - k0) * 128],
                                   i=bank[:, 0:(k1 - k0) * 128]: v.tensor_copy(o, i), waits=[trs[-1]])
                    prev_at[(idx + bt) % 2] = e_te
                    te_evs.append(e_te)
                for bt in range(nbt):
                    k0 = bt * 8
                    k1 = min(nk, k0 + 8)
                    abase = ((idx + bt) % 2) * 1024
                    for kt in range(k0, k1):
                        last = (kt == nk - 1)
                        ev = ph.emit("pe", lambda t, o=avp[0:64, 0:128],
                                     l=vb[:, kt * IW + h * 64: kt * IW + h * 64 + 64],
                                     r=atb[:, abase + (kt - k0) * 128: abase + (kt - k0 + 1) * 128],
                                     s_=(kt == 0), p_=last:
                                     t.matmul(o, l, r, start=s_, stop=p_),
                                     waits=[te_evs[bt], prev_avp.get(par)] if kt == k0 else (),
                                     sem=False if not last else None)
                        if last:
                            e_av = ev
                dst = outg[rb:rb + 64, it * S + qt * 128: it * S + qt * 128 + 128]
                e_oe = ph.emit("act", lambda a, o=dst, i=avp[0:64, 0:128]:
                               a.activation(o, i, AF.Copy, bias=0.0, scale=1.0), waits=[e_av])
                prev_avp[par] = e_oe

            for idx, (h, qt) in enumerate(tiles):
                par = idx % 2
                nk = qt + 1
                ncols = nk * 128
                it = h // 2
                rb = 64 * (h % 2)
                qsl = qbT[rb:rb + 64, it * S + qt * 128: it * S + qt * 128 + 128]
                e_e = None
                for c0 in range(0, ncols, 1024):
                    c1 = min(ncols, c0 + 1024)
                    half = cidx % 2
                    base = half * 1024
                    sc_evs = []
                    firstmm = True
                    for cc0 in range(c0, c1, 512):
                        cc1 = min(c1, cc0 + 512)
                        w0 = []
                        if firstmm:
                            w0 = [half_ev.get(half)] + ([ev_proj] if first else [])
                            first = False
                            firstmm = False
                        ev = ph.emit("pe", lambda t, o=ps_big[:, base + cc0 - c0: base + cc1 - c0],
                                     l=qsl, r=kbT[rb:rb + 64, it * S + cc0: it * S + cc1]:
                                     t.matmul(o, l, r, start=True, stop=True), waits=w0)
                        sc_evs.append(ev)
                    we = sc_evs + ([prev_bs.get(par)] if c0 == 0 else [])
                    e_e = ph.emit("act", lambda a, o=bscr[:, par * S + c0: par * S + c1],
                                  i=ps_big[:, base: base + c1 - c0]:
                                  a.activation(o, i, AF.Exp, bias=0.0, scale=0.125), waits=we)
                    half_ev[half] = e_e
                    cidx += 1
                if pending:
                    emit_tail(*pending.pop(0))
                bsl = bscr[:, par * S: par * S + ncols]
                dsl = bscr[:, par * S + (nk - 1) * 128: par * S + ncols]
                e_z = ph.emit("dve", lambda v, o=dsl, i=dsl, j=mask01[:, :]:
                              v.tensor_mul(o, i, j), waits=[e_e])
                cr = stats[:, 52 + par: 52 + par + 1]
                e_r = ph.emit("dve", lambda v, o=cr, i=bsl: v.reduce_sum(o, i, AX.X), waits=[e_z])
                e_rc = ph.emit("dve", lambda v, o=cr, i=cr: v.reciprocal(o, i), waits=[e_r])
                absl = abf[:, par * S: par * S + ncols]
                e_ab = ph.emit("dve", lambda v, o=absl, i=bsl, r=cr:
                               v.tensor_scalar(o, i, r, None, ALU.mult),
                               waits=[e_rc, prev_abf_rd.get(par)])
                prev_bs[par] = e_ab
                pending.append((h, qt, par, e_ab, idx))
            while pending:
                emit_tail(*pending.pop(0))
            lastout = P.snap("c_act")
            pvt2 = dict(prev_avp)
            for tt in range(NT):
                for oc in range(2):
                    key = (tt * 2 + oc) % 2
                    pbank = pb[key]
                    emm = None
                    for it in range(NIT):
                        last = (it == NIT - 1) and (g != 0)
                        w0 = [lastout, pvt2.get(key), ewo] if it == 0 else ()
                        emm = ph.emit("pe", lambda t, o=pbank[:, :],
                                      l=outg[:, it * S + tt * 128: it * S + tt * 128 + 128],
                                      r=woutb[:, it * 1024 + oc * 512: it * 1024 + (oc + 1) * 512],
                                      s_=(it == 0), p_=last:
                                      t.matmul(o, l, r, start=s_, stop=p_),
                                      waits=w0, sem=False if not last else None)
                    if g == 0:
                        emm = ph.emit("pe", lambda t, o=pbank[:, :], l=L["ones32"][:1, :],
                                      r=L["boutr"][:1, oc * 512:(oc + 1) * 512]:
                                      t.matmul(o, l, r, start=False, stop=True))
                    dst = resid[:, tt * 1024 + oc * 512: tt * 1024 + (oc + 1) * 512]
                    eev = ph.emit("dve", lambda v, o=dst, i=dst, j=pbank[:, :]:
                                  v.tensor_add(o, i, j), waits=[emm])
                    pvt2[key] = eev
            ph.flush(f"s{s}_ab_g{g}_at")
            if feeder is not None:
                feeder.feed(16)


KEYS_DEV = ("L0_attn_a_Wq", "L0_attn_a_Wk", "L0_attn_a_Wv", "L0_attn_a_Wo",
            "L0_attn_b_Wqkv", "L0_attn_b_Wout", "L0_ffn_W1", "L0_ffn_W2",
            "L0_ffn_b1", "L0_ffn_b2")


def kernel(**inputs):
    x = np.ascontiguousarray(np.asarray(inputs["L0_attn_a_x"], np.float32))
    aux = build_aux(inputs)
    nc = build_program(n_seq=2)
    base = {k: np.ascontiguousarray(np.asarray(inputs[k], np.float32)) for k in KEYS_DEV}
    base.update(aux)
    in_maps = []
    for c in range(8):
        m = dict(base)
        m["x"] = x[2 * c:2 * c + 2]
        in_maps.append(m)
    res = run_bass_kernel_spmd(nc, in_maps, core_ids=list(range(8)))
    return np.concatenate([r["y"] for r in res.results], axis=0)


# version 10 (final)
# speedup vs baseline: 1.1140x; 1.0058x over previous
import numpy as np
import ml_dtypes
from contextlib import ExitStack
import concourse.bass as bass
import concourse.mybir as mybir
from concourse.bass_utils import run_bass_kernel_spmd
from concourse.bass_core_rust import IndirectOffsetOnAxis

F32 = mybir.dt.float32
BF16 = mybir.dt.bfloat16
I32 = mybir.dt.int32
AF = mybir.ActivationFunctionType
ALU = mybir.AluOpType
AX = mybir.AxisListType
bfloat16 = ml_dtypes.bfloat16

S = 2048
D = 1024
NH = 16
HD = 64
WIN = 128
GLB = 64
DH = 4096
NE = 16
NT = S // 128
NDC = D // 128
NT2 = 2 * NT
CAP_G = 32
CAP_S = CAP_G * 512
THRN = 8
BLK = 512
NBLK2 = DH // BLK
HCL = BLK // 128

ENGS = ("pe", "dve", "act", "gp", "sp")
DEBUG = False


class Ev:
    __slots__ = ("sem", "val")

    def __init__(self, sem, val):
        self.sem = sem
        self.val = val


class Prog:
    def __init__(self, nc, es):
        self.nc = nc
        self.sems = {}
        self.cnt = {}
        for e in ENGS:
            self._mk(es, "c_" + e)
        for d in ("dW", "dA", "dX"):
            self._mk(es, d)
        self.waited = {e: {} for e in ENGS}

    def _mk(self, es, name):
        self.sems[name] = es.enter_context(self.nc.semaphore(name))
        self.cnt[name] = 0

    def snap(self, name):
        return Ev(self.sems[name], self.cnt[name])


class Phase:
    def __init__(self, P):
        self.P = P
        self.items = {e: [] for e in ENGS}
        self.start = {e: P.cnt["c_" + e] for e in ENGS}

    def emit(self, eng, fn, waits=(), sem=None, k=1):
        P = self.P
        ws = [w for w in waits if w is not None]
        if sem is None:
            sem = "c_" + eng
        if sem is not False:
            P.cnt[sem] += k
            ev = Ev(P.sems[sem], P.cnt[sem])
            self.items[eng].append((ws, fn, P.sems[sem], k))
            return ev
        self.items[eng].append((ws, fn, None, 0))
        return None

    def dma(self, queue_eng, dsem, dst, src, waits=()):
        return self.emit(queue_eng, lambda g, d=dst, s=src: g.dma_start(d, s),
                         waits=waits, sem=dsem, k=16)

    def idma_gather(self, dsem, dst, src_ap, idx_ap, eo=0, waits=()):
        return self.emit(
            "gp",
            lambda g, d=dst, s=src_ap, i=idx_ap, e=eo: g.indirect_dma_start(
                d, None, s, IndirectOffsetOnAxis(i, 0), element_offset=e),
            waits=waits, sem=dsem, k=16)

    def idma_scatter(self, dsem, dst_ap, idx_ap, src, eo=0, waits=()):
        return self.emit(
            "gp",
            lambda g, d=dst_ap, i=idx_ap, s=src, e=eo: g.indirect_dma_start(
                d, IndirectOffsetOnAxis(i, 0), s, None, element_offset=e),
            waits=waits, sem=dsem, k=16)

    def flush(self, name):
        P = self.P
        nc = P.nc
        with nc.Block(name) as b:
            def mk(eng_name):
                def go(eng):
                    pos = self.start[eng_name]
                    own = P.sems["c_" + eng_name]
                    for ws, fn, sem, k in self.items[eng_name]:
                        for w in ws:
                            if w.sem is own and w.val > pos:
                                raise RuntimeError(
                                    f"{name}/{eng_name}: wait on own future")
                            seen = P.waited[eng_name].get(w.sem.name, -1)
                            if w.val > seen:
                                eng.wait_ge(w.sem, w.val)
                                P.waited[eng_name][w.sem.name] = w.val
                        inst = fn(eng)
                        if sem is not None:
                            inst.then_inc(sem, k)
                            if sem is own:
                                pos += 1
                return go
            if self.items["gp"]:
                b.gpsimd(mk("gp"))
            if self.items["sp"]:
                b.sync(mk("sp"))
            if self.items["pe"]:
                b.tensor(mk("pe"))
            if self.items["dve"]:
                b.vector(mk("dve"))
            if self.items["act"]:
                b.scalar(mk("act"))


def build_aux(inputs):
    b1 = np.asarray(inputs["L0_ffn_b1"], np.float32)
    bqkv = np.asarray(inputs["L0_attn_b_bqkv"], np.float32)
    wr = np.asarray(inputs["L0_ffn_Wr"], np.float32)
    aux = {
        "aux_idf": np.eye(128, dtype=np.float32),
        "aux_idb": np.eye(128, dtype=bfloat16),
        "aux_ones": np.ones((1, 128), dtype=np.float32),
        "aux_mask": np.tril(np.ones((128, 128), dtype=np.float32)),
        "aux_b1c": np.ascontiguousarray(
            b1.reshape(NE, 32, 128).transpose(2, 0, 1).reshape(128, NE * 32)),
        "aux_bqc": np.ascontiguousarray(
            bqkv[:, 0:64].reshape(8, 2, 64).transpose(2, 1, 0).reshape(128, 8)),
        "aux_bkc": np.ascontiguousarray(
            bqkv[:, 64:128].reshape(8, 2, 64).transpose(2, 1, 0).reshape(128, 8)),
        "aux_bvr": np.ascontiguousarray(bqkv[:, 128:192].reshape(1, NH * 64)),
        "aux_wr": np.ascontiguousarray(
            wr.reshape(NDC, 128, NE).transpose(1, 0, 2).reshape(128, NDC * NE)),
        "aux_boutr": np.asarray(inputs["L0_attn_b_bout"], np.float32).reshape(1, D).copy(),
        "aux_brr": np.asarray(inputs["L0_ffn_br"], np.float32).reshape(1, NE).copy(),
        "aux_su128": np.triu(np.ones((128, 128), np.float32), 1),
        "aux_su16": np.triu(np.ones((16, 16), np.float32), 1),
        "aux_io16": np.broadcast_to(np.arange(16, dtype=np.float32), (128, 16)).copy(),
        "aux_io16r": np.broadcast_to(16.0 - np.arange(16, dtype=np.float32), (128, 16)).copy(),
        "aux_iota128": np.arange(128, dtype=np.float32).reshape(128, 1),
        "aux_iota16": np.arange(16, dtype=np.float32).reshape(16, 1),
        "aux_ones128c": np.ones((128, 1), np.float32),
        "aux_ones16c": np.ones((16, 1), np.float32),
        "aux_g512": np.broadcast_to(512.0 * np.arange(CAP_G, dtype=np.float32), (16, CAP_G)).copy(),
        "aux_su32": np.triu(np.ones((32, 32), np.float32), 1),
        "aux_thr16": np.broadcast_to(512.0 * np.arange(THRN, dtype=np.float32), (16, THRN)).copy(),
        "aux_c512": np.full((16, 1), 512.0, np.float32),
        "aux_c1024": np.full((128, 1), 1024.0, np.float32),
        "aux_c4096": np.full((128, 1), 4096.0, np.float32),
    }
    return aux


def build_program(n_seq=2, parts=("attn_a", "ffn1", "attn_b", "ffn2")):
    nc = bass.Bass("TRN2", target_bir_lowering=False)
    L = {}

    def din(key, name, shape, dtype=F32):
        L[key] = nc.dram_tensor(name, list(shape), dtype, kind="ExternalInput").ap()

    din("x_in", "x", [n_seq, S, D])
    L["y_out"] = nc.dram_tensor("y", [n_seq, S, D], F32, kind="ExternalOutput").ap()
    din("Wq", "L0_attn_a_Wq", [D, D])
    din("Wk", "L0_attn_a_Wk", [D, D])
    din("Wv", "L0_attn_a_Wv", [D, D])
    din("Wo", "L0_attn_a_Wo", [D, D])
    din("Wqkv", "L0_attn_b_Wqkv", [NH, D, 3 * HD])
    din("Wout", "L0_attn_b_Wout", [D, D])
    din("W1", "L0_ffn_W1", [NE, D, DH])
    din("W2", "L0_ffn_W2", [NE, DH, D])
    din("b2_in", "L0_ffn_b2", [NE, D])
    din("a_idf", "aux_idf", [128, 128])
    din("a_idb", "aux_idb", [128, 128], BF16)
    din("a_ones", "aux_ones", [1, 128])
    din("a_mask", "aux_mask", [128, 128])
    din("a_b1c", "aux_b1c", [128, NE * 32])
    din("a_bqc", "aux_bqc", [128, 8])
    din("a_bkc", "aux_bkc", [128, 8])
    din("a_bvr", "aux_bvr", [1, NH * 64])
    din("a_wr", "aux_wr", [128, NDC * NE])
    din("a_boutr", "aux_boutr", [1, D])
    din("a_brr", "aux_brr", [1, NE])
    din("b1_in", "L0_ffn_b1", [NE, DH])
    din("a_su128", "aux_su128", [128, 128])
    din("a_su16", "aux_su16", [16, 16])
    din("a_io16", "aux_io16", [128, 16])
    din("a_io16r", "aux_io16r", [128, 16])
    din("a_iota128", "aux_iota128", [128, 1])
    din("a_iota16", "aux_iota16", [16, 1])
    din("a_ones128c", "aux_ones128c", [128, 1])
    din("a_ones16c", "aux_ones16c", [16, 1])
    din("a_g512", "aux_g512", [16, CAP_G])
    din("a_su32", "aux_su32", [32, 32])
    din("a_thr16", "aux_thr16", [16, THRN])
    din("a_c512", "aux_c512", [16, 1])
    din("a_c1024", "aux_c1024", [128, 1])
    din("a_c4096", "aux_c4096", [128, 1])
    L["w1bf"] = nc.dram_tensor("w1bf", [NE * D, DH], BF16, kind="Internal").ap()
    L["w2bf"] = nc.dram_tensor("w2bf", [NE * D, DH], BF16, kind="Internal").ap()
    L["xg_d"] = nc.dram_tensor("xg_d", [CAP_S, D], BF16, kind="Internal").ap()
    L["y_d"] = nc.dram_tensor("y_d", [CAP_S, D], BF16, kind="Internal").ap()
    L["meta_d"] = nc.dram_tensor("meta_d", [32, 16], F32, kind="Internal").ap()
    if DEBUG:
        for nm, shape, dt in (("dbg_sl", [128, 2 * NT2], I32), ("dbg_g12", [128, 2 * NT2], F32),
                              ("dbg_cb", [32, 16], F32), ("dbg_eog", [1, CAP_G], F32),
                              ("dbg_mc", [16, 16], F32), ("dbg_mc2", [16, 16], F32),
                              ("dbg_b1", [128, 32 * CAP_G], F32),
                              ("dbg_xg", [CAP_S, D], BF16), ("dbg_y", [CAP_S, D], BF16)):
            L[nm] = nc.dram_tensor(nm, shape, dt, kind="ExternalOutput").ap()

    es = ExitStack()
    sb = lambda name, shape, dt: es.enter_context(nc.sbuf_tensor(name, shape, dt))
    L["resid0"] = sb("resid0", [128, NT * D], BF16)
    L["resid1"] = sb("resid1", [128, NT * D], BF16)
    L["gate0"] = sb("gate0", [128, NT * NE], F32)
    L["gate1"] = sb("gate1", [128, NT * NE], F32)
    L["idf"] = sb("idf", [128, 128], F32)
    L["idb"] = sb("idb", [128, 128], BF16)
    L["ones32"] = sb("ones32", [1, 128], F32)
    L["mask01"] = sb("mask01", [128, 128], F32)
    L["b1c"] = sb("b1c", [128, NE * 32], F32)
    L["wr_sb"] = sb("wr_sb", [128, NDC * NE], F32)
    L["bqc"] = sb("bqc", [128, 8], F32)
    L["bkc"] = sb("bkc", [128, 8], F32)
    L["gT0"] = sb("gT0", [NE, S], F32)
    L["gT1"] = sb("gT1", [NE, S], F32)
    L["trash"] = sb("trash", [128, D], F32)
    L["su128"] = sb("su128", [128, 128], F32)
    L["su16"] = sb("su16", [16, 16], F32)
    L["io16"] = sb("io16", [128, 16], F32)
    L["io16r"] = sb("io16r", [128, 16], F32)
    L["iota128"] = sb("iota128", [128, 1], F32)
    L["iota16"] = sb("iota16", [16, 1], F32)
    L["ones128c"] = sb("ones128c", [128, 1], F32)
    L["ones16c"] = sb("ones16c", [16, 1], F32)
    L["g512"] = sb("g512", [16, CAP_G], F32)
    L["su32"] = sb("su32", [32, 32], F32)
    L["thr16"] = sb("thr16", [16, THRN], F32)
    L["c512"] = sb("c512", [16, 1], F32)
    L["c1024"] = sb("c1024", [128, 1], F32)
    L["c4096"] = sb("c4096", [128, 1], F32)
    L["bvr"] = sb("bvr", [1, NH * 64], F32)
    L["boutr"] = sb("boutr", [1, D], F32)
    L["brr"] = sb("brr", [1, NE], F32)
    L["stats"] = sb("stats", [128, 64], F32)
    L["u16"] = sb("u16", [128, 2 * 16], F32)
    L["u16b"] = sb("u16b", [128, 2 * 16], F32)

    L["ps_big"] = es.enter_context(nc.psum_tensor("ps_big", [128, 2048], F32))
    L["pb"] = [es.enter_context(nc.psum_tensor(f"pb{i}", [128, 512], F32)) for i in range(2)]
    L["pt"] = [es.enter_context(nc.psum_tensor(f"pt{i}", [128, 1024], BF16)) for i in range(2)]

    P = Prog(nc, es)
    with es:
        _emit_all(P, L, n_seq, parts)
    return nc


class _Feeder:

    def __init__(self, P, L, src_key, dst_key, nchunks=128):
        self.P, self.L = P, L
        self.src, self.dst = src_key, dst_key
        self.k = 0
        self.n = nchunks

    def feed(self, cnt, ph=None):
        if self.k >= self.n:
            return
        own = ph is None
        if own:
            ph = Phase(self.P)
        c1 = min(self.n, self.k + cnt)
        for k in range(self.k, c1):
            e, blk = k // 8, k % 8
            dst = bass.AP(self.L[self.dst].tensor, (e * 1024 + blk * 128) * 4096,
                          [[4096, 128], [1, 4096]])
            if self.src == "W1":
                s = bass.AP(self.L[self.src].tensor, e * 4194304 + blk * 512,
                            [[4096, 128], [524288, 8], [1, 512]])
                dst = dst.rearrange("p (dc j) -> p dc j", dc=8)
            else:
                s = bass.AP(self.L[self.src].tensor, (e * 4096 + blk * 512) * 1024,
                            [[1024, 128], [131072, 4], [1, 1024]])
                dst = dst.rearrange("p (hc j) -> p hc j", hc=4)
            ph.dma("gp", "dW", dst, s)
        self.k = c1
        if own:
            ph.flush(f"pc_{self.dst}_{c1}")

    def drain(self):
        self.feed(self.n - self.k)


def _set_seq(L, s):
    L["resid"] = L[f"resid{s}"]
    L["gate"] = L[f"gate{s}"]
    L["gT"] = L[f"gT{s}"]


def _emit_all(P, L, n_seq, parts):
    nc = P.nc
    ph = Phase(P)
    dx = "dX"
    for dst, src in (
        (L["idf"], "a_idf"), (L["idb"], "a_idb"), (L["ones32"], "a_ones"),
        (L["mask01"], "a_mask"), (L["b1c"], "a_b1c"), (L["wr_sb"], "a_wr"),
        (L["bqc"], "a_bqc"), (L["bkc"], "a_bkc"),
        (L["bvr"], "a_bvr"), (L["boutr"], "a_boutr"), (L["brr"], "a_brr"),
        (L["su128"], "a_su128"), (L["su16"], "a_su16"),
        (L["io16"], "a_io16"), (L["io16r"], "a_io16r"),
        (L["iota128"], "a_iota128"), (L["iota16"], "a_iota16"),
        (L["ones128c"], "a_ones128c"), (L["ones16c"], "a_ones16c"),
        (L["g512"], "a_g512"), (L["su32"], "a_su32"), (L["thr16"], "a_thr16"), (L["c512"], "a_c512"),
        (L["c1024"], "a_c1024"), (L["c4096"], "a_c4096"),
    ):
        ph.dma("sp", dx, dst[:, :], L[src])
    tc = L["stats"][:, 60:61]
    ph.emit("dve", lambda v: v.tensor_scalar_mul(tc, L["ones128c"][:, :], 1.0 / D))
    ph.emit("dve", lambda v: v.tensor_scalar_add(tc, L["ones128c"][:, :], 1e-5))
    ph.emit("dve", lambda v: v.tensor_scalar(tc, L["ones128c"][:, :], -1.0, 1.0,
                                             ALU.mult, ALU.add))
    ph.flush("consts")
    ev_consts = P.snap(dx)

    ph = Phase(P)
    for s in range(n_seq):
        rsrc = L["x_in"][s].rearrange("(tt p) d -> p tt d", p=128)
        rdst = L[f"resid{s}"][:, :].rearrange("p (tt d) -> p tt d", tt=NT)
        for h in range(2):
            ph.dma("gp", dx, rdst[:, h * 8:(h + 1) * 8, :], rsrc[:, h * 8:(h + 1) * 8, :])
    ph.flush("loadx")
    ev_x = P.snap(dx)

    fdr1 = _Feeder(P, L, "W1", "w1bf")
    fdr2 = _Feeder(P, L, "W2", "w2bf")
    dep = [ev_x, ev_consts]
    if "attn_a" in parts:
        for s in range(n_seq):
            _set_seq(L, s)
            stkA = ExitStack()
            L["xlt"] = stkA.enter_context(nc.sbuf_tensor(f"xlta_{s}", [128, NDC * S], BF16))
            _ph_ln(P, L, s, 0, router=False, dep=dep, xlt=L["xlt"])
            _ph_attn_a(P, L, s, feeder=fdr1)
            stkA.close()
            dep = []
        fdr1.drain()
    if "ffn1" in parts:
        _ph_ffn_sparse(P, L, 1, n_seq, dep=dep)
        dep = []
    if "attn_b" in parts:
        for s in range(n_seq):
            _set_seq(L, s)
            stkB = ExitStack()
            L["xlt"] = stkB.enter_context(nc.sbuf_tensor(f"xltb_{s}", [128, NDC * S], BF16))
            _ph_ln(P, L, s, 2, router=False,
                   dep=dep + [L["ffn_done"][s], P.snap("c_act")], xlt=L["xlt"])
            _ph_attn_b(P, L, s, feeder=fdr2)
            stkB.close()
            dep = []
        fdr2.drain()
    if "ffn2" in parts:
        _ph_ffn_sparse(P, L, 3, n_seq, dep=dep)
        dep = []

    ph = Phase(P)
    evout = [P.snap("c_dve"), P.snap("c_pe")]
    for s in range(n_seq):
        ysrc = L[f"resid{s}"][:, :].rearrange("p (tt d) -> p tt d", tt=NT)
        ydst = L["y_out"][s].rearrange("(tt p) d -> p tt d", p=128)
        for h in range(2):
            ph.dma("gp", dx, ydst[:, h * 8:(h + 1) * 8, :], ysrc[:, h * 8:(h + 1) * 8, :],
                   waits=evout + ([dep[0]] if dep else []))
    ph.flush("store")

    ph = Phase(P)
    evd = P.snap(dx)
    evw = P.snap("dW")
    ph.emit("sp", lambda e: None, waits=[evd, evw], sem=False)
    ph.flush("final")


def _ph_ln(P, L, s, which, router, dep=(), xlt=None, xtm=None, xtm_off=0):
    nc = P.nc
    stk = ExitStack()
    sb = lambda name, shape, dt: stk.enter_context(nc.sbuf_tensor(name, shape, dt))
    normscr = sb(f"ln{s}{which}_ns", [128, 4 * D], F32)
    trash = L["trash"]
    ln32 = sb(f"ln{s}{which}_l32", [128, 2 * 1024], F32) if router else None
    ph = Phase(P)
    resid, stats = L["resid"], L["stats"]
    idf, ps_big, pb = L["idf"][:, :], L["ps_big"], L["pb"]
    ev_stB = {}
    ev_trans = {}
    ev_xtm = {}
    prev_bank = {b: P.snap("c_dve") for b in range(4)}
    prev_ln32 = {}
    prev_gate = {}
    prev_pbe = {0: P.snap("c_dve"), 1: P.snap("c_dve")}
    ev_enm = {}

    def col(tt, i):
        p4 = tt % 4
        return stats[:, p4 * 8 + i: p4 * 8 + i + 1]

    def stage_a(tt):
        rtile = resid[:, tt * 1024:(tt + 1) * 1024]
        w0 = list(dep) + [ev_stB.get(tt - 4)]
        e1 = ph.emit("dve", lambda v, a=col(tt, 0), b=rtile: v.reduce_sum(a, b, AX.X), waits=w0)
        e2 = ph.emit("act", lambda a, o=trash[:, :], i=rtile, c=col(tt, 2):
                     a.activation(o, i, AF.Square, bias=0.0, scale=1.0, accum_out=c),
                     waits=w0)
        return e1, e2

    def stage_b(tt, e1, e2):
        rtile = resid[:, tt * 1024:(tt + 1) * 1024]
        ns = normscr[:, (tt % 4) * 1024:((tt % 4) + 1) * 1024]
        em = ph.emit("dve", lambda v, a=col(tt, 1), b=col(tt, 0): v.tensor_scalar_mul(a, b, 1.0 / D), waits=[e1])
        emm = ph.emit("dve", lambda v, a=col(tt, 4), b=col(tt, 1): v.tensor_mul(a, b, b), waits=[em])
        v1 = ph.emit("dve", lambda v, a=col(tt, 5), b=col(tt, 2), c=col(tt, 4):
                     v.tensor_scalar(a, b, 1.0 / D, c, ALU.mult, ALU.subtract), waits=[e2, emm])
        v2 = ph.emit("dve", lambda v, a=col(tt, 5), b=col(tt, 5): v.tensor_scalar_add(a, b, 1e-5), waits=[v1])
        eln = ph.emit("act", lambda a, o=col(tt, 6), i=col(tt, 5): a.activation(o, i, AF.Ln, bias=0.0, scale=1.0), waits=[v2])
        ers = ph.emit("act", lambda a, o=col(tt, 7), i=col(tt, 6): a.activation(o, i, AF.Exp, bias=0.0, scale=-0.5), waits=[eln])
        enm = ph.emit("dve", lambda v, o=ns, i=rtile, m=col(tt, 1), r=col(tt, 7):
                      v.tensor_scalar(o, i, m, r, ALU.subtract, ALU.mult),
                      waits=[ers, ev_trans.get(tt - 4), ev_xtm.get(tt - 4)])
        ev_enm[tt] = enm
        ev_stB[tt] = enm
        if xtm is not None:
            ev_xtm[tt] = ph.emit("act",
                                 lambda a, o=xtm[:, (xtm_off + tt) * D:(xtm_off + tt + 1) * D],
                                 i=ns:
                                 a.activation(o, i, AF.Copy, bias=0.0, scale=1.0),
                                 waits=[enm])

    def stage_c(tt):
        par2 = tt % 2
        ns = normscr[:, (tt % 4) * 1024:((tt % 4) + 1) * 1024]
        enm = ev_enm[tt]
        last_tr = None
        ev_last = []
        for half in range(2):
            bank = (2 * tt + half) % 4
            bslots = ps_big[:, bank * 512:(bank + 1) * 512]
            trs = []
            for j in range(4):
                dc = half * 4 + j
                etr = ph.emit("pe", lambda t, o=bslots[:, j * 128:(j + 1) * 128],
                              i=ns[:, dc * 128:(dc + 1) * 128], ii=idf:
                              t.transpose(o, i, ii),
                              waits=[enm, prev_bank.get(bank)] if j == 0 else ())
                trs.append(etr)
            evs = []
            eng = "act" if half == 0 else "dve"
            if xlt is not None:
                for j in range(4):
                    dc = half * 4 + j
                    dst = xlt[:, dc * S + tt * 128: dc * S + tt * 128 + 128]
                    if eng == "act":
                        ea = ph.emit("act", lambda a, o=dst, i=bslots[:, j * 128:(j + 1) * 128]:
                                     a.activation(o, i, AF.Copy, bias=0.0, scale=1.0), waits=[trs[-1]])
                    else:
                        ea = ph.emit("dve", lambda v, o=dst, i=bslots[:, j * 128:(j + 1) * 128]:
                                     v.tensor_copy(o, i), waits=[trs[-1]])
                    evs.append(ea)
            if router:
                for j in range(4):
                    dc = half * 4 + j
                    dst32 = ln32[:, par2 * 1024 + dc * 128: par2 * 1024 + dc * 128 + 128]
                    if eng == "act":
                        ev = ph.emit("act", lambda a, o=dst32, i=bslots[:, j * 128:(j + 1) * 128]:
                                     a.activation(o, i, AF.Copy, bias=0.0, scale=1.0),
                                     waits=[trs[-1], prev_ln32.get(par2)])
                    else:
                        ev = ph.emit("dve", lambda v, o=dst32, i=bslots[:, j * 128:(j + 1) * 128]:
                                     v.tensor_copy(o, i),
                                     waits=[trs[-1], prev_ln32.get(par2)])
                    evs.append(ev)
            prev_bank[bank] = evs[-1]
            ev_last.extend(evs)
            last_tr = trs[-1]
        ev_trans[tt] = last_tr
        if router:
            pbank = pb[par2]
            wmm = ev_last + [prev_pbe.get(par2)]
            for dc in range(NDC):
                ph.emit("pe", lambda t, o=pbank[:, 0:16],
                        l=ln32[:, par2 * 1024 + dc * 128: par2 * 1024 + dc * 128 + 128],
                        r=L["wr_sb"][:, dc * NE:(dc + 1) * NE], s_=(dc == 0):
                        t.matmul(o, l, r, start=s_, stop=False),
                        waits=wmm if dc == 0 else (), sem=False)
            mm_ev = ph.emit("pe", lambda t, o=pbank[:, 0:16], l=L["ones32"][:1, :],
                            r=L["brr"][:1, :]: t.matmul(o, l, r, start=False, stop=True))
            prev_ln32[par2] = mm_ev
            uA = L["u16"][:, par2 * 16:(par2 + 1) * 16]
            uB = L["u16b"][:, par2 * 16:(par2 + 1) * 16]
            c = lambda i: stats[:, 32 + par2 * 8 + i: 32 + par2 * 8 + i + 1]
            e0 = ph.emit("dve", lambda v, o=c(0), i=pbank[:, 0:16]:
                         v.reduce_max(o, i, AX.X, negate=True),
                         waits=[mm_ev, prev_gate.get(par2)])
            e1b = ph.emit("act", lambda a, o=uA, i=pbank[:, 0:16], b=c(0):
                          a.activation(o, i, AF.Exp, bias=b, scale=1.0), waits=[e0])
            prev_pbe[par2] = e1b
            e2 = ph.emit("dve", lambda v, o=c(1), i=uA: v.reduce_sum(o, i, AX.X), waits=[e1b])
            e3 = ph.emit("dve", lambda v, o=c(2), i=c(1): v.reciprocal(o, i), waits=[e2])
            e4 = ph.emit("dve", lambda v, o=c(3), i=uA: v.reduce_max(o, i, AX.X), waits=[e1b])
            e5 = ph.emit("dve", lambda v, o=uB, i=uA, m=c(3): v.tensor_scalar(o, i, m, None, ALU.is_equal), waits=[e4])
            e6 = ph.emit("dve", lambda v, o=uB, i=uB: v.tensor_scalar(o, i, -1.0, 1.0, ALU.mult, ALU.add), waits=[e5])
            e7 = ph.emit("dve", lambda v, o=uB, i=uA, j=uB: v.tensor_mul(o, i, j), waits=[e6])
            e8 = ph.emit("dve", lambda v, o=c(4), i=uB: v.reduce_max(o, i, AX.X), waits=[e7])
            e9 = ph.emit("dve", lambda v, o=uB, i=uA, m=c(4): v.tensor_scalar(o, i, m, None, ALU.is_ge), waits=[e8])
            e10 = ph.emit("dve", lambda v, o=uB, i=uA, j=uB: v.tensor_mul(o, i, j), waits=[e9])
            gdst = L["gate"][:, tt * NE:(tt + 1) * NE]
            e11 = ph.emit("dve", lambda v, o=gdst, i=uB, r=c(2): v.tensor_scalar(o, i, r, None, ALU.mult), waits=[e10, e3])
            prev_gate[par2] = e11
            egt = ph.emit("pe", lambda t, o=pbank[0:16, 128:256], i=gdst, ii=idf:
                          t.transpose(o, i, ii), waits=[e11, prev_pbe[par2]])
            egte = ph.emit("dve", lambda v, o=L["gT"][0:16, tt * 128:(tt + 1) * 128],
                           i=pbank[0:16, 128:256]: v.tensor_copy(o, i), waits=[egt])
            prev_pbe[par2] = egte

    ab = {}
    for tt in range(NT):
        ab[tt] = stage_a(tt)
        if tt >= 1:
            stage_b(tt - 1, *ab.pop(tt - 1))
        if tt >= 2:
            stage_c(tt - 2)
    stage_b(NT - 1, *ab.pop(NT - 1))
    stage_c(NT - 2)
    stage_c(NT - 1)
    ph.flush(f"s{s}_ln{which}")
    stk.close()


def _ph_attn_a(P, L, s, feeder=None):
    nc = P.nc
    NG = 4
    HPG = NH // NG
    IW = HPG * HD
    es2 = ExitStack()
    sb = lambda name, shape, dt: es2.enter_context(nc.sbuf_tensor(name, shape, dt))
    wstage = [sb(f"aa_ws{i}_{s}", [128, 2048], BF16) for i in range(2)]
    wqT = sb(f"aa_wq_{s}", [128, NDC * IW], BF16)
    wkT = sb(f"aa_wk_{s}", [128, NDC * IW], BF16)
    wvT = sb(f"aa_wv_{s}", [128, NDC * IW], BF16)
    woT = sb(f"aa_wo_{s}", [128, 2 * 1024], BF16)
    qaT = sb(f"aa_q_{s}", [128, 2 * S], BF16)
    kaT = sb(f"aa_k_{s}", [128, 2 * S], BF16)
    vga = sb(f"aa_v_{s}", [128, NT * IW], BF16)
    outg = sb(f"aa_o_{s}", [128, 2 * S], BF16)
    ascr = sb(f"aa_s_{s}", [128, 4 * 192], F32)
    abf = sb(f"aa_ab_{s}", [128, 4 * 192], BF16)
    atT = sb(f"aa_at_{s}", [128, 2 * 256], BF16)
    xlt, resid, idb = L["xlt"], L["resid"], L["idb"][:, :]
    ps_big, pb, pt = L["ps_big"], L["pb"], L["pt"]
    Wsrcs = (L["Wq"], L["Wk"], L["Wv"])
    with es2:
        pre = {}
        for g in range(NG):
            ph = Phase(P)
            dA = "dA"

            def tr_batches(items, src_of, dst_of, prev_pt, wait_extra):
                done = []
                for b0 in range(0, len(items), 8):
                    chunk = items[b0:b0 + 8]
                    bank = pt[(b0 // 8) % 2]
                    trs = []
                    for j, itx in enumerate(chunk):
                        etr = ph.emit("pe", lambda t, o=bank[:, j * 128:(j + 1) * 128],
                                      i=src_of(itx), ii=idb: t.transpose(o, i, ii),
                                      waits=([wait_extra, prev_pt.get((b0 // 8) % 2)]
                                             if j == 0 else ()))
                        trs.append(etr)
                    for j, itx in enumerate(chunk):
                        ea = ph.emit("act", lambda a, o=dst_of(itx), i=bank[:, j * 128:(j + 1) * 128]:
                                     a.activation(o, i, AF.Copy, bias=0.0, scale=1.0),
                                     waits=[trs[-1]])
                        done.append(ea)
                    prev_pt[(b0 // 8) % 2] = done[-1]
                return done

            prev_pt = {0: P.snap("c_dve"), 1: P.snap("c_dve")}
            evq = pre.pop((g, 0), None) or ph.dma(
                "gp", dA, wstage[0][:, :].rearrange("p (it d) -> p it d", it=2),
                bass.AP(Wsrcs[0].tensor, (g * IW) * D, [[D, 128], [128 * D, 2], [1, D]]))
            evk = pre.pop((g, 1), None) or ph.dma(
                "gp", dA, wstage[1][:, :].rearrange("p (it d) -> p it d", it=2),
                bass.AP(Wsrcs[1].tensor, (g * IW) * D, [[D, 128], [128 * D, 2], [1, D]]))
            items = [(it, dc) for it in range(2) for dc in range(NDC)]
            evs_q = tr_batches(
                items, lambda x: wstage[0][:, x[0] * 1024 + x[1] * 128: x[0] * 1024 + x[1] * 128 + 128],
                lambda x: wqT[:, x[1] * IW + x[0] * 128: x[1] * IW + x[0] * 128 + 128],
                prev_pt, evq)
            evv = ph.dma("gp", dA, wstage[0][:, :].rearrange("p (it d) -> p it d", it=2),
                         bass.AP(Wsrcs[2].tensor, (g * IW) * D, [[D, 128], [128 * D, 2], [1, D]]),
                         waits=[evs_q[-1]])
            evs_k = tr_batches(
                items, lambda x: wstage[1][:, x[0] * 1024 + x[1] * 128: x[0] * 1024 + x[1] * 128 + 128],
                lambda x: wkT[:, x[1] * IW + x[0] * 128: x[1] * IW + x[0] * 128 + 128],
                prev_pt, evk)
            evo = ph.dma("gp", dA, wstage[1][:, :].rearrange("p (oc i) -> p oc i", oc=NDC),
                         bass.AP(L["Wo"].tensor, g * IW, [[D, 128], [128 * D, 8], [1, IW]]),
                         waits=[evs_k[-1]])
            tr_batches(
                items, lambda x: wstage[0][:, x[0] * 1024 + x[1] * 128: x[0] * 1024 + x[1] * 128 + 128],
                lambda x: wvT[:, x[1] * IW + x[0] * 128: x[1] * IW + x[0] * 128 + 128],
                prev_pt, evv)
            items_o = [(oc, it) for oc in range(NDC) for it in range(2)]
            tr_batches(
                items_o, lambda x: wstage[1][:, x[0] * 256 + x[1] * 128: x[0] * 256 + x[1] * 128 + 128],
                lambda x: woT[:, x[1] * 1024 + x[0] * 128: x[1] * 1024 + x[0] * 128 + 128],
                prev_pt, evo)
            ph.flush(f"s{s}_aa_g{g}_w")

            ph = Phase(P)
            pvt = {0: P.snap("c_dve"), 1: P.snap("c_dve")}
            for wi, (wT, dstT) in enumerate(((wqT, qaT), (wkT, kaT))):
                for it in range(2):
                    for ng in range(4):
                        key = (it * 4 + ng) % 2
                        pbank = pb[key]
                        emm = None
                        for dc in range(NDC):
                            emm = ph.emit("pe", lambda t, o=pbank[:, :],
                                          l=wT[:, dc * IW + it * 128: dc * IW + it * 128 + 128],
                                          r=xlt[:, dc * S + ng * 512: dc * S + (ng + 1) * 512],
                                          s_=(dc == 0), p_=(dc == NDC - 1):
                                          t.matmul(o, l, r, start=s_, stop=p_),
                                          waits=[pvt.get(key)] if dc == 0 else (),
                                          sem=False if dc < NDC - 1 else None)
                        dst = dstT[:, it * S + ng * 512: it * S + (ng + 1) * 512]
                        eev = ph.emit("act", lambda a, o=dst, i=pbank[:, :]:
                                      a.activation(o, i, AF.Copy, bias=0.0, scale=1.0), waits=[emm])
                        pvt[key] = eev
            for tt in range(NT):
                key = tt % 2
                pbank = pb[key]
                emm = None
                for dc in range(NDC):
                    emm = ph.emit("pe", lambda t, o=pbank[:, 0:IW],
                                  l=xlt[:, dc * S + tt * 128: dc * S + tt * 128 + 128],
                                  r=wvT[:, dc * IW:(dc + 1) * IW],
                                  s_=(dc == 0), p_=(dc == NDC - 1):
                                  t.matmul(o, l, r, start=s_, stop=p_),
                                  waits=[pvt.get(key)] if dc == 0 else (),
                                  sem=False if dc < NDC - 1 else None)
                dst = vga[:, tt * IW:(tt + 1) * IW]
                eev = ph.emit("act", lambda a, o=dst, i=pbank[:, 0:IW]:
                              a.activation(o, i, AF.Copy, bias=0.0, scale=1.0), waits=[emm])
                pvt[key] = eev
            ev_proj = P.snap("c_act")

            sc_free = {}
            prev_ab4 = {}
            prev_abf_rd = {}
            prev_at = {}
            prev_av = {0: pvt.get(0), 1: pvt.get(1)}
            stc = L["stats"]
            pending = []
            first = True

            def emit_tail(w, h, par4, e_ab, idx):
                par2 = idx % 2
                it = h // 2
                rb = 64 * (h % 2)
                absl = abf[:, par4 * 192:(par4 + 1) * 192]
                bank = pt[par2]
                e_t1 = ph.emit("pe", lambda t, o=bank[0:64, 0:128], i=absl[:, 0:64], ii=idb:
                               t.transpose(o, i, ii), waits=[e_ab, prev_at.get(par2)])
                e_t2 = ph.emit("pe", lambda t, o=bank[:, 128:256], i=absl[:, 64:192], ii=idb:
                               t.transpose(o, i, ii))
                prev_abf_rd[par4] = e_t2
                atsl = atT[:, par2 * 256:(par2 + 1) * 256]
                e_te = ph.emit("dve", lambda v, o=atsl, i=bank[:, 0:256]:
                               v.tensor_copy(o, i), waits=[e_t2])
                prev_at[par2] = e_te
                avp = pb[par2]
                ph.emit("pe", lambda t, o=avp[0:64, 0:128],
                        l=vga[0:64, h * 64: h * 64 + 64],
                        r=atsl[0:64, 0:128]:
                        t.matmul(o, l, r, start=True, stop=False),
                        waits=[e_te, prev_av.get(par2)], sem=False)
                e_a2 = ph.emit("pe", lambda t, o=avp[0:64, 0:128],
                               l=vga[:, w * IW + h * 64: w * IW + h * 64 + 64],
                               r=atsl[:, 128:256]:
                               t.matmul(o, l, r, start=False, stop=True))
                dst = outg[rb:rb + 64, it * S + w * 128: it * S + w * 128 + 128]
                e_oe = ph.emit("act", lambda a, o=dst, i=avp[0:64, 0:128]:
                               a.activation(o, i, AF.Copy, bias=0.0, scale=1.0), waits=[e_a2])
                prev_av[par2] = e_oe

            for idx, (w, h) in enumerate([(w, h) for w in range(NT) for h in range(HPG)]):
                par4 = idx % 4
                it = h // 2
                rb = 64 * (h % 2)
                qsl = qaT[rb:rb + 64, it * S + w * 128: it * S + w * 128 + 128]
                scp = ps_big[:, par4 * 512: par4 * 512 + 192]
                e_s1 = ph.emit("pe", lambda t, o=scp[:, 0:64], l=qsl,
                               r=kaT[rb:rb + 64, it * S: it * S + 64]:
                               t.matmul(o, l, r, start=True, stop=True),
                               waits=[sc_free.get(par4)] + ([ev_proj] if first else []))
                first = False
                e_s2 = ph.emit("pe", lambda t, o=scp[:, 64:192], l=qsl,
                               r=kaT[rb:rb + 64, it * S + w * 128: it * S + w * 128 + 128]:
                               t.matmul(o, l, r, start=True, stop=True))
                asl = ascr[:, par4 * 192:(par4 + 1) * 192]
                absl = abf[:, par4 * 192:(par4 + 1) * 192]
                c1 = stc[:, 48 + par4: 48 + par4 + 1]
                e_e = ph.emit("act", lambda a, o=asl, i=scp:
                              a.activation(o, i, AF.Exp, bias=0.0, scale=0.125),
                              waits=[e_s1, e_s2, prev_ab4.get(par4)])
                sc_free[par4] = e_e
                if len(pending) > 1:
                    emit_tail(*pending.pop(0))
                e_r = ph.emit("dve", lambda v, o=c1, i=asl: v.reduce_sum(o, i, AX.X), waits=[e_e])
                e_rc = ph.emit("dve", lambda v, o=c1, i=c1: v.reciprocal(o, i), waits=[e_r])
                e_ab = ph.emit("dve", lambda v, o=absl, i=asl, r=c1:
                               v.tensor_scalar(o, i, r, None, ALU.mult),
                               waits=[e_rc, prev_abf_rd.get(par4)])
                prev_ab4[par4] = e_ab
                pending.append((w, h, par4, e_ab, idx))
            while pending:
                emit_tail(*pending.pop(0))

            lastout = P.snap("c_act")
            pvt2 = dict(prev_av)
            for tt in range(NT):
                for oc in range(2):
                    key = (tt * 2 + oc) % 2
                    pbank = pb[key]
                    emm = None
                    for it in range(2):
                        emm = ph.emit("pe", lambda t, o=pbank[:, :],
                                      l=outg[:, it * S + tt * 128: it * S + tt * 128 + 128],
                                      r=woT[:, it * 1024 + oc * 512: it * 1024 + (oc + 1) * 512],
                                      s_=(it == 0), p_=(it == 1):
                                      t.matmul(o, l, r, start=s_, stop=p_),
                                      waits=[lastout, pvt2.get(key)] if it == 0 else (),
                                      sem=False if it < 1 else None)
                    dst = resid[:, tt * 1024 + oc * 512: tt * 1024 + (oc + 1) * 512]
                    eev = ph.emit("dve", lambda v, o=dst, i=dst, j=pbank[:, :]:
                                  v.tensor_add(o, i, j), waits=[emm])
                    pvt2[key] = eev
            if g + 1 < NG:
                pre[(g + 1, 0)] = ph.dma(
                    "gp", dA, wstage[0][:, :].rearrange("p (it d) -> p it d", it=2),
                    bass.AP(Wsrcs[0].tensor, ((g + 1) * IW) * D, [[D, 128], [128 * D, 2], [1, D]]))
                pre[(g + 1, 1)] = ph.dma(
                    "gp", dA, wstage[1][:, :].rearrange("p (it d) -> p it d", it=2),
                    bass.AP(Wsrcs[1].tensor, ((g + 1) * IW) * D, [[D, 128], [128 * D, 2], [1, D]]))
            ph.flush(f"s{s}_aa_g{g}_at")
            if feeder is not None:
                feeder.feed(8)


def _ph_ffn(P, L, s, which):
    nc = P.nc
    NBLK = 4
    BH = DH // NBLK
    NHC = BH // 128
    NTG = 4
    es2 = ExitStack()
    sb = lambda name, shape, dt: es2.enter_context(nc.sbuf_tensor(name, shape, dt))
    w1b = [sb(f"f{which}w1{i}_{s}", [128, NDC * BH], BF16) for i in range(2)]
    w2b = [sb(f"f{which}w2{i}_{s}", [128, NHC * D], BF16) for i in range(2)]
    hT = sb(f"f{which}h_{s}", [128, NHC * 512], BF16)
    xlt, resid, gate = L["xlt"], L["resid"], L["gate"]
    ps_big, pb = L["ps_big"], L["pb"]
    with es2:
        ph = Phase(P)
        dW = "dW"
        pv_hbank = {}
        pv_ybank = {}
        w_ld = {}

        def load_w(e, blk, par, waits):
            ph.dma("gp", dW, w1b[par][:, :].rearrange("p (dc hh) -> p dc hh", dc=NDC),
                   bass.AP(L["W1"].tensor, e * D * DH + blk * BH,
                           [[DH, 128], [128 * DH, NDC], [1, BH]]), waits=waits)
            return ph.dma("gp", dW, w2b[par][:, :].rearrange("p (hc d) -> p hc d", hc=NHC),
                          bass.AP(L["W2"].tensor, e * DH * D + blk * BH * D,
                                  [[D, 128], [128 * D, NHC], [1, D]]))

        seqi = [(e, blk) for e in range(NE) for blk in range(NBLK)]
        w_ld[0] = load_w(*seqi[0], 0, [])
        w_ld[1] = load_w(*seqi[1], 1, [])
        nload = 2
        for ib, (e, blk) in enumerate(seqi):
            par = ib % 2
            lastB = None
            for g in range(NTG):
                relu_evs = []
                for hc in range(NHC):
                    bank = hc % 4
                    psl = ps_big[:, bank * 512:(bank + 1) * 512]
                    emm = None
                    for dc in range(NDC):
                        w0 = [w_ld[par], pv_hbank.get(bank)] if dc == 0 else ()
                        emm = ph.emit("pe", lambda t, o=psl,
                                      l=w1b[par][:, dc * BH + hc * 128: dc * BH + hc * 128 + 128],
                                      r=xlt[:, dc * S + g * 512: dc * S + (g + 1) * 512],
                                      s_=(dc == 0), p_=(dc == NDC - 1):
                                      t.matmul(o, l, r, start=s_, stop=p_),
                                      waits=w0, sem=False if dc < NDC - 1 else None)
                    b1col = L["b1c"][:, e * 32 + blk * NHC + hc: e * 32 + blk * NHC + hc + 1]
                    erl = ph.emit("act", lambda a, o=hT[:, hc * 512:(hc + 1) * 512], i=psl, b=b1col:
                                  a.activation(o, i, AF.Relu, bias=b, scale=1.0), waits=[emm])
                    pv_hbank[bank] = erl
                    relu_evs.append(erl)
                for tt4 in range(4):
                    tt = g * 4 + tt4
                    for dc2 in range(2):
                        ypar = (tt4 * 2 + dc2) % 2
                        pbank = pb[ypar]
                        emm = None
                        for hc in range(NHC):
                            last = (hc == NHC - 1)
                            w0 = [relu_evs[hc], pv_ybank.get(ypar) if hc == 0 else None]
                            emm = ph.emit("pe", lambda t, o=pbank[:, :],
                                          l=hT[:, hc * 512 + tt4 * 128: hc * 512 + tt4 * 128 + 128],
                                          r=w2b[par][:, hc * D + dc2 * 512: hc * D + (dc2 + 1) * 512],
                                          s_=(hc == 0), p_=last:
                                          t.matmul(o, l, r, start=s_, stop=p_),
                                          waits=w0, sem=False if not last else None)
                        gcol = gate[:, tt * NE + e: tt * NE + e + 1]
                        dst = resid[:, tt * 1024 + dc2 * 512: tt * 1024 + (dc2 + 1) * 512]
                        est = ph.emit("dve", lambda v, o=dst, i=pbank[:, :], sc=gcol, j=dst:
                                      v.scalar_tensor_tensor(o, i, sc, j, ALU.mult, ALU.add),
                                      waits=[emm, pv_ybank.get(ypar)])
                        pv_ybank[ypar] = est
                        lastB = emm
            if nload < len(seqi):
                w_ld[par] = load_w(*seqi[nload], par, [lastB])
                nload += 1
        for tt in range(NT):
            for dc2 in range(2):
                ypar = (tt * 2 + dc2) % 2
                pbank = pb[ypar]
                emm = ph.emit("pe", lambda t, o=pbank[:, :],
                              l=L["gT"][0:16, tt * 128:(tt + 1) * 128],
                              r=L["b2nat"][0:16, dc2 * 512:(dc2 + 1) * 512]:
                              t.matmul(o, l, r, start=True, stop=True),
                              waits=[pv_ybank.get(ypar)])
                dst = resid[:, tt * 1024 + dc2 * 512: tt * 1024 + (dc2 + 1) * 512]
                est = ph.emit("dve", lambda v, o=dst, i=dst, j=pbank[:, :]:
                              v.tensor_add(o, i, j), waits=[emm, pv_ybank.get(ypar)])
                pv_ybank[ypar] = est
        ph.flush(f"s{s}_ffn{which}")


def _ph_ffn_sparse(P, L, which, n_seq, dep=()):
    nc = P.nc
    ntile = n_seq * NT
    stk = ExitStack()
    sb = lambda name, shape, dt: stk.enter_context(nc.sbuf_tensor(name, shape, dt))
    pref = f"fs{which}"
    sloti = sb(pref + "_sl", [128, 2 * ntile], I32)
    g12 = sb(pref + "_g12", [128, 2 * ntile], F32)
    b1all = sb(pref + "_b1a", [128, 32 * CAP_G], F32)
    w1i = sb(pref + "_w1i", [128, CAP_G], I32)
    w2i = sb(pref + "_w2i", [128, CAP_G], I32)
    stk2 = ExitStack()
    xtm = stk2.enter_context(nc.sbuf_tensor(pref + "_xtm", [128, ntile * D], BF16))
    for s in range(n_seq):
        _set_seq(L, s)
        _ph_ln(P, L, s, which, router=True, dep=(dep if s == 0 else ()),
               xtm=xtm, xtm_off=s * NT)
    _ph_meta(P, L, which, ntile, xtm, sloti, g12, b1all, w1i, w2i)
    stk2.close()
    w1src = bass.AP(L["w1bf"].tensor, 0, [[4096, NE * D], [1, 4096]])
    if which == 1:
        w2src = bass.AP(L["W2"].tensor, 0, [[D, NE * DH], [1, D]])
        w2_shuf = False
    else:
        w2src = bass.AP(L["w2bf"].tensor, 0, [[4096, NE * D], [1, 4096]])
        w2_shuf = True
    _ph_groups(P, L, which, b1all, w1i, w2i, w1src, w2src, w2_shuf)
    _ph_consume(P, L, which, ntile, sloti, g12)
    stk.close()


def _ph_meta(P, L, which, ntile, xtm, sloti, g12, b1all, w1i, w2i):
    nc = P.nc
    stk = ExitStack()
    sb = lambda name, shape, dt: stk.enter_context(nc.sbuf_tensor(name, shape, dt))
    pref = f"fm{which}"
    ohb = sb(pref + "_oh", [128, ntile * 16], F32)
    mcols = sb(pref + "_mc", [16, 16], F32)
    Mge = sb(pref + "_mge", [16, CAP_G], F32)
    ebc16 = sb(pref + "_eb16", [16, CAP_G], F32)
    eoh = sb(pref + "_eoh", [16, CAP_G], F32)
    ebc128 = sb(pref + "_eb128", [128, CAP_G], F32)
    wif = sb(pref + "_wif", [128, 2 * CAP_G], F32)
    eogr = sb(pref + "_eog", [1, CAP_G], F32)
    tcsb = sb(pref + "_tc", [16, 32], F32)
    tctsb = sb(pref + "_tct", [32, 16], F32)
    cbsb = sb(pref + "_cb", [32, 16], F32)
    brow = sb(pref + "_br", [1, 16], F32)
    cbrow = sb(pref + "_cbr", [1, 512], F32)
    Ssb = sb(pref + "_S", [128, 2 * 16], F32)
    mscr = sb(pref + "_ms", [128, 40], F32)
    b1nat = sb(pref + "_b1n", [NE, DH], F32)
    ph = Phase(P)
    ev_b1n = ph.dma("sp", "dX", b1nat[:, :], L["b1_in"])
    pb0, pb1, ps_big = L["pb"][0], L["pb"][1], L["ps_big"]
    io16, io16r = L["io16"], L["io16r"]
    onesr = L["ones32"]
    d_pe, d_dve, d_act = P.snap("c_pe"), P.snap("c_dve"), P.snap("c_act")
    d_dw = P.snap("dW")

    _last = [None]

    def dv(fn, waits=()):
        e = ph.emit("dve", fn, waits=list(waits) + [_last[0]])
        _last[0] = e
        return e

    for T in range(ntile):
        gate_s = L[f"gate{T // NT}"]
        tt = T % NT
        dv(lambda v, T=T, g=gate_s, tt=tt: v.tensor_scalar(
            ohb[:, T * 16:(T + 1) * 16], g[:, tt * NE:(tt + 1) * NE], 0.0, None, ALU.is_gt),
           waits=[d_act, d_pe] if T == 0 else ())
    ev_oh = _last[0]
    ev_tc = None
    for T in range(ntile):
        ev_tc = ph.emit("pe", lambda t, T=T: t.matmul(
            pb0[0:16, 64 + T:65 + T], ohb[:, T * 16:(T + 1) * 16], L["ones128c"][:, :],
            start=True, stop=True), waits=[ev_oh, d_act, d_dve] if T == 0 else ())
    ev_tce = ph.emit("act", lambda a: a.activation(tcsb[:, 0:ntile], pb0[0:16, 64:64 + ntile],
                                                   AF.Copy, bias=0.0, scale=1.0), waits=[ev_tc])
    dv(lambda v: v.reduce_sum(mcols[:, 0:1], tcsb[:, 0:ntile], AX.X), waits=[ev_tce])
    dv(lambda v: v.tensor_scalar(
        mcols[:, 2:2 + THRN], L["thr16"][:, :], mcols[:, 0:1], None, ALU.is_lt))
    dv(lambda v: v.reduce_sum(mcols[:, 10:11], mcols[:, 2:2 + THRN], AX.X))
    ev_pc = dv(lambda v: v.tensor_scalar(
        mcols[:, 11:12], mcols[:, 10:11], L["c512"][:, :], None, ALU.mult))
    ev_bm = ph.emit("pe", lambda t: t.matmul(pb0[0:16, 0:1], L["su16"][:, :], mcols[:, 11:12],
                                             start=True, stop=True),
                    waits=[ev_pc, ev_tce])
    ev_bc = ph.emit("act", lambda a: a.activation(mcols[:, 12:13], pb0[0:16, 0:1], AF.Copy,
                                                  bias=0.0, scale=1.0), waits=[ev_bm])
    ev_brm = ph.emit("pe", lambda t: t.transpose(pb0[0:1, 16:32], mcols[:, 12:13],
                                                 L["idf"][0:16, 0:16]),
                     waits=[ev_bc])
    ev_br = dv(lambda v: v.tensor_copy(brow[0:1, 0:16], pb0[0:1, 16:32]), waits=[ev_brm])
    ev_mg = dv(lambda v: v.tensor_scalar(Mge[:, :], L["g512"][:, :], mcols[0:16, 12:13],
                                         None, ALU.is_ge), waits=[ev_bc])
    ev_em = ph.emit("pe", lambda t: t.matmul(pb1[0:1, 0:CAP_G], L["ones16c"][:, :], Mge[:, :],
                                             start=True, stop=True), waits=[ev_mg, d_act, d_dve])
    ev_eog = ph.emit("act", lambda a: a.activation(eogr[0:1, :], pb1[0:1, 0:CAP_G], AF.Copy,
                                                   bias=-1.0, scale=1.0), waits=[ev_em])
    ev_b16m = ph.emit("pe", lambda t: t.matmul(pb0[0:16, 32:32 + CAP_G], onesr[0:1, 0:16],
                                               eogr[0:1, :], start=True, stop=True),
                      waits=[ev_eog])
    ev_b16 = dv(lambda v: v.tensor_copy(ebc16[:, :], pb0[0:16, 32:32 + CAP_G]),
                waits=[ev_b16m])
    ev_eoh = dv(lambda v: v.tensor_scalar(eoh[:, :], ebc16[:, :], L["iota16"][:, :],
                                          None, ALU.is_equal))
    ev_b128m = ph.emit("pe", lambda t: t.matmul(pb1[:, 32:32 + CAP_G], onesr[0:1, :],
                                                eogr[0:1, :], start=True, stop=True),
                       waits=[ev_eog])
    ev_b128 = dv(lambda v: v.tensor_copy(ebc128[:, :], pb1[:, 32:32 + CAP_G]),
                 waits=[ev_b128m])
    dv(lambda v: v.tensor_scalar(wif[:, 0:CAP_G], ebc128[:, :], L["c1024"][:, :],
                                 L["iota128"][:, :], ALU.mult, ALU.add))
    dv(lambda v: v.tensor_scalar(wif[:, CAP_G:2 * CAP_G], ebc128[:, :], L["c4096"][:, :],
                                 L["iota128"][:, :], ALU.mult, ALU.add))
    dv(lambda v: v.tensor_copy(w1i[:, :], wif[:, 0:CAP_G]))
    ev_wi = dv(lambda v: v.tensor_copy(w2i[:, :], wif[:, CAP_G:2 * CAP_G]))
    ev_b1mm = None
    for hc in range(32):
        ev_b1mm = ph.emit("pe", lambda t, hc=hc: t.matmul(
            ps_big[:, hc * CAP_G:(hc + 1) * CAP_G], b1nat[0:16, hc * 128:(hc + 1) * 128],
            eoh[:, :], start=True, stop=True),
            waits=[ev_eoh, ev_b1n, d_act, d_dve] if hc == 0 else ())
    ev_b1e = ph.emit("act", lambda a: a.activation(
        b1all[:, :], ps_big[:, 0:32 * CAP_G], AF.Copy, bias=0.0, scale=1.0),
        waits=[ev_b1mm])
    ev_tct = ph.emit("pe", lambda t: t.transpose(pb1[0:ntile, 64:80], tcsb[0:16, 0:ntile],
                                                 L["idf"][0:16, 0:16]),
                     waits=[ev_tce, ev_b128])
    ev_tcte = dv(lambda v, n=ntile: v.tensor_copy(tctsb[0:n, :], pb1[0:n, 64:80]), waits=[ev_tct])
    ph.emit("pe", lambda t, n=ntile: t.matmul(pb1[0:n, 80:96], L["su32"][0:n, 0:n], tctsb[0:n, :],
                                              start=True, stop=False), waits=[ev_tcte], sem=False)
    ev_cum = ph.emit("pe", lambda t, n=ntile: t.matmul(pb1[0:n, 80:96], onesr[0:1, 0:n],
                                                       brow[0:1, 0:16], start=False, stop=True),
                     waits=[ev_br])
    ev_cbe = ph.emit("act", lambda a, n=ntile: a.activation(cbsb[0:n, :], pb1[0:n, 80:96], AF.Copy,
                                                            bias=0.0, scale=1.0), waits=[ev_cum])
    ev_d1 = ph.dma("sp", "dX", L["meta_d"], cbsb[:, :], waits=[ev_cbe])
    ev_d2 = ph.dma("sp", "dX", cbrow[0:1, :],
                   bass.AP(L["meta_d"].tensor, 0, [[512, 1], [1, 512]]), waits=[ev_d1])
    ev_Sread = {}
    ev_Se = {}
    first_sc = [d_dw]
    A, B = mscr[:, 0:16], mscr[:, 16:32]
    ehi, tmp = mscr[:, 32:33], mscr[:, 33:34]
    sc1, sc2 = mscr[:, 35:36], mscr[:, 36:37]
    for T in range(ntile):
        sl = T % 2
        gate_s = L[f"gate{T // NT}"]
        tt = T % NT
        ph.emit("pe", lambda t, T=T, sl=sl: t.matmul(
            pb1[:, 96 + sl * 16:112 + sl * 16], L["su128"][:, :], ohb[:, T * 16:(T + 1) * 16],
            start=True, stop=False),
            waits=[ev_Sread.get(T - 2), ev_Se.get(T - 1),
                   ev_d2 if T == 0 else None], sem=False)
        ev_Sm = ph.emit("pe", lambda t, T=T, sl=sl: t.matmul(
            pb1[:, 96 + sl * 16:112 + sl * 16], onesr[0:1, :], cbrow[0:1, T * 16:T * 16 + 16],
            start=False, stop=True))
        ev_Se[T] = ph.emit("act", lambda a, sl=sl: a.activation(
            Ssb[:, sl * 16:(sl + 1) * 16], pb1[:, 96 + sl * 16:112 + sl * 16], AF.Copy,
            bias=0.0, scale=1.0), waits=[ev_Sm])
        oh_t = ohb[:, T * 16:(T + 1) * 16]
        G_t = gate_s[:, tt * NE:(tt + 1) * NE]
        S_sl = Ssb[:, sl * 16:(sl + 1) * 16]
        dv(lambda v, o=oh_t: v.tensor_mul(A, o, io16[:, :]))
        dv(lambda v: v.reduce_max(ehi, A, AX.X))
        dv(lambda v, o=oh_t: v.tensor_mul(A, o, io16r[:, :]))
        dv(lambda v: v.reduce_max(tmp, A, AX.X))
        dv(lambda v: v.tensor_scalar(A, io16r[:, :], tmp, None, ALU.is_equal))
        dv(lambda v, g=G_t: v.tensor_mul(B, g, A))
        dv(lambda v, T=T: v.reduce_sum(g12[:, 2 * T:2 * T + 1], B, AX.X))
        dv(lambda v, ss=S_sl: v.tensor_mul(B, ss, A), waits=[ev_Se[T]])
        dv(lambda v: v.reduce_sum(sc1, B, AX.X))
        ev_i1 = dv(lambda v, T=T: v.tensor_copy(sloti[:, 2 * T:2 * T + 1], sc1))
        dv(lambda v: v.tensor_scalar(A, io16[:, :], ehi, None, ALU.is_equal))
        dv(lambda v, g=G_t: v.tensor_mul(B, g, A))
        dv(lambda v, T=T: v.reduce_sum(g12[:, 2 * T + 1:2 * T + 2], B, AX.X))
        dv(lambda v, ss=S_sl: v.tensor_mul(B, ss, A))
        dv(lambda v: v.reduce_sum(sc2, B, AX.X))
        ev_i2 = dv(lambda v, T=T: v.tensor_copy(sloti[:, 2 * T + 1:2 * T + 2], sc2))
        ev_Sread[T] = ev_i2
        ph.idma_scatter("dW", L["xg_d"], sloti[:, 2 * T:2 * T + 1],
                        xtm[:, T * D:(T + 1) * D],
                        waits=first_sc + [ev_i1, d_act])
        first_sc = []
        ph.idma_scatter("dW", L["xg_d"], sloti[:, 2 * T + 1:2 * T + 2],
                        xtm[:, T * D:(T + 1) * D], waits=[ev_i2])
    if DEBUG and which == 1:
        evl = P.snap("c_dve")
        ph.dma("sp", "dX", L["dbg_sl"], sloti[:, :], waits=[evl])
        ph.dma("sp", "dX", L["dbg_g12"], g12[:, :], waits=[evl])
        ph.dma("sp", "dX", L["dbg_cb"], cbsb[:, :], waits=[ev_cbe])
        ph.dma("sp", "dX", L["dbg_eog"], eogr[0:1, :], waits=[ev_eog])
        ph.dma("sp", "dX", L["dbg_mc"], mcols[:, :], waits=[evl])
        ph.dma("sp", "dX", L["dbg_b1"], b1all[:, :], waits=[ev_b1e])
        evs = P.snap("dW")
        ph.dma("gp", "dW", L["dbg_xg"].rearrange("(g p) d -> p g d", p=128),
               bass.AP(L["xg_d"].tensor, 0, [[D, 128], [128 * D, CAP_S // 128], [1, D]]),
               waits=[evs])
    ph.flush(f"meta{which}")
    stk.close()


def _ph_groups(P, L, which, b1all, w1i, w2i, w1src, w2src, w2_shuf):
    nc = P.nc
    stk = ExitStack()
    sb = lambda name, shape, dt: stk.enter_context(nc.sbuf_tensor(name, shape, dt))
    pref = f"fg{which}"
    NWB = 3
    w1g = [sb(pref + f"_w1{i}", [128, NDC * BLK], BF16) for i in range(NWB)]
    w2g = [sb(pref + f"_w2{i}", [128, HCL * D], BF16) for i in range(NWB)]
    xg_tm = [sb(pref + "_xt0", [128, 4 * D], BF16)] * 2
    xgT = sb(pref + "_xT", [128, NDC * 512], BF16)
    hT = sb(pref + "_hT", [128, HCL * 512], BF16)
    y_sb = [sb(pref + "_y0", [128, 4 * D], F32)] * 2
    ph = Phase(P)
    ps_big, pb, pt = L["ps_big"], L["pb"], L["pt"]
    idb = L["idb"][:, :]
    d_pe, d_dve, d_act = P.snap("c_pe"), P.snap("c_dve"), P.snap("c_act")
    ev_scat = P.snap("dW")
    ev_dx = P.snap("dX")
    ev_xload = {}
    ev_w = {}
    ev_wcons = {}
    ev_yd = {}
    ev_tr = {}
    pv_h = {0: d_act, 1: d_act}
    pv_y = {q: d_dve for q in range(4)}
    pv_pt = {0: d_dve, 1: d_dve}

    def gather(gi):
        g, blk = gi // NBLK2, gi % NBLK2
        par = gi % NWB
        w0 = [ev_scat, ev_dx, ev_wcons.get(gi - NWB)]
        ph.idma_gather("dW", w1g[par][:, :], w1src, w1i[:, g:g + 1],
                       eo=blk * 524288, waits=w0)
        if w2_shuf:
            ev = ph.idma_gather("dW", w2g[par][:, :], w2src, w1i[:, g:g + 1],
                                eo=blk * 524288)
        else:
            for hcl in range(HCL):
                ev = ph.idma_gather("dW", w2g[par][:, hcl * D:(hcl + 1) * D], w2src,
                                    w2i[:, g:g + 1], eo=(blk * BLK + hcl * 128) * D)
        ev_w[gi] = ev

    for gi0 in range(3):
        gather(gi0)
    ngath = 3
    for g in range(CAP_G):
        ev_xload[g] = ph.dma(
            "sp", "dX", xg_tm[g % 2][:, :].rearrange("p (st d) -> p st d", st=4),
            bass.AP(L["xg_d"].tensor, 512 * g * D, [[D, 128], [128 * D, 4], [1, D]]),
            waits=[ev_scat, ev_tr.get(g - 1)])
        tr_done = []
        for st in range(4):
            for dc8 in range(0, NDC, 4):
                par2 = (st * 2 + dc8 // 4) % 2
                bank = pt[par2]
                trs = []
                for j in range(4):
                    dc = dc8 + j
                    etr = ph.emit("pe", lambda t, o=bank[:, j * 128:(j + 1) * 128],
                                  i=xg_tm[g % 2][:, st * D + dc * 128: st * D + dc * 128 + 128],
                                  ii=idb: t.transpose(o, i, ii),
                                  waits=([ev_xload[g], pv_pt[par2], d_pe] if j == 0 else ()))
                    trs.append(etr)
                for j in range(4):
                    dc = dc8 + j
                    ea = ph.emit("dve", lambda v, o=xgT[:, dc * 512 + st * 128: dc * 512 + st * 128 + 128],
                                 i=bank[:, j * 128:(j + 1) * 128]: v.tensor_copy(o, i),
                                 waits=[trs[-1]])
                pv_pt[par2] = ea
                tr_done.append(ea)
        ev_tr[g] = trs[-1]
        ev_ylast = None
        est_prev = {}
        for blk in range(NBLK2):
            gi = g * NBLK2 + blk
            par = gi % NWB
            relu_evs = []
            for hcl in range(HCL):
                hb = pb[hcl % 2]
                w0 = [ev_w[gi], pv_h[hcl % 2], tr_done[-1] if hcl == 0 else None]
                emm = None
                for dc in range(NDC):
                    emm = ph.emit("pe", lambda t, o=hb[:, :],
                                  l=w1g[par][:, dc * BLK + hcl * 128: dc * BLK + hcl * 128 + 128],
                                  r=xgT[:, dc * 512:(dc + 1) * 512],
                                  s_=(dc == 0), p_=(dc == NDC - 1):
                                  t.matmul(o, l, r, start=s_, stop=p_),
                                  waits=w0 if dc == 0 else (),
                                  sem=False if dc < NDC - 1 else None)
                hc = blk * HCL + hcl
                erl = ph.emit("act", lambda a, o=hT[:, hcl * 512:(hcl + 1) * 512], i=hb[:, :],
                              b=b1all[:, hc * CAP_G + g: hc * CAP_G + g + 1]:
                              a.activation(o, i, AF.Relu, bias=b, scale=1.0), waits=[emm])
                pv_h[hcl % 2] = erl
                relu_evs.append(erl)
            for st in range(4):
                for dh in range(2):
                    q = (st * 2 + dh) % 4
                    yb = ps_big[:, q * 512:(q + 1) * 512]
                    w0 = [relu_evs[-1], pv_y[q]]
                    emm = None
                    for hcl in range(HCL):
                        emm = ph.emit("pe", lambda t, o=yb,
                                      l=hT[:, hcl * 512 + st * 128: hcl * 512 + st * 128 + 128],
                                      r=w2g[par][:, hcl * D + dh * 512: hcl * D + (dh + 1) * 512],
                                      s_=(hcl == 0), p_=(hcl == HCL - 1):
                                      t.matmul(o, l, r, start=s_, stop=p_),
                                      waits=w0 if hcl == 0 else (),
                                      sem=False if hcl < HCL - 1 else None)
                    ev_ylast = emm
                    dst = y_sb[g % 2][:, st * D + dh * 512: st * D + (dh + 1) * 512]
                    if blk == 0:
                        est = ph.emit("dve", lambda v, o=dst, i=yb: v.tensor_copy(o, i),
                                      waits=[emm, ev_yd.get(g - 1)])
                    else:
                        est = ph.emit("dve", lambda v, o=dst, i=yb: v.tensor_add(o, o, i),
                                      waits=[emm, est_prev.get((st, dh))])
                    est_prev[(st, dh)] = est
                    pv_y[q] = est
            ev_wcons[gi] = ev_ylast
            if ngath < CAP_G * NBLK2:
                gather(ngath)
                ngath += 1
        ev_yd[g] = ph.dma(
            "gp", "dW",
            bass.AP(L["y_d"].tensor, 512 * g * D, [[D, 128], [128 * D, 4], [1, D]]),
            y_sb[g % 2][:, :].rearrange("p (st d) -> p st d", st=4),
            waits=[pv_y[0], pv_y[1], pv_y[2], pv_y[3]])
    if DEBUG and which == 1:
        ph.dma("gp", "dW", L["dbg_y"].rearrange("(g p) d -> p g d", p=128),
               bass.AP(L["y_d"].tensor, 0, [[D, 128], [128 * D, CAP_S // 128], [1, D]]),
               waits=[ev_yd[CAP_G - 1]])
    ph.flush(f"grp{which}")
    stk.close()


def _ph_consume(P, L, which, ntile, sloti, g12):
    nc = P.nc
    stk = ExitStack()
    sb = lambda name, shape, dt: stk.enter_context(nc.sbuf_tensor(name, shape, dt))
    pref = f"fc{which}"
    yg = sb(pref + "_yg", [128, 4 * D], BF16)
    ygf = sb(pref + "_yf", [128, 4 * D], F32)
    b2nat = sb(pref + "_b2n", [NE, D], F32)
    ph = Phase(P)
    pb = L["pb"]
    ev_b2n = ph.dma("sp", "dX", b2nat[:, :], L["b2_in"])
    ev_ydall = P.snap("dW")
    d_act, d_dve = P.snap("c_act"), P.snap("c_dve")
    ysrc = bass.AP(L["y_d"].tensor, 0, [[D, CAP_S], [1, D]])
    ev_cast = {}
    pv_stt = {}
    pv_b2 = {0: P.snap("c_act"), 1: P.snap("c_act")}
    L["ffn_done"] = {}
    for T in range(ntile):
        par = T % 2
        resid = L[f"resid{T // NT}"]
        tt = T % NT
        evg = []
        for k in range(2):
            evg.append(ph.idma_gather(
                "dW", yg[:, (par * 2 + k) * D:(par * 2 + k + 1) * D], ysrc,
                sloti[:, 2 * T + k:2 * T + k + 1],
                waits=[ev_ydall, ev_cast.get(T - 2)] if k == 0 else ()))
        ev_cast[T] = ph.emit("act", lambda a, o=ygf[:, par * 2 * D:(par * 2 + 2) * D],
                             i=yg[:, par * 2 * D:(par * 2 + 2) * D]:
                             a.activation(o, i, AF.Copy, bias=0.0, scale=1.0),
                             waits=[evg[-1], d_act if T == 0 else None,
                                    pv_stt.get(T - 2)])
        e_sl = {}
        for k in range(2):
            for dh in range(2):
                e_sl[(k, dh)] = ph.emit(
                    "dve",
                    lambda v, o=resid[:, tt * D + dh * 512: tt * D + (dh + 1) * 512],
                    i=ygf[:, (par * 2 + k) * D + dh * 512:(par * 2 + k) * D + (dh + 1) * 512],
                    sc=g12[:, 2 * T + k:2 * T + k + 1]:
                    v.scalar_tensor_tensor(o, i, sc, o, ALU.mult, ALU.add),
                    waits=[ev_cast[T], e_sl.get((k - 1, dh)), pv_stt.get(T - 1)])
        pv_stt[T] = e_sl[(1, 1)]
        gTs = L[f"gT{T // NT}"]
        for dc2 in range(2):
            ypar = (T * 2 + dc2) % 2
            pbank = pb[ypar]
            emm = ph.emit("pe", lambda t, o=pbank[:, :],
                          l=gTs[0:16, tt * 128:(tt + 1) * 128],
                          r=b2nat[0:16, dc2 * 512:(dc2 + 1) * 512]:
                          t.matmul(o, l, r, start=True, stop=True),
                          waits=[pv_b2[ypar], d_dve, ev_b2n if T == 0 else None])
            dst = resid[:, tt * 1024 + dc2 * 512: tt * 1024 + (dc2 + 1) * 512]
            est = ph.emit("dve", lambda v, o=dst, i=dst, j=pbank[:, :]:
                          v.tensor_add(o, i, j), waits=[emm, e_sl[(1, dc2)]])
            pv_b2[ypar] = est
        if T % NT == NT - 1:
            L["ffn_done"][T // NT] = pv_b2[(T * 2 + 1) % 2]
    ph.flush(f"cons{which}")
    stk.close()


def _ph_attn_b(P, L, s, feeder=None):
    nc = P.nc
    NG = 4
    HPG = NH // NG
    IW = HPG * HD
    NIT = IW // 128
    es2 = ExitStack()
    sb = lambda name, shape, dt: es2.enter_context(nc.sbuf_tensor(name, shape, dt))
    wbuf = [sb(f"ab_w{i}_{s}", [128, NDC * IW], BF16) for i in range(2)]
    woutb = sb(f"ab_wo_{s}", [128, NIT * D], BF16)
    qbT = sb(f"ab_q_{s}", [128, NIT * S], BF16)
    kbT = sb(f"ab_k_{s}", [128, NIT * S], BF16)
    vb = sb(f"ab_v_{s}", [128, NT * IW], BF16)
    outg = sb(f"ab_o_{s}", [128, NIT * S], BF16)
    bscr = sb(f"ab_s_{s}", [128, 2 * S], F32)
    abf = sb(f"ab_ab_{s}", [128, 2 * S], BF16)
    atb = sb(f"ab_at_{s}", [128, 2 * 1024], BF16)
    xlt, resid, idb, mask01 = L["xlt"], L["resid"], L["idb"][:, :], L["mask01"]
    ps_big, pb, pt = L["ps_big"], L["pb"], L["pt"]
    stats = L["stats"]
    with es2:
        for g in range(NG):
            ph = Phase(P)
            dA = "dA"
            def load_qkv(off, par, waits):
                evs = []
                for dc in range(NDC):
                    evs.append(ph.dma(
                        "gp", dA,
                        wbuf[par][:, dc * IW:(dc + 1) * IW].rearrange("p (hh e) -> p hh e", hh=HPG),
                        bass.AP(L["Wqkv"].tensor, g * HPG * D * 192 + dc * 128 * 192 + off,
                                [[192, 128], [D * 192, HPG], [1, 64]]),
                        waits=waits if dc == 0 else ()))
                return evs[-1]

            ev_q = load_qkv(0, 0, [])
            ev_k = load_qkv(64, 1, [])
            ewo = ph.dma("gp", dA, woutb[:, :].rearrange("p (it d) -> p it d", it=NIT),
                         bass.AP(L["Wout"].tensor, g * IW * D, [[D, 128], [128 * D, NIT], [1, D]]))
            pvt = {0: P.snap("c_dve"), 1: P.snap("c_dve")}
            qdone = None
            for wi, dstT, wev in ((0, qbT, ev_q), (1, kbT, ev_k)):
                bcol = L["bqc"] if wi == 0 else L["bkc"]
                for it in range(NIT):
                    for ng in range(4):
                        key = (it * 4 + ng) % 2
                        pbank = pb[key]
                        emm = None
                        for dc in range(NDC):
                            w0 = [wev, pvt.get(key)] if dc == 0 else ()
                            emm = ph.emit("pe", lambda t, o=pbank[:, :],
                                          l=wbuf[wi][:, dc * IW + it * 128: dc * IW + it * 128 + 128],
                                          r=xlt[:, dc * S + ng * 512: dc * S + (ng + 1) * 512],
                                          s_=(dc == 0), p_=(dc == NDC - 1):
                                          t.matmul(o, l, r, start=s_, stop=p_),
                                          waits=w0, sem=False if dc < NDC - 1 else None)
                        dst = dstT[:, it * S + ng * 512: it * S + (ng + 1) * 512]
                        bc = bcol[:, g * NIT + it: g * NIT + it + 1]
                        eev = ph.emit("act", lambda a, o=dst, i=pbank[:, :], b=bc:
                                      a.activation(o, i, AF.Identity, bias=b, scale=1.0), waits=[emm])
                        pvt[key] = eev
                if wi == 0:
                    qdone = emm
            ev_v = load_qkv(128, 0, [qdone])
            for tt in range(NT):
                key = tt % 2
                pbank = pb[key]
                emm = None
                for dc in range(NDC):
                    w0 = [ev_v, pvt.get(key)] if dc == 0 else ()
                    emm = ph.emit("pe", lambda t, o=pbank[:, 0:IW],
                                  l=xlt[:, dc * S + tt * 128: dc * S + tt * 128 + 128],
                                  r=wbuf[0][:, dc * IW:(dc + 1) * IW],
                                  s_=(dc == 0), p_=False:
                                  t.matmul(o, l, r, start=s_, stop=False),
                                  waits=w0, sem=False)
                emm = ph.emit("pe", lambda t, o=pbank[:, 0:IW], l=L["ones32"][:1, :],
                              r=L["bvr"][:1, g * IW:(g + 1) * IW]:
                              t.matmul(o, l, r, start=False, stop=True))
                dst = vb[:, tt * IW:(tt + 1) * IW]
                eev = ph.emit("act", lambda a, o=dst, i=pbank[:, 0:IW]:
                              a.activation(o, i, AF.Copy, bias=0.0, scale=1.0), waits=[emm])
                pvt[key] = eev
            ph.flush(f"s{s}_ab_g{g}_w")

            ph = Phase(P)
            ev_proj = P.snap("c_act")
            half_ev = {}
            prev_bs = {}
            prev_abf_rd = {}
            prev_at = {0: P.snap("c_dve"), 1: P.snap("c_dve")}
            prev_avp = {}
            cidx = 0
            first = True
            pending = []
            tiles = [(h, qt) for h in range(HPG) for qt in range(NT)]

            def emit_tail(h, qt, par, e_ab, idx):
                nk = qt + 1
                it = h // 2
                rb = 64 * (h % 2)
                avp = pb[par]
                e_av = None
                nbt = (nk + 7) // 8
                te_evs = []
                for bt in range(nbt):
                    k0 = bt * 8
                    k1 = min(nk, k0 + 8)
                    bank = pt[(idx + bt) % 2]
                    trs = []
                    for kt in range(k0, k1):
                        ev = ph.emit("pe", lambda t, o=bank[:, (kt - k0) * 128:(kt - k0 + 1) * 128],
                                     i=abf[:, par * S + kt * 128: par * S + (kt + 1) * 128], ii=idb:
                                     t.transpose(o, i, ii),
                                     waits=[e_ab, prev_at.get((idx + bt) % 2)] if kt == k0 else ())
                        trs.append(ev)
                    prev_abf_rd[par] = trs[-1]
                    abase = ((idx + bt) % 2) * 1024
                    e_te = ph.emit("dve", lambda v, o=atb[:, abase: abase + (k1 - k0) * 128],
                                   i=bank[:, 0:(k1 - k0) * 128]: v.tensor_copy(o, i), waits=[trs[-1]])
                    prev_at[(idx + bt) % 2] = e_te
                    te_evs.append(e_te)
                for bt in range(nbt):
                    k0 = bt * 8
                    k1 = min(nk, k0 + 8)
                    abase = ((idx + bt) % 2) * 1024
                    for kt in range(k0, k1):
                        last = (kt == nk - 1)
                        ev = ph.emit("pe", lambda t, o=avp[0:64, 0:128],
                                     l=vb[:, kt * IW + h * 64: kt * IW + h * 64 + 64],
                                     r=atb[:, abase + (kt - k0) * 128: abase + (kt - k0 + 1) * 128],
                                     s_=(kt == 0), p_=last:
                                     t.matmul(o, l, r, start=s_, stop=p_),
                                     waits=[te_evs[bt], prev_avp.get(par)] if kt == k0 else (),
                                     sem=False if not last else None)
                        if last:
                            e_av = ev
                dst = outg[rb:rb + 64, it * S + qt * 128: it * S + qt * 128 + 128]
                e_oe = ph.emit("act", lambda a, o=dst, i=avp[0:64, 0:128]:
                               a.activation(o, i, AF.Copy, bias=0.0, scale=1.0), waits=[e_av])
                prev_avp[par] = e_oe

            for idx, (h, qt) in enumerate(tiles):
                par = idx % 2
                nk = qt + 1
                ncols = nk * 128
                it = h // 2
                rb = 64 * (h % 2)
                qsl = qbT[rb:rb + 64, it * S + qt * 128: it * S + qt * 128 + 128]
                e_e = None
                for c0 in range(0, ncols, 1024):
                    c1 = min(ncols, c0 + 1024)
                    half = cidx % 2
                    base = half * 1024
                    sc_evs = []
                    firstmm = True
                    for cc0 in range(c0, c1, 512):
                        cc1 = min(c1, cc0 + 512)
                        w0 = []
                        if firstmm:
                            w0 = [half_ev.get(half)] + ([ev_proj] if first else [])
                            first = False
                            firstmm = False
                        ev = ph.emit("pe", lambda t, o=ps_big[:, base + cc0 - c0: base + cc1 - c0],
                                     l=qsl, r=kbT[rb:rb + 64, it * S + cc0: it * S + cc1]:
                                     t.matmul(o, l, r, start=True, stop=True), waits=w0)
                        sc_evs.append(ev)
                    we = sc_evs + ([prev_bs.get(par)] if c0 == 0 else [])
                    e_e = ph.emit("act", lambda a, o=bscr[:, par * S + c0: par * S + c1],
                                  i=ps_big[:, base: base + c1 - c0]:
                                  a.activation(o, i, AF.Exp, bias=0.0, scale=0.125), waits=we)
                    half_ev[half] = e_e
                    cidx += 1
                if pending:
                    emit_tail(*pending.pop(0))
                bsl = bscr[:, par * S: par * S + ncols]
                dsl = bscr[:, par * S + (nk - 1) * 128: par * S + ncols]
                e_z = ph.emit("dve", lambda v, o=dsl, i=dsl, j=mask01[:, :]:
                              v.tensor_mul(o, i, j), waits=[e_e])
                cr = stats[:, 52 + par: 52 + par + 1]
                e_r = ph.emit("dve", lambda v, o=cr, i=bsl: v.reduce_sum(o, i, AX.X), waits=[e_z])
                e_rc = ph.emit("dve", lambda v, o=cr, i=cr: v.reciprocal(o, i), waits=[e_r])
                absl = abf[:, par * S: par * S + ncols]
                e_ab = ph.emit("dve", lambda v, o=absl, i=bsl, r=cr:
                               v.tensor_scalar(o, i, r, None, ALU.mult),
                               waits=[e_rc, prev_abf_rd.get(par)])
                prev_bs[par] = e_ab
                pending.append((h, qt, par, e_ab, idx))
            while pending:
                emit_tail(*pending.pop(0))
            lastout = P.snap("c_act")
            pvt2 = dict(prev_avp)
            for tt in range(NT):
                for oc in range(2):
                    key = (tt * 2 + oc) % 2
                    pbank = pb[key]
                    emm = None
                    for it in range(NIT):
                        last = (it == NIT - 1) and (g != 0)
                        w0 = [lastout, pvt2.get(key), ewo] if it == 0 else ()
                        emm = ph.emit("pe", lambda t, o=pbank[:, :],
                                      l=outg[:, it * S + tt * 128: it * S + tt * 128 + 128],
                                      r=woutb[:, it * 1024 + oc * 512: it * 1024 + (oc + 1) * 512],
                                      s_=(it == 0), p_=last:
                                      t.matmul(o, l, r, start=s_, stop=p_),
                                      waits=w0, sem=False if not last else None)
                    if g == 0:
                        emm = ph.emit("pe", lambda t, o=pbank[:, :], l=L["ones32"][:1, :],
                                      r=L["boutr"][:1, oc * 512:(oc + 1) * 512]:
                                      t.matmul(o, l, r, start=False, stop=True))
                    dst = resid[:, tt * 1024 + oc * 512: tt * 1024 + (oc + 1) * 512]
                    eev = ph.emit("dve", lambda v, o=dst, i=dst, j=pbank[:, :]:
                                  v.tensor_add(o, i, j), waits=[emm])
                    pvt2[key] = eev
            ph.flush(f"s{s}_ab_g{g}_at")
            if feeder is not None:
                feeder.feed(8)


KEYS_DEV = ("L0_attn_a_Wq", "L0_attn_a_Wk", "L0_attn_a_Wv", "L0_attn_a_Wo",
            "L0_attn_b_Wqkv", "L0_attn_b_Wout", "L0_ffn_W1", "L0_ffn_W2",
            "L0_ffn_b1", "L0_ffn_b2")


def kernel(**inputs):
    x = np.ascontiguousarray(np.asarray(inputs["L0_attn_a_x"], np.float32))
    aux = build_aux(inputs)
    nc = build_program(n_seq=2)
    base = {k: np.ascontiguousarray(np.asarray(inputs[k], np.float32)) for k in KEYS_DEV}
    base.update(aux)
    in_maps = []
    for c in range(8):
        m = dict(base)
        m["x"] = x[2 * c:2 * c + 2]
        in_maps.append(m)
    res = run_bass_kernel_spmd(nc, in_maps, core_ids=list(range(8)))
    return np.concatenate([r["y"] for r in res.results], axis=0)
